# version 02
# baseline (speedup 1.0000x reference)
import numpy as np
import ml_dtypes
from contextlib import ExitStack

import concourse.bass as bass
import concourse.mybir as mybir
from concourse.bass_utils import run_bass_kernel_spmd

F32 = mybir.dt.float32
BF16 = mybir.dt.bfloat16
AF = mybir.ActivationFunctionType
ALU = mybir.AluOpType
AX = mybir.AxisListType
bf16 = ml_dtypes.bfloat16

N_CORES = 8
T_TOK = 8192
D = 4096
FS = 512
GT = 512
NDC = 32
SCALE = 1.0 / 16.0
BIV_IDX = [4, 5, 6]


def _derived():
    global TOK_PC, NG, NCP, NMC, LNCH
    TOK_PC = T_TOK // N_CORES
    NG = T_TOK // GT
    NCP = T_TOK // 2048
    NMC = T_TOK // 128
    LNCH = TOK_PC // 128


def _set_tokens(t):
    global T_TOK
    T_TOK = t
    _derived()


_derived()

ENGINES = ("sync", "tensor", "vector", "scalar", "gpsimd")


def _build_tables():
    g = [1.0, 1.0, 1.0]
    dim, n = 3, 8
    masks = sorted(range(n), key=lambda m: (bin(m).count("1"), m))
    inv = {m: i for i, m in enumerate(masks)}
    C = np.zeros((n, n, n), dtype=np.float32)
    for i, a in enumerate(masks):
        for j, bb in enumerate(masks):
            sw, at = 0, a >> 1
            while at:
                sw += bin(at & bb).count("1")
                at >>= 1
            s = -1.0 if sw & 1 else 1.0
            for t in range(dim):
                if a >> t & 1 and bb >> t & 1:
                    s *= g[t]
            C[i, j, inv[a ^ bb]] = s
    grades = np.array([bin(m).count("1") for m in masks])
    rev = np.where(grades * (grades - 1) // 2 % 2 == 1, -1.0, 1.0).astype(np.float32)
    return C, rev


class Sem:
    def __init__(self, handle):
        self.h = handle
        self.n = 0

    def inc(self, k=1):
        self.n += k
        return self.n


class B:
    CHAIN_ENGINES = ("vector", "scalar")

    def __init__(self):
        self.prog = {e: [] for e in ENGINES}
        self.last_wait = {e: {} for e in ENGINES}
        self.tail = {e: None for e in ENGINES}
        self.chain_sem = {}

    def op(self, eng, fn, sem=None, k=1, chain=True):
        do_chain = chain and eng in self.CHAIN_ENGINES
        if do_chain and self.tail[eng] is not None:
            self.wait(eng, *self.tail[eng])
        if sem is None and do_chain:
            sem, k = self.chain_sem[eng], 1
        if sem is None:
            self.prog[eng].append(lambda e, fn=fn: fn(e))
            return None
        v = sem.inc(k)
        self.prog[eng].append(lambda e, fn=fn, s=sem, k=k: fn(e).then_inc(s.h, k))
        if do_chain:
            self.tail[eng] = (sem, v)
        return v

    def wait(self, eng, sem, val):
        if val is None or val <= 0:
            return
        seen = self.last_wait[eng].get(id(sem), -1)
        if val <= seen:
            return
        self.last_wait[eng][id(sem)] = val
        self.prog[eng].append(lambda e, s=sem, v=val: e.wait_ge(s.h, v))


def build_program():
    nc = bass.Bass("TRN2", target_bir_lowering=False)

    def din(name, shape, dt=F32):
        return nc.dram_tensor(name, list(shape), dt, kind="ExternalInput").ap()

    x_tok = din("x_tok", [TOK_PC, D])
    xT = din("xT", [FS, T_TOK])
    wq = din("wq", [FS, D])
    wkT = din("wkT", [D, FS])
    wvT = din("wvT", [D, FS])
    bq4 = din("bq4", [128, 4])
    bk4 = din("bk4", [128, 4])
    bv4 = din("bv4", [128, 4])
    bw = din("bw", [2, 3])
    wfT = din("wfT", [D, FS])
    bf4 = din("bf4", [128, 4])
    afT = din("afT", [D, 256])
    bfp = din("bfp", [256, FS])
    cs4 = din("cs4", [128, 4])
    bivC = din("bivC", [2, 192])
    revs = din("revs", [2, 64])
    d0c = din("d0c", [2, 64])
    ident_f = din("ident_f", [128, 128])
    ident_b = din("ident_b", [128, 128], BF16)
    ones_b = din("ones_b", [1, 128], BF16)
    ones128_b = din("ones128_b", [128, 128], BF16)

    out_d = nc.dram_tensor("out", [FS, T_TOK], F32, kind="ExternalOutput").ap()

    m_scr = nc.dram_tensor("m_scr", [2, 64], F32, kind="Internal")
    xnT_own = nc.dram_tensor("xnT_own", [D, TOK_PC], BF16, kind="Internal")
    xn_all = nc.dram_tensor("xn_all", [N_CORES * D, TOK_PC], BF16, kind="Internal")
    wqeT_d = nc.dram_tensor("wqeT_d", [D, FS], BF16, kind="Internal")
    o_scr = nc.dram_tensor("o_scr", [FS, T_TOK], BF16, kind="Internal")
    v_scr = nc.dram_tensor("v_scr", [256, T_TOK], BF16, kind="Internal")

    b = B()

    with ExitStack() as ctx:
        sbt = lambda name, shp, dt: ctx.enter_context(nc.sbuf_tensor(name, shp, dt))
        ident_f_s = sbt("ident_f_s", [128, 128], F32)
        ident_b_s = sbt("ident_b_s", [128, 128], BF16)
        ones_b_s = sbt("ones_b_s", [1, 128], BF16)
        ones128_s = sbt("ones128_s", [128, 128], BF16)
        bk4_s = sbt("bk4_s", [128, 4], F32)
        bf4_s = sbt("bf4_s", [128, 4], F32)
        bqe_s = sbt("bqe_s", [128, 4], F32)
        bv4_s = sbt("bv4_s", [128, 4], F32)
        cs_s = sbt("cs_s", [128, 4], F32)
        bfp_b = sbt("bfp_b", [128, 1024], BF16)
        pt = sbt("pt", [128, 3 * 1024], BF16)
        ptT = sbt("ptT", [128, 4 * 4096], BF16)
        bc_s = sbt("bc_s", [128, 512], F32)
        dstripe = sbt("dstripe", [128, 512], BF16)
        asml = sbt("asml", [128, 40], F32)
        xn_t = sbt("xn_t", [128, 2 * GT], BF16)
        o_stage = sbt("o_stage", [128, 1024], BF16)
        vt_stage = sbt("vt_stage", [128, 1024], BF16)
        eps_s = sbt("eps_s", [128, 1], F32)

        MRUN, MU, BETA, ZC, ZP, ZT, RT, DLT = 0, 4, 8, 12, 14, 16, 20, 24

        def sem(name):
            return Sem(ctx.enter_context(nc.semaphore(name)))

        b.chain_sem["vector"] = sem("s_chv")
        b.chain_sem["scalar"] = sem("s_chs")
        s_ld = sem("s_ld")
        s_g = sem("s_g")
        s_g3 = sem("s_g3")
        s_gset = sem("s_gset")
        s_v = sem("s_v")
        s_s = sem("s_s")
        s_t = sem("s_t")
        s_cc = sem("s_cc")
        s_xn = sem("s_xn")
        s_xt = [sem("s_xt0"), sem("s_xt1")]
        s_tf = sem("s_tf")
        s_evs = sem("s_evs")
        s_evv = sem("s_evv")
        s_wq = sem("s_wq")
        s_qk = sem("s_qk")
        s_mx = sem("s_mx")
        s_exp = sem("s_exp")
        s_ptf = [sem("s_pt0"), sem("s_pt1"), sem("s_pt2")]
        s_tpc = sem("s_tpc")
        s_av = sem("s_av")
        s_rsc = sem("s_rsc")
        s_bc = sem("s_bc")
        s_bcm = sem("s_bcm")
        s_oev = sem("s_oev")
        s_osp = sem("s_osp")
        s_ou = [sem("s_ou0"), sem("s_ou1"), sem("s_ou2"), sem("s_ou3")]
        s_xnst = sem("s_xnst")
        s_lntp = sem("s_lntp")
        s_vst = sem("s_vst")
        s_vtp = sem("s_vtp")

        PS_A = ctx.enter_context(nc.psum_tensor("PS_A", [128, 1024], F32))
        PS_B = ctx.enter_context(nc.psum_tensor("PS_B", [128, 1024], F32))
        PS_C = ctx.enter_context(nc.psum_tensor("PS_C", [128, 1024], F32))
        PS_D = ctx.enter_context(nc.psum_tensor("PS_D", [128, 1024], F32))
        LBUF = (PS_A, PS_B, PS_C)

        block = ctx.enter_context(nc.Block())

        with ExitStack() as s1:
            sb1 = lambda name, shp, dt: s1.enter_context(nc.sbuf_tensor(name, shp, dt))
            wq_sb = sb1("wq_sb", [128, 4 * D], BF16)
            Tf = sb1("Tf", [128, 2048], F32)
            T_b = sb1("T_b", [128, 2048], BF16)
            x_chunk = sb1("x_chunk", [128, 2 * D], F32)
            xsq = sb1("xsq", [128, D], F32)
            xn_bf = sb1("xn_bf", [128, D], BF16)
            xnT_stage = sb1("xnT_stage", [128, D], BF16)
            stageT = sb1("stageT", [128, 512], BF16)
            ln_apply_done = []
            lns = sb1("lns", [128, 16], F32)
            bw_s = sb1("bw_s", [2, 3], F32)
            wk_s = sb1("wk_s", [2, 64], F32)
            M_s = sb1("M_s", [2, 64], F32)
            bivC_s = sb1("bivC_s", [2, 192], F32)
            revs_s = sb1("revs_s", [2, 64], F32)
            d0_s = sb1("d0_s", [2, 64], F32)
            bq4_b = sb1("bq4_b", [128, 4], BF16)

            for src, dst in [(ident_f, ident_f_s), (ident_b, ident_b_s),
                             (ones_b, ones_b_s), (ones128_b, ones128_s),
                             (bk4, bk4_s), (bf4, bf4_s), (cs4, cs_s),
                             (bivC, bivC_s), (revs, revs_s), (d0c, d0_s),
                             (bw, bw_s)]:
                b.op("sync", lambda e, s=src, d=dst: e.dma_start(out=d[:, :], in_=s), s_ld, 16)
            v_consts = s_ld.n
            b.op("gpsimd", lambda e: e.dma_start(out=bv4_s[:, :], in_=bv4), s_gset, 16)
            b.op("gpsimd", lambda e: e.dma_start(out=bq4_b[:, :], in_=bq4), s_gset, 16)
            b.op("gpsimd", lambda e: e.dma_start(
                out=wq_sb.rearrange("p (ic d) -> p ic d", d=D),
                in_=wq.rearrange("(ic p) d -> p ic d", p=128)), s_gset, 16)
            b.op("gpsimd", lambda e: e.dma_start(
                out=bfp_b.rearrange("p (ch o) -> p ch o", o=FS),
                in_=bfp.rearrange("(ch p) o -> p ch o", p=128)), s_gset, 16)
            v_gcasts = s_gset.n

            b.wait("vector", s_ld, v_consts)
            b.op("vector", lambda e: e.memset(eps_s[:, :], 1e-5))
            b.op("vector", lambda e: e.tensor_scalar_mul(wk_s[:, :], bivC_s[:, 0:64], bw_s[:, 0:1]))
            b.op("vector", lambda e: e.scalar_tensor_tensor(
                out=wk_s[:, :], in0=bivC_s[:, 64:128], scalar=bw_s[:, 1:2], in1=wk_s[:, :],
                op0=ALU.mult, op1=ALU.add))
            b.op("vector", lambda e: e.scalar_tensor_tensor(
                out=wk_s[:, :], in0=bivC_s[:, 128:192], scalar=bw_s[:, 2:3], in1=wk_s[:, :],
                op0=ALU.mult, op1=ALU.add))
            b.op("vector", lambda e: e.tensor_mul(M_s[:, :], wk_s[:, :], revs_s[:, :]))
            b.op("vector", lambda e: e.tensor_add(M_s[:, :], M_s[:, :], d0_s[:, :]))
            b.op("vector", lambda e: e.memset(Tf[:, :], 0.0), s_v, 1)
            v_m = s_v.n
            b.wait("sync", s_v, v_m)
            b.op("sync", lambda e: e.dma_start(out=m_scr.ap(), in_=M_s[:, :]), s_ld, 16)
            b.wait("sync", s_ld, s_ld.n)
            for h in range(2):
                for dd in range(32):
                    r0 = h * 256 + dd * 8
                    ic, p0 = r0 // 128, r0 % 128
                    b.op("sync", lambda e, h=h, p0=p0, ic=ic, r0=r0: e.dma_start(
                        out=Tf[p0:p0 + 8, ic * 512 + r0: ic * 512 + r0 + 8],
                        in_=m_scr.ap()[h:h + 1, :].rearrange("one (i j) -> (one i) j", i=8)),
                        s_ld, 16)
            v_scat = s_ld.n
            b.wait("vector", s_ld, v_scat)
            b.op("vector", lambda e: e.tensor_copy(out=T_b[:, :], in_=Tf[:, :]), s_v, 1)
            v_tb = s_v.n

            b.wait("tensor", s_v, v_tb)
            b.wait("tensor", s_gset, v_gcasts)
            fold_ev = []
            for dc in range(NDC):
                half = (dc % 2) * 512
                if dc >= 2:
                    b.wait("tensor", s_v, fold_ev[dc - 2])
                for icc in range(4):
                    b.op("tensor", lambda e, dc=dc, icc=icc, half=half: e.matmul(
                        PS_A[:, half:half + 512],
                        wq_sb[:, icc * D + dc * 128: icc * D + dc * 128 + 128],
                        T_b[:, icc * 512: icc * 512 + 512],
                        start=(icc == 0), stop=(icc == 3)),
                        s_t if icc == 3 else None, 1)
                vt = s_t.n
                b.wait("vector", s_t, vt)
                b.wait("vector", s_wq, s_wq.n)
                b.op("vector", lambda e, half=half: e.tensor_copy(
                    out=stageT[:, :], in_=PS_A[:, half:half + 512]), s_v, 1)
                fold_ev.append(s_v.n)
                b.wait("sync", s_v, s_v.n)
                b.op("sync", lambda e, dc=dc: e.dma_start(
                    out=wqeT_d.ap()[dc * 128:(dc + 1) * 128, :], in_=stageT[:, :]),
                    s_wq, 16)
            v_wqe_stores = s_wq.n

            bq_ps = [(PS_A, 0), (PS_A, 512), (PS_B, 0), (PS_B, 512)]
            b.wait("tensor", s_v, fold_ev[-1])
            b.wait("tensor", s_v, fold_ev[-2])
            for jc in range(4):
                ps, off = bq_ps[jc]
                for icc in range(4):
                    b.op("tensor", lambda e, jc=jc, icc=icc, ps=ps, off=off: e.matmul(
                        ps[:, off:off + 1],
                        T_b[:, icc * 512 + jc * 128: icc * 512 + jc * 128 + 128],
                        bq4_b[:, icc:icc + 1],
                        start=(icc == 0), stop=(icc == 3)),
                        s_t if (jc == 3 and icc == 3) else None, 1)
            v_bq = s_t.n
            b.wait("vector", s_t, v_bq)
            for jc in range(4):
                ps, off = bq_ps[jc]
                b.op("vector", lambda e, jc=jc, ps=ps, off=off: e.tensor_copy(
                    out=bqe_s[:, jc:jc + 1], in_=ps[:, off:off + 1]),
                    s_v if jc == 3 else None, 1)

            b.wait("vector", s_gset, v_gcasts)
            b.op("vector", lambda e: e.tensor_mul(cs_s[:, 0:2], cs_s[:, 0:2], cs_s[:, 2:4]))
            for chh in range(2):
                b.op("vector", lambda e, chh=chh: e.tensor_scalar_mul(
                    bfp_b[:, chh * 512: chh * 512 + 512],
                    bfp_b[:, chh * 512: chh * 512 + 512],
                    cs_s[:, chh:chh + 1]))

            b.wait("vector", s_wq, v_wqe_stores)
            last_apply = 0
            last_tp = 0
            for i in range(LNCH):
                xb = (i % 2) * D
                if i >= 2:
                    b.wait("sync", s_s, ln_apply_done[i - 2])
                b.op("sync", lambda e, i=i, xb=xb: e.dma_start(
                    out=x_chunk[:, xb:xb + D], in_=x_tok[i * 128:(i + 1) * 128, :]), s_xn, 16)
                v_x = s_xn.n
                b.wait("vector", s_xn, v_x)
                b.op("vector", lambda e, xb=xb: e.reduce_sum(lns[:, 0:1], x_chunk[:, xb:xb + D], axis=AX.X))
                b.wait("scalar", s_xn, v_x)
                b.op("scalar", lambda e, xb=xb: e.activation(
                    out=xsq[:, :], in_=x_chunk[:, xb:xb + D], func=AF.Square), s_s, 1)
                b.wait("vector", s_s, s_s.n)
                b.op("vector", lambda e: e.reduce_sum(lns[:, 1:2], xsq[:, :], axis=AX.X))
                b.op("vector", lambda e: e.tensor_scalar_mul(lns[:, 6:7], lns[:, 0:1], 1.0 / D))
                b.op("vector", lambda e: e.tensor_scalar_mul(lns[:, 7:8], lns[:, 1:2], 1.0 / D))
                b.op("vector", lambda e: e.tensor_mul(lns[:, 8:9], lns[:, 6:7], lns[:, 6:7]))
                b.op("vector", lambda e: e.tensor_sub(lns[:, 9:10], lns[:, 7:8], lns[:, 8:9]), s_v, 1)
                b.wait("scalar", s_v, s_v.n)
                b.op("scalar", lambda e: e.activation(
                    out=lns[:, 10:11], in_=lns[:, 9:10], func=AF.Sqrt, bias=eps_s[:, 0:1]), s_s, 1)
                b.wait("vector", s_s, s_s.n)
                b.op("vector", lambda e: e.reciprocal(out=lns[:, 11:12], in_=lns[:, 10:11]))
                b.op("vector", lambda e: e.scalar_tensor_tensor(
                    out=lns[:, 12:13], in0=lns[:, 6:7], scalar=-1.0, in1=lns[:, 11:12],
                    op0=ALU.mult, op1=ALU.mult), s_v, 1)
                b.wait("scalar", s_v, s_v.n)
                if i >= 1:
                    b.wait("scalar", s_lntp, 16 * i)
                b.op("scalar", lambda e, xb=xb: e.activation(
                    out=xn_bf[:, :], in_=x_chunk[:, xb:xb + D], func=AF.Identity,
                    bias=lns[:, 12:13], scale=lns[:, 11:12]), s_s, 1)
                v_apply = s_s.n
                ln_apply_done.append(v_apply)
                b.wait("sync", s_s, v_apply)
                if i >= 1:
                    b.wait("sync", s_xnst, 16 * i)
                b.op("sync", lambda e: e.dma_start(
                    out=xnT_stage.rearrange("p (dc t) -> p dc t", t=128),
                    in_=xn_bf[:, :], transpose=True), s_lntp, 16)
                b.wait("sync", s_lntp, s_lntp.n)
                b.op("sync", lambda e, i=i: e.dma_start(
                    out=xnT_own.ap().rearrange("(dc p) t -> p dc t", p=128)[
                        :, :, i * 128:(i + 1) * 128],
                    in_=xnT_stage.rearrange("p (dc t) -> p dc t", t=128)),
                    s_xnst, 16)
            v_xnst = s_xnst.n

            b.wait("gpsimd", s_xnst, v_xnst)
            b.op("gpsimd", lambda e: e.collective_compute(
                "AllGather", ALU.bypass,
                replica_groups=[list(range(N_CORES))],
                ins=[xnT_own.ap().opt()], outs=[xn_all.ap().opt()]), s_cc, 1)
            v_cc = s_cc.n


        def xn_tile_ap(g, dc):
            r = (g * GT) // TOK_PC
            t0 = (g * GT) % TOK_PC
            return xn_all.ap()[r * D + dc * 128: r * D + dc * 128 + 128, t0:t0 + GT]

        with ExitStack() as s2:
            sb2 = lambda name, shp, dt: s2.enter_context(nc.sbuf_tensor(name, shp, dt))
            wqe_sb = sb2("wqe_sb", [128, NDC * 256], BF16)
            wk_sb = sb2("wk_sb", [128, NDC * 256], BF16)
            wv_sb = sb2("wv_sb", [128, NDC * 256], BF16)
            q_sb = sb2("q_sb", [128, 2 * T_TOK], BF16)
            k_sb = sb2("k_sb", [128, 2 * T_TOK], BF16)
            v_sb = sb2("v_sb", [128, NMC * 256], BF16)

            p1_last_mm = [0, 0]

            for hL in range(2):
                b.wait("gpsimd", s_wq, v_wqe_stores)
                if hL == 1:
                    b.wait("gpsimd", *p1_last_mm[0])
                b.op("gpsimd", lambda e, hL=hL: e.dma_start(
                    out=wqe_sb.rearrange("p (dc j) -> p dc j", j=256),
                    in_=wqeT_d.ap().rearrange("(dc p) j -> p dc j", p=128)[
                        :, :, hL * 256:(hL + 1) * 256]), s_g, 16)
                b.op("gpsimd", lambda e, hL=hL: e.dma_start(
                    out=wk_sb.rearrange("p (dc j) -> p dc j", j=256),
                    in_=wkT.rearrange("(dc p) j -> p dc j", p=128)[
                        :, :, hL * 256:(hL + 1) * 256]), s_g, 16)
                b.op("gpsimd", lambda e, hL=hL: e.dma_start(
                    out=wv_sb.rearrange("p (dc j) -> p dc j", j=256),
                    in_=wvT.rearrange("(dc p) j -> p dc j", p=128)[
                        :, :, hL * 256:(hL + 1) * 256]), s_g, 16)
                v_wpre = s_g.n
                b.wait("tensor", s_g, v_wpre)
                if hL == 1:
                    b.wait("tensor", s_oev, s_oev.n)
                    b.wait("tensor", s_exp, s_exp.n)
                    b.wait("tensor", s_bc, s_bc.n)

                b.wait("sync", s_cc, v_cc)
                base_tf = s_tf.n
                base_xt = [s_xt[0].n, s_xt[1].n]

                def p1_emit_loads(g):
                    for dc in range(NDC):
                        flat = g * NDC + dc
                        if flat >= 2:
                            b.wait("sync", s_tf, base_tf + flat - 1)
                        bsl = (flat % 2) * GT
                        b.op("sync", lambda e, g=g, dc=dc, bsl=bsl: e.dma_start(
                            out=xn_t[:, bsl:bsl + GT], in_=xn_tile_ap(g, dc)),
                            s_xt[flat % 2], 16)

                p1_evs_base = s_evs.n
                p1_tf_base = s_tf.n
                vst_base = s_vst.n
                for g in range(NG):
                    p1_emit_loads(g)
                    for dc in range(NDC):
                        flat = g * NDC + dc
                        b.wait("tensor", s_xt[flat % 2],
                               base_xt[flat % 2] + 16 * (flat // 2 + 1))
                        bsl = (flat % 2) * GT
                        if dc == 0 and g > 0:
                            b.wait("tensor", s_evs, p1_evs_base + 6 * g)
                        for jc in range(2):
                            b.op("tensor", lambda e, dc=dc, jc=jc, bsl=bsl: e.matmul(
                                PS_A[:, jc * 512:(jc + 1) * 512],
                                wqe_sb[:, dc * 256 + jc * 128: dc * 256 + jc * 128 + 128],
                                xn_t[:, bsl:bsl + GT],
                                start=(dc == 0), stop=(dc == NDC - 1)))
                        for jc in range(2):
                            b.op("tensor", lambda e, dc=dc, jc=jc, bsl=bsl: e.matmul(
                                PS_B[:, jc * 512:(jc + 1) * 512],
                                wk_sb[:, dc * 256 + jc * 128: dc * 256 + jc * 128 + 128],
                                xn_t[:, bsl:bsl + GT],
                                start=(dc == 0), stop=(dc == NDC - 1)))
                        for jc in range(2):
                            b.op("tensor", lambda e, dc=dc, jc=jc, bsl=bsl: e.matmul(
                                PS_C[:, jc * 512:(jc + 1) * 512],
                                wv_sb[:, dc * 256 + jc * 128: dc * 256 + jc * 128 + 128],
                                xn_t[:, bsl:bsl + GT],
                                start=(dc == 0), stop=(dc == NDC - 1)),
                                s_tf if jc == 1 else None, 1)
                    v_gmm = p1_tf_base + NDC * (g + 1)
                    b.wait("scalar", s_tf, v_gmm)
                    for jc in range(2):
                        jcg = hL * 2 + jc
                        b.op("scalar", lambda e, g=g, jc=jc, jcg=jcg: e.activation(
                            out=q_sb[:, jc * T_TOK + g * GT: jc * T_TOK + g * GT + GT],
                            in_=PS_A[:, jc * 512:(jc + 1) * 512], func=AF.Identity,
                            bias=bqe_s[:, jcg:jcg + 1]), s_evs, 1)
                    for jc in range(2):
                        jcg = hL * 2 + jc
                        b.op("scalar", lambda e, g=g, jc=jc, jcg=jcg: e.activation(
                            out=k_sb[:, jc * T_TOK + g * GT: jc * T_TOK + g * GT + GT],
                            in_=PS_B[:, jc * 512:(jc + 1) * 512], func=AF.Identity,
                            bias=bk4_s[:, jcg:jcg + 1]), s_evs, 1)
                    b.wait("scalar", s_vst, vst_base + 16 * g)
                    for jc in range(2):
                        jcg = hL * 2 + jc
                        b.op("scalar", lambda e, jc=jc, jcg=jcg: e.activation(
                            out=vt_stage[:, jc * 512:(jc + 1) * 512],
                            in_=PS_C[:, jc * 512:(jc + 1) * 512], func=AF.Identity,
                            bias=bv4_s[:, jcg:jcg + 1]), s_evs, 1)
                    v_vev = s_evs.n
                    b.wait("sync", s_evs, v_vev)
                    b.op("sync", lambda e, g=g: e.dma_start(
                        out=v_scr.ap().rearrange("(ic p) t -> p ic t", p=128)[
                            :, :, g * GT:(g + 1) * GT],
                        in_=vt_stage.rearrange("p (ic t) -> p ic t", t=512)),
                        s_vst, 16)
                b.wait("sync", s_vst, s_vst.n)
                if hL == 1:
                    b.wait("sync", s_av, s_av.n)
                for ic in range(2):
                    b.op("sync", lambda e, ic=ic: e.dma_start(
                        out=v_sb.rearrange("p (mc f) -> p mc f", f=256)[
                            :, :, ic * 128:(ic + 1) * 128],
                        in_=v_scr.ap()[ic * 128:(ic + 1) * 128, :].rearrange(
                            "ii (mc t) -> ii mc t", t=128),
                        transpose=True), s_vtp, 16)
                v_vtp = s_vtp.n

                p1_last_mm[hL] = (s_tf, s_tf.n)
                lguard = [(s_evs, s_evs.n)] * 3
                ptguard = [(s_ptf[i], s_ptf[i].n) for i in range(3)]
                ptTguard = [(s_av, s_av.n)] * 4
                oguard = None if hL == 0 else (s_oev, s_oev.n)
                last_dstripe = s_s.n
                last_zupd = s_v.n
                last_final_dst = s_s.n
                pending_av = []
                pair_tp = {}
                rt_ready = None
                final_G = None

                def drain_av(n):
                    for _ in range(min(n, len(pending_av))):
                        pending_av.pop(0)()

                def queue_av(cpav, v_vtp=None, og=None):
                    base_av = s_av.n
                    for ci2 in range(2):
                        cav = cpav * 2 + ci2
                        tslot = cav % 4
                        first = (cav == 0)
                        mark = pair_tp[cav]

                        def first_waits(cav=cav, mark=mark, og=og, vv=v_vtp):
                            b.wait("tensor", s_vtp, vv)
                            b.wait("tensor", s_tpc, mark)
                            if og is not None:
                                b.wait("tensor", *og)
                        emitted_first = {"v": False}
                        for msub in range(8):
                            for dch in range(2):
                                st = (first and msub == 0)
                                sp = (msub == 7)

                                def part(cav=cav, msub=msub, dch=dch, tslot=tslot,
                                         st=st, sp=sp, first=first,
                                         first_waits=first_waits, ef=emitted_first):
                                    if not ef["v"]:
                                        first_waits()
                                        ef["v"] = True
                                    b.op("tensor",
                                         lambda e: e.matmul(
                                             PS_D[:, dch * 512:(dch + 1) * 512],
                                             v_sb[:, (cav * 8 + msub) * 256 + dch * 128:
                                                  (cav * 8 + msub) * 256 + dch * 128 + 128],
                                             ptT[:, tslot * 4096 + msub * 512:
                                                 tslot * 4096 + msub * 512 + 512],
                                             start=st, stop=sp,
                                             skip_group_check=(not first)),
                                         s_av if (msub == 7 and dch == 1) else None, 1)
                                pending_av.append(part)
                        ptTguard[tslot] = (s_av, base_av + ci2 + 1)

                def flush_av():
                    while pending_av:
                        pending_av.pop(0)()

                def emit_rescale(scale_col_base, sem_src, v_src):
                    b.wait("scalar", s_bcm, s_bcm.n)
                    b.wait("scalar", sem_src, v_src)
                    for t in range(4):
                        b.op("scalar", lambda e, t=t, base=scale_col_base: e.activation(
                            out=dstripe[:, t * 128:(t + 1) * 128],
                            in_=ident_b_s[:, :], func=AF.Identity,
                            scale=asml[:, base + t:base + t + 1]), s_s, 1)
                    v_dst = s_s.n
                    bufb = (step + 1) % 3
                    Lb = LBUF[bufb]
                    b.wait("tensor", s_s, v_dst)
                    if lguard[bufb] is not None:
                        b.wait("tensor", *lguard[bufb])
                    b.op("tensor", lambda e, Lb=Lb: e.matmul(
                        Lb[:, 0:512], ones128_s[:, :], dstripe[:, :],
                        start=True, stop=True), s_bcm, 1)
                    b.wait("vector", s_bcm, s_bcm.n)
                    b.op("vector", lambda e, Lb=Lb: e.tensor_copy(
                        out=bc_s[:, :], in_=Lb[:, 0:512]), s_bc, 1)
                    lguard[bufb] = (s_bc, s_bc.n)
                    b.wait("vector", s_av, s_av.n)
                    for dch in range(2):
                        b.op("vector", lambda e, dch=dch: e.tensor_mul(
                            PS_D[:, dch * 512:(dch + 1) * 512],
                            PS_D[:, dch * 512:(dch + 1) * 512],
                            bc_s[:, :]), s_rsc if dch == 1 else None, 1)
                    return v_dst, (s_rsc, s_rsc.n)

                def emit_final(Gf):
                    nonlocal last_final_dst, oguard
                    v_dst, _ = emit_rescale(RT, s_v, rt_ready)
                    last_final_dst = v_dst
                    v_final_rsc = s_rsc.n
                    b.wait("scalar", s_rsc, v_final_rsc)
                    b.wait("scalar", s_osp, s_osp.n)
                    for dch in range(2):
                        b.op("scalar", lambda e, dch=dch: e.copy(
                            out=o_stage[:, dch * 512:(dch + 1) * 512],
                            in_=PS_D[:, dch * 512:(dch + 1) * 512]), s_oev, 1)
                    v_oev = s_oev.n
                    oguard = (s_oev, v_oev)
                    b.wait("sync", s_oev, v_oev)
                    b.op("sync", lambda e, Gf=Gf, hL=hL: e.dma_start(
                        out=o_scr.ap().rearrange("(r p) t -> p r t", p=128)[
                            :, hL * 2: hL * 2 + 2, Gf * GT:(Gf + 1) * GT],
                        in_=o_stage.rearrange("p (r t) -> p r t", t=512)), s_osp, 16)

                step = 0
                for G in range(NG):
                    for cp in range(NCP):
                        tp_records = {cp * 2: [], cp * 2 + 1: []}
                        for t in range(4):
                            for cisub in range(2):
                                c = cp * 2 + cisub
                                buf = step % 3
                                L = LBUF[buf]
                                if lguard[buf] is not None:
                                    b.wait("tensor", *lguard[buf])
                                for icc in range(2):
                                    for nb2 in range(2):
                                        b.op("tensor",
                                             lambda e, G=G, t=t, c=c, icc=icc, nb2=nb2, L=L: e.matmul(
                                                 L[:, nb2 * 512:(nb2 + 1) * 512],
                                                 q_sb[:, icc * T_TOK + G * GT + t * 128:
                                                      icc * T_TOK + G * GT + t * 128 + 128],
                                                 k_sb[:, icc * T_TOK + c * 1024 + nb2 * 512:
                                                      icc * T_TOK + c * 1024 + nb2 * 512 + 512],
                                                 start=(icc == 0), stop=(icc == 1)),
                                             s_qk if (icc == 1 and nb2 == 1) else None, 1)
                                v_qkstep = s_qk.n
                                drain_av(4)
                                b.wait("vector", s_qk, v_qkstep)
                                b.op("vector", lambda e, L=L, cisub=cisub: e.reduce_max(
                                    asml[:, MU + cisub:MU + cisub + 1], L[:, :],
                                    axis=AX.X, negate=True), s_mx, 1)
                                if cisub == 1:
                                    b.op("vector", lambda e: e.tensor_tensor(
                                        out=asml[:, MU:MU + 1], in0=asml[:, MU:MU + 1],
                                        in1=asml[:, MU + 1:MU + 2], op=ALU.min))
                                    if cp == 0:
                                        b.op("vector", lambda e, t=t: e.tensor_copy(
                                            out=asml[:, MRUN + t:MRUN + t + 1],
                                            in_=asml[:, MU:MU + 1]), s_v, 1)
                                    else:
                                        b.wait("vector", s_s, last_dstripe)
                                        b.op("vector", lambda e, t=t: e.tensor_tensor(
                                            out=asml[:, DLT:DLT + 1], in0=asml[:, MU:MU + 1],
                                            in1=asml[:, MRUN + t:MRUN + t + 1], op=ALU.min))
                                        b.op("vector", lambda e, t=t: e.tensor_sub(
                                            asml[:, BETA + t:BETA + t + 1],
                                            asml[:, DLT:DLT + 1],
                                            asml[:, MRUN + t:MRUN + t + 1]))
                                        b.op("vector", lambda e, t=t: e.tensor_copy(
                                            out=asml[:, MRUN + t:MRUN + t + 1],
                                            in_=asml[:, DLT:DLT + 1]), s_v, 1)
                                    v_mready = s_v.n
                                    for cc2 in range(2):
                                        st2 = step - 1 + cc2
                                        slot2 = st2 % 3
                                        L2 = LBUF[slot2]
                                        b.wait("scalar", s_v, v_mready)
                                        b.wait("scalar", s_v, last_zupd)
                                        if ptguard[slot2] is not None:
                                            b.wait("scalar", *ptguard[slot2])
                                        b.op("scalar", lambda e, L2=L2, slot2=slot2, t=t, cc2=cc2: e.activation(
                                            out=pt[:, slot2 * 1024:(slot2 + 1) * 1024],
                                            in_=L2[:, :], func=AF.Exp,
                                            bias=asml[:, MRUN + t:MRUN + t + 1],
                                            accum_out=asml[:, ZC + cc2:ZC + cc2 + 1]),
                                            s_exp, 1)
                                        lguard[slot2] = (s_exp, s_exp.n)
                                        cthis = cp * 2 + cc2
                                        tslot = cthis % 4
                                        b.wait("sync", s_exp, s_exp.n)
                                        if t == 0 and ptTguard[tslot] is not None:
                                            b.wait("sync", *ptTguard[tslot])
                                        vtp = b.op("sync", lambda e, slot2=slot2, tslot=tslot, t=t: e.dma_start(
                                            out=ptT.rearrange("m (sl s q) -> m (sl s) q", q=GT, s=8)[
                                                :, tslot * 8:(tslot + 1) * 8, t * 128:(t + 1) * 128],
                                            in_=pt[:, slot2 * 1024:(slot2 + 1) * 1024],
                                            transpose=True), s_ptf[slot2], 16)
                                        ptguard[slot2] = (s_ptf[slot2], vtp)
                                        tp_records[cthis].append((s_ptf[slot2], vtp))
                                    b.wait("vector", s_exp, s_exp.n)
                                    b.op("vector", lambda e: e.tensor_add(
                                        asml[:, ZP:ZP + 1], asml[:, ZC:ZC + 1],
                                        asml[:, ZC + 1:ZC + 2]))
                                    if cp == 0:
                                        b.op("vector", lambda e, t=t: e.tensor_copy(
                                            out=asml[:, ZT + t:ZT + t + 1],
                                            in_=asml[:, ZP:ZP + 1]), s_v, 1)
                                    else:
                                        b.wait("scalar", s_v, v_mready)
                                        b.op("scalar", lambda e, t=t: e.activation(
                                            out=asml[:, BETA + t:BETA + t + 1],
                                            in_=asml[:, BETA + t:BETA + t + 1],
                                            func=AF.Exp), s_s, 1)
                                        b.wait("vector", s_s, s_s.n)
                                        b.op("vector", lambda e, t=t: e.scalar_tensor_tensor(
                                            out=asml[:, ZT + t:ZT + t + 1],
                                            in0=asml[:, ZT + t:ZT + t + 1],
                                            scalar=asml[:, BETA + t:BETA + t + 1],
                                            in1=asml[:, ZP:ZP + 1],
                                            op0=ALU.mult, op1=ALU.add), s_v, 1)
                                    last_zupd = s_v.n
                                step += 1
                        for cch in (cp * 2, cp * 2 + 1):
                            for semx, valx in tp_records[cch]:
                                b.wait("sync", semx, valx)
                            b.op("sync", lambda e: e.nop(), s_tpc, 1)
                            pair_tp[cch] = s_tpc.n
                        flush_av()
                        if cp > 0:
                            last_dstripe, og2 = emit_rescale(BETA, s_s, s_s.n)
                            oguard = og2
                        elif final_G is not None:
                            emit_final(final_G)
                            final_G = None
                        queue_av(cp, v_vtp=v_vtp, og=oguard)
                        if cp == NCP - 1:
                            b.wait("vector", s_exp, s_exp.n)
                            b.wait("vector", s_s, last_final_dst)
                            for t in range(4):
                                b.op("vector", lambda e, t=t: e.reciprocal(
                                    out=asml[:, RT + t:RT + t + 1],
                                    in_=asml[:, ZT + t:ZT + t + 1]), s_v if t == 3 else None, 1)
                            rt_ready = s_v.n
                            final_G = G
                flush_av()
                emit_final(final_G)
                final_G = None
            v_osp_all = s_osp.n


        with ExitStack() as s3:
            sb3 = lambda name, shp, dt: s3.enter_context(nc.sbuf_tensor(name, shp, dt))
            wf_sb = sb3("wf_sb", [128, NDC * 512], BF16)
            af_sb = sb3("af_sb", [128, NDC * 256], BF16)
            o_tile = sb3("o_tile", [128, 4 * 512], F32)
            xT_tile = sb3("xT_tile", [128, 4 * 512], F32)
            xa_sb = sb3("xa_sb", [128, 1024], BF16)
            stage_out = sb3("stage_out", [128, 4 * 512], F32)

            b.wait("gpsimd", *p1_last_mm[1])
            b.op("gpsimd", lambda e: e.dma_start(
                out=wf_sb.rearrange("p (dc o) -> p dc o", o=512),
                in_=wfT.rearrange("(dc p) o -> p dc o", p=128)), s_g, 16)
            b.op("gpsimd", lambda e: e.dma_start(
                out=af_sb.rearrange("p (dc o) -> p dc o", o=256),
                in_=afT.rearrange("(dc p) o -> p dc o", p=128)), s_g, 16)
            v_p3w = s_g.n
            b.wait("tensor", s_g, v_p3w)
            b.wait("tensor", s_exp, s_exp.n)
            b.wait("tensor", s_bc, s_bc.n)
            b.wait("tensor", s_oev, s_oev.n)

            base_tf = s_tf.n
            base_xt = [s_xt[0].n, s_xt[1].n]

            def p3_emit_loads(g):
                for dc in range(NDC):
                    flat = g * NDC + dc
                    if flat >= 2:
                        b.wait("sync", s_tf, base_tf + flat - 1)
                    bsl = (flat % 2) * GT
                    b.op("sync", lambda e, g=g, dc=dc, bsl=bsl: e.dma_start(
                        out=xn_t[:, bsl:bsl + GT], in_=xn_tile_ap(g, dc)),
                        s_xt[flat % 2], 16)

            b.wait("gpsimd", s_osp, v_osp_all)
            base_g3 = s_g3.n
            for g in range(NG):
                if g >= 1:
                    for _oc in range(4):
                        b.wait("gpsimd", s_ou[_oc], 16 * g)
                for oc in range(4):
                    b.op("gpsimd", lambda e, g=g, oc=oc: e.dma_start(
                        out=o_tile[:, oc * 512:(oc + 1) * 512],
                        in_=o_scr.ap()[oc * 128:(oc + 1) * 128, g * GT:(g + 1) * GT]),
                        s_g3, 16)
                    b.op("gpsimd", lambda e, g=g, oc=oc: e.dma_start(
                        out=xT_tile[:, oc * 512:(oc + 1) * 512],
                        in_=xT[oc * 128:(oc + 1) * 128, g * GT:(g + 1) * GT]),
                        s_g3, 16)

            PSO = [(PS_A, 0), (PS_A, 512), (PS_B, 0), (PS_B, 512)]
            ev3_base = s_evv.n
            p3_tf_base = s_tf.n
            p3_evac_done = 0
            pending_stores = []
            for g in range(NG):
                p3_emit_loads(g)
                for emit_store in pending_stores:
                    emit_store()
                pending_stores = []
                for dc in range(NDC):
                    flat = g * NDC + dc
                    b.wait("tensor", s_xt[flat % 2],
                           base_xt[flat % 2] + 16 * (flat // 2 + 1))
                    bsl = (flat % 2) * GT
                    if dc == 0 and g > 0:
                        b.wait("tensor", s_s, p3_evac_done)
                        b.wait("tensor", s_evv, ev3_base + 2 * g)
                    for oc in range(4):
                        ps, off = PSO[oc]
                        b.op("tensor", lambda e, dc=dc, oc=oc, bsl=bsl, ps=ps, off=off: e.matmul(
                            ps[:, off:off + 512],
                            wf_sb[:, dc * 512 + oc * 128: dc * 512 + oc * 128 + 128],
                            xn_t[:, bsl:bsl + GT],
                            start=(dc == 0), stop=False))
                    for chh in range(2):
                        b.op("tensor", lambda e, dc=dc, chh=chh, bsl=bsl: e.matmul(
                            PS_C[:, chh * 512:(chh + 1) * 512],
                            af_sb[:, dc * 256 + chh * 128: dc * 256 + chh * 128 + 128],
                            xn_t[:, bsl:bsl + GT],
                            start=(dc == 0), stop=(dc == NDC - 1)),
                            s_tf if chh == 1 else None, 1)
                v_xa = p3_tf_base + NDC * (g + 1)
                b.wait("vector", s_tf, v_xa)
                for chh in range(2):
                    b.op("vector", lambda e, chh=chh: e.tensor_copy(
                        out=xa_sb[:, chh * 512:(chh + 1) * 512],
                        in_=PS_C[:, chh * 512:(chh + 1) * 512]), s_evv, 1)
                v_xaev = s_evv.n
                b.wait("tensor", s_evv, v_xaev)
                for chh in range(2):
                    for oc in range(4):
                        ps, off = PSO[oc]
                        b.op("tensor", lambda e, chh=chh, oc=oc, ps=ps, off=off: e.matmul(
                            ps[:, off:off + 512],
                            bfp_b[:, chh * 512 + oc * 128: chh * 512 + oc * 128 + 128],
                            xa_sb[:, chh * 512:(chh + 1) * 512],
                            start=False, stop=(chh == 1)),
                            s_t if (chh == 1 and oc == 3) else None, 1)
                v_updmm = s_t.n
                b.wait("scalar", s_t, v_updmm)
                for oc in range(4):
                    ps, off = PSO[oc]
                    ssl = oc * 512
                    b.wait("scalar", s_ou[oc], 16 * g)
                    b.op("scalar", lambda e, ps=ps, off=off, oc=oc, ssl=ssl: e.activation(
                        out=stage_out[:, ssl:ssl + 512], in_=ps[:, off:off + 512],
                        func=AF.Identity, bias=bf4_s[:, oc:oc + 1]), s_s, 1)
                    v_ev1 = s_s.n
                    b.wait("vector", s_s, v_ev1)
                    b.wait("vector", s_g3, base_g3 + 16 * 8 * (g + 1))
                    b.op("vector", lambda e, ssl=ssl, oc=oc: e.tensor_add(
                        stage_out[:, ssl:ssl + 512], stage_out[:, ssl:ssl + 512],
                        o_tile[:, oc * 512:(oc + 1) * 512]))
                    b.op("vector", lambda e, ssl=ssl, oc=oc: e.tensor_add(
                        stage_out[:, ssl:ssl + 512], stage_out[:, ssl:ssl + 512],
                        xT_tile[:, oc * 512:(oc + 1) * 512]), s_v, 1)
                    v_add = s_v.n

                    def emit_store(g=g, oc=oc, ssl=ssl, v_add=v_add):
                        b.wait("sync", s_v, v_add)
                        b.op("sync", lambda e, g=g, oc=oc, ssl=ssl: e.dma_start(
                            out=out_d[oc * 128:(oc + 1) * 128, g * GT:(g + 1) * GT],
                            in_=stage_out[:, ssl:ssl + 512]), s_ou[oc], 16)
                    pending_stores.append(emit_store)
                p3_evac_done = s_s.n
            for emit_store in pending_stores:
                emit_store()

            for _oc in range(4):
                b.wait("sync", s_ou[_oc], s_ou[_oc].n)

        @block.sync
        def _(eng):
            for f in b.prog["sync"]:
                f(eng)

        @block.tensor
        def _(eng):
            for f in b.prog["tensor"]:
                f(eng)

        @block.vector
        def _(eng):
            for f in b.prog["vector"]:
                f(eng)

        @block.scalar
        def _(eng):
            for f in b.prog["scalar"]:
                f(eng)

        @block.gpsimd
        def _(eng):
            for f in b.prog["gpsimd"]:
                f(eng)

    return nc


def _host_prep(inputs):
    C_np, rev_np = _build_tables()
    biv = C_np[:, :, BIV_IDX].astype(np.float32)
    bivC_flat = biv.transpose(2, 0, 1).reshape(3, 64)
    bivC_rep = np.repeat(np.concatenate([bivC_flat[c].reshape(1, 64) for c in range(3)],
                                        axis=1), 2, axis=0).astype(np.float32)
    revs_rep = np.repeat(np.repeat(rev_np.reshape(8, 1), 8, axis=1).reshape(1, 64),
                         2, axis=0).astype(np.float32)
    d0 = (SCALE * rev_np.reshape(8, 1) * np.eye(8, dtype=np.float32)).reshape(1, 64)
    d0_rep = np.repeat(d0, 2, axis=0).astype(np.float32)

    x = np.asarray(inputs["L0_attn_x"], dtype=np.float32)[0]
    xTfull = np.ascontiguousarray(x.T)
    Wq = np.asarray(inputs["L0_attn_Wq"], dtype=np.float32)
    Wk = np.asarray(inputs["L0_attn_Wk"], dtype=np.float32)
    Wv = np.asarray(inputs["L0_attn_Wv"], dtype=np.float32)
    bq = np.asarray(inputs["L0_attn_bq"], dtype=np.float32)
    bk = np.asarray(inputs["L0_attn_bk"], dtype=np.float32)
    bv = np.asarray(inputs["L0_attn_bv"], dtype=np.float32)
    bwf = np.asarray(inputs["L0_attn_bias_weight"], dtype=np.float32)
    Wf = np.asarray(inputs["L0_ffn_W"], dtype=np.float32)
    bff = np.asarray(inputs["L0_ffn_bias"], dtype=np.float32)
    Af = np.asarray(inputs["L0_ffn_A"], dtype=np.float32)
    Bff = np.asarray(inputs["L0_ffn_Bf"], dtype=np.float32)
    coeffs = np.asarray(inputs["L0_ffn_coeffs"], dtype=np.float32)
    scales = np.asarray(inputs["L0_ffn_scales"], dtype=np.float32)

    afT_full = np.ascontiguousarray(Af.reshape(256, D).T)
    coeffs_rep = np.ascontiguousarray(np.repeat(coeffs, 16).reshape(2, 128).T)
    scales_rep = np.ascontiguousarray(np.repeat(scales, 16).reshape(2, 128).T)
    cs4 = np.concatenate([coeffs_rep, scales_rep], axis=1).astype(np.float32)

    ident_f = np.eye(128, dtype=np.float32)
    ident_bb = np.eye(128, dtype=np.float32).astype(bf16)
    ones_bb = np.ones((1, 128), dtype=bf16)
    ones128_bb = np.ones((128, 128), dtype=bf16)

    in_maps = []
    for c in range(N_CORES):
        Fsl = slice(c * FS, (c + 1) * FS)
        in_maps.append(dict(
            x_tok=np.ascontiguousarray(x[c * TOK_PC:(c + 1) * TOK_PC, :]),
            xT=np.ascontiguousarray(xTfull[Fsl, :]),
            wq=np.ascontiguousarray(Wq[Fsl, :]),
            wkT=np.ascontiguousarray(Wk[Fsl, :].T),
            wvT=np.ascontiguousarray(Wv[Fsl, :].T),
            bq4=np.ascontiguousarray(bq[Fsl].reshape(4, 128).T),
            bk4=np.ascontiguousarray(bk[Fsl].reshape(4, 128).T),
            bv4=np.ascontiguousarray(bv[Fsl].reshape(4, 128).T),
            bw=np.ascontiguousarray(bwf[2 * c:2 * c + 2, :]),
            wfT=np.ascontiguousarray(Wf[Fsl, :].T),
            bf4=np.ascontiguousarray(bff[Fsl].reshape(4, 128).T),
            afT=afT_full,
            bfp=np.ascontiguousarray(Bff[:, Fsl, :].transpose(0, 2, 1).reshape(256, FS)),
            cs4=cs4,
            bivC=bivC_rep, revs=revs_rep, d0c=d0_rep,
            ident_f=ident_f, ident_b=ident_bb, ones_b=ones_bb, ones128_b=ones128_bb,
        ))
    return in_maps


def kernel(**inputs):
    in_maps = _host_prep(inputs)
    nc = build_program()
    res = run_bass_kernel_spmd(nc, in_maps, core_ids=list(range(N_CORES)))
    shards = [np.asarray(r["out"], dtype=np.float32) for r in res.results]
    out = np.concatenate([s.T for s in shards], axis=1)
    return out.reshape(1, T_TOK, D)


# version 03
# speedup vs baseline: 1.0365x; 1.0365x over previous
import numpy as np
import ml_dtypes
from contextlib import ExitStack

import concourse.bass as bass
import concourse.mybir as mybir
from concourse.bass_utils import run_bass_kernel_spmd

F32 = mybir.dt.float32
BF16 = mybir.dt.bfloat16
AF = mybir.ActivationFunctionType
ALU = mybir.AluOpType
AX = mybir.AxisListType
bf16 = ml_dtypes.bfloat16

N_CORES = 8
T_TOK = 8192
D = 4096
FS = 512
GT = 512
NDC = 32
SCALE = 1.0 / 16.0
BIV_IDX = [4, 5, 6]


def _derived():
    global TOK_PC, NG, NCP, NMC, LNCH
    TOK_PC = T_TOK // N_CORES
    NG = T_TOK // GT
    NCP = T_TOK // 2048
    NMC = T_TOK // 128
    LNCH = TOK_PC // 128


def _set_tokens(t):
    global T_TOK
    T_TOK = t
    _derived()


_derived()

ENGINES = ("sync", "tensor", "vector", "scalar", "gpsimd")


def _build_tables():
    g = [1.0, 1.0, 1.0]
    dim, n = 3, 8
    masks = sorted(range(n), key=lambda m: (bin(m).count("1"), m))
    inv = {m: i for i, m in enumerate(masks)}
    C = np.zeros((n, n, n), dtype=np.float32)
    for i, a in enumerate(masks):
        for j, bb in enumerate(masks):
            sw, at = 0, a >> 1
            while at:
                sw += bin(at & bb).count("1")
                at >>= 1
            s = -1.0 if sw & 1 else 1.0
            for t in range(dim):
                if a >> t & 1 and bb >> t & 1:
                    s *= g[t]
            C[i, j, inv[a ^ bb]] = s
    grades = np.array([bin(m).count("1") for m in masks])
    rev = np.where(grades * (grades - 1) // 2 % 2 == 1, -1.0, 1.0).astype(np.float32)
    return C, rev


class Sem:
    def __init__(self, handle):
        self.h = handle
        self.n = 0

    def inc(self, k=1):
        self.n += k
        return self.n


class B:
    CHAIN_ENGINES = ("vector", "scalar")

    def __init__(self):
        self.prog = {e: [] for e in ENGINES}
        self.last_wait = {e: {} for e in ENGINES}
        self.tail = {e: None for e in ENGINES}
        self.chain_sem = {}

    def op(self, eng, fn, sem=None, k=1, chain=True):
        do_chain = chain and eng in self.CHAIN_ENGINES
        if do_chain and self.tail[eng] is not None:
            self.wait(eng, *self.tail[eng])
        if sem is None and do_chain:
            sem, k = self.chain_sem[eng], 1
        if sem is None:
            self.prog[eng].append(lambda e, fn=fn: fn(e))
            return None
        v = sem.inc(k)
        self.prog[eng].append(lambda e, fn=fn, s=sem, k=k: fn(e).then_inc(s.h, k))
        if do_chain:
            self.tail[eng] = (sem, v)
        return v

    def wait(self, eng, sem, val):
        if val is None or val <= 0:
            return
        seen = self.last_wait[eng].get(id(sem), -1)
        if val <= seen:
            return
        self.last_wait[eng][id(sem)] = val
        self.prog[eng].append(lambda e, s=sem, v=val: e.wait_ge(s.h, v))


def build_program():
    nc = bass.Bass("TRN2", target_bir_lowering=False)

    def din(name, shape, dt=F32):
        return nc.dram_tensor(name, list(shape), dt, kind="ExternalInput").ap()

    x_tok = din("x_tok", [TOK_PC, D])
    xT = din("xT", [FS, T_TOK])
    wq = din("wq", [FS, D])
    wkT = din("wkT", [D, FS])
    wvT = din("wvT", [D, FS])
    bq4 = din("bq4", [128, 4])
    bk4 = din("bk4", [128, 4])
    bv4 = din("bv4", [128, 4])
    bw = din("bw", [2, 3])
    wfT = din("wfT", [D, FS])
    bf4 = din("bf4", [128, 4])
    afT = din("afT", [D, 256])
    bfp = din("bfp", [256, FS])
    cs4 = din("cs4", [128, 4])
    bivC = din("bivC", [2, 192])
    revs = din("revs", [2, 64])
    d0c = din("d0c", [2, 64])
    ident_f = din("ident_f", [128, 128])
    ident_b = din("ident_b", [128, 128], BF16)
    ones_b = din("ones_b", [1, 128], BF16)
    ones128_b = din("ones128_b", [128, 128], BF16)

    out_d = nc.dram_tensor("out", [FS, T_TOK], F32, kind="ExternalOutput").ap()

    m_scr = nc.dram_tensor("m_scr", [2, 64], F32, kind="Internal")
    xnT_own = nc.dram_tensor("xnT_own", [D, TOK_PC], BF16, kind="Internal")
    xn_all = nc.dram_tensor("xn_all", [N_CORES * D, TOK_PC], BF16, kind="Internal")
    wqeT_d = nc.dram_tensor("wqeT_d", [D, FS], BF16, kind="Internal")
    o_scr = nc.dram_tensor("o_scr", [FS, T_TOK], BF16, kind="Internal")
    v_scr = nc.dram_tensor("v_scr", [256, T_TOK], BF16, kind="Internal")

    b = B()

    with ExitStack() as ctx:
        sbt = lambda name, shp, dt: ctx.enter_context(nc.sbuf_tensor(name, shp, dt))
        ident_f_s = sbt("ident_f_s", [128, 128], F32)
        ident_b_s = sbt("ident_b_s", [128, 128], BF16)
        ones_b_s = sbt("ones_b_s", [1, 128], BF16)
        ones128_s = sbt("ones128_s", [128, 128], BF16)
        bk4_s = sbt("bk4_s", [128, 4], F32)
        bf4_s = sbt("bf4_s", [128, 4], F32)
        bqe_s = sbt("bqe_s", [128, 4], F32)
        bv4_s = sbt("bv4_s", [128, 4], F32)
        cs_s = sbt("cs_s", [128, 4], F32)
        bfp_b = sbt("bfp_b", [128, 1024], BF16)
        pt = sbt("pt", [128, 3 * 1024], BF16)
        ptT = sbt("ptT", [128, 4 * 4096], BF16)
        bc_s = sbt("bc_s", [128, 512], F32)
        dstripe = sbt("dstripe", [128, 512], BF16)
        asml = sbt("asml", [128, 40], F32)
        xn_t = sbt("xn_t", [128, 2 * GT], BF16)
        o_stage = sbt("o_stage", [128, 1024], BF16)
        vt_stage = sbt("vt_stage", [128, 1024], BF16)
        eps_s = sbt("eps_s", [128, 1], F32)

        MRUN, MU, BETA, ZC, ZP, ZT, RT, DLT = 0, 4, 8, 12, 20, 24, 28, 32

        def sem(name):
            return Sem(ctx.enter_context(nc.semaphore(name)))

        b.chain_sem["vector"] = sem("s_chv")
        b.chain_sem["scalar"] = sem("s_chs")
        s_ld = sem("s_ld")
        s_g = sem("s_g")
        s_g3 = sem("s_g3")
        s_gset = sem("s_gset")
        s_v = sem("s_v")
        s_s = sem("s_s")
        s_t = sem("s_t")
        s_cc = sem("s_cc")
        s_xn = sem("s_xn")
        s_xt = [sem("s_xt0"), sem("s_xt1")]
        s_tf = sem("s_tf")
        s_evs = sem("s_evs")
        s_evv = sem("s_evv")
        s_wq = sem("s_wq")
        s_qk = sem("s_qk")
        s_mx = sem("s_mx")
        s_exp = sem("s_exp")
        s_ptf = [sem("s_pt0"), sem("s_pt1"), sem("s_pt2")]
        s_tpc = sem("s_tpc")
        s_av = sem("s_av")
        s_rsc = sem("s_rsc")
        s_bc = sem("s_bc")
        s_bcm = sem("s_bcm")
        s_oev = sem("s_oev")
        s_osp = sem("s_osp")
        s_ou = [sem("s_ou0"), sem("s_ou1"), sem("s_ou2"), sem("s_ou3")]
        s_xnst = sem("s_xnst")
        s_lntp = sem("s_lntp")
        s_dst = sem("s_dst")
        s_vst = sem("s_vst")
        s_vtp = sem("s_vtp")

        PS_A = ctx.enter_context(nc.psum_tensor("PS_A", [128, 1024], F32))
        PS_B = ctx.enter_context(nc.psum_tensor("PS_B", [128, 1024], F32))
        PS_C = ctx.enter_context(nc.psum_tensor("PS_C", [128, 1024], F32))
        PS_D = ctx.enter_context(nc.psum_tensor("PS_D", [128, 1024], F32))
        LBUF = (PS_A, PS_B, PS_C)

        block = ctx.enter_context(nc.Block())

        with ExitStack() as s1:
            sb1 = lambda name, shp, dt: s1.enter_context(nc.sbuf_tensor(name, shp, dt))
            wq_sb = sb1("wq_sb", [128, 4 * D], BF16)
            Tf = sb1("Tf", [128, 2048], F32)
            T_b = sb1("T_b", [128, 2048], BF16)
            x_chunk = sb1("x_chunk", [128, 2 * D], F32)
            xsq = sb1("xsq", [128, D], F32)
            xn_bf = sb1("xn_bf", [128, D], BF16)
            xnT_stage = sb1("xnT_stage", [128, D], BF16)
            stageT = sb1("stageT", [128, 512], BF16)
            ln_apply_done = []
            lns = sb1("lns", [128, 16], F32)
            bw_s = sb1("bw_s", [2, 3], F32)
            wk_s = sb1("wk_s", [2, 64], F32)
            M_s = sb1("M_s", [2, 64], F32)
            bivC_s = sb1("bivC_s", [2, 192], F32)
            revs_s = sb1("revs_s", [2, 64], F32)
            d0_s = sb1("d0_s", [2, 64], F32)
            bq4_b = sb1("bq4_b", [128, 4], BF16)

            for src, dst in [(ident_f, ident_f_s), (ident_b, ident_b_s),
                             (ones_b, ones_b_s), (ones128_b, ones128_s),
                             (bk4, bk4_s), (bf4, bf4_s), (cs4, cs_s),
                             (bivC, bivC_s), (revs, revs_s), (d0c, d0_s),
                             (bw, bw_s)]:
                b.op("sync", lambda e, s=src, d=dst: e.dma_start(out=d[:, :], in_=s), s_ld, 16)
            v_consts = s_ld.n
            b.op("gpsimd", lambda e: e.dma_start(out=bv4_s[:, :], in_=bv4), s_gset, 16)
            b.op("gpsimd", lambda e: e.dma_start(out=bq4_b[:, :], in_=bq4), s_gset, 16)
            b.op("gpsimd", lambda e: e.dma_start(
                out=wq_sb.rearrange("p (ic d) -> p ic d", d=D),
                in_=wq.rearrange("(ic p) d -> p ic d", p=128)), s_gset, 16)
            b.op("gpsimd", lambda e: e.dma_start(
                out=bfp_b.rearrange("p (ch o) -> p ch o", o=FS),
                in_=bfp.rearrange("(ch p) o -> p ch o", p=128)), s_gset, 16)
            v_gcasts = s_gset.n

            b.wait("vector", s_ld, v_consts)
            b.op("vector", lambda e: e.memset(eps_s[:, :], 1e-5))
            b.op("vector", lambda e: e.tensor_scalar_mul(wk_s[:, :], bivC_s[:, 0:64], bw_s[:, 0:1]))
            b.op("vector", lambda e: e.scalar_tensor_tensor(
                out=wk_s[:, :], in0=bivC_s[:, 64:128], scalar=bw_s[:, 1:2], in1=wk_s[:, :],
                op0=ALU.mult, op1=ALU.add))
            b.op("vector", lambda e: e.scalar_tensor_tensor(
                out=wk_s[:, :], in0=bivC_s[:, 128:192], scalar=bw_s[:, 2:3], in1=wk_s[:, :],
                op0=ALU.mult, op1=ALU.add))
            b.op("vector", lambda e: e.tensor_mul(M_s[:, :], wk_s[:, :], revs_s[:, :]))
            b.op("vector", lambda e: e.tensor_add(M_s[:, :], M_s[:, :], d0_s[:, :]))
            b.op("vector", lambda e: e.memset(Tf[:, :], 0.0), s_v, 1)
            v_m = s_v.n
            b.wait("sync", s_v, v_m)
            b.op("sync", lambda e: e.dma_start(out=m_scr.ap(), in_=M_s[:, :]), s_ld, 16)
            b.wait("sync", s_ld, s_ld.n)
            for h in range(2):
                for dd in range(32):
                    r0 = h * 256 + dd * 8
                    ic, p0 = r0 // 128, r0 % 128
                    b.op("sync", lambda e, h=h, p0=p0, ic=ic, r0=r0: e.dma_start(
                        out=Tf[p0:p0 + 8, ic * 512 + r0: ic * 512 + r0 + 8],
                        in_=m_scr.ap()[h:h + 1, :].rearrange("one (i j) -> (one i) j", i=8)),
                        s_ld, 16)
            v_scat = s_ld.n
            b.wait("vector", s_ld, v_scat)
            b.op("vector", lambda e: e.tensor_copy(out=T_b[:, :], in_=Tf[:, :]), s_v, 1)
            v_tb = s_v.n

            b.wait("tensor", s_v, v_tb)
            b.wait("tensor", s_gset, v_gcasts)
            fold_ev = []
            for dc in range(NDC):
                half = (dc % 2) * 512
                if dc >= 2:
                    b.wait("tensor", s_v, fold_ev[dc - 2])
                for icc in range(4):
                    b.op("tensor", lambda e, dc=dc, icc=icc, half=half: e.matmul(
                        PS_A[:, half:half + 512],
                        wq_sb[:, icc * D + dc * 128: icc * D + dc * 128 + 128],
                        T_b[:, icc * 512: icc * 512 + 512],
                        start=(icc == 0), stop=(icc == 3)),
                        s_t if icc == 3 else None, 1)
                vt = s_t.n
                b.wait("vector", s_t, vt)
                b.wait("vector", s_wq, s_wq.n)
                b.op("vector", lambda e, half=half: e.tensor_copy(
                    out=stageT[:, :], in_=PS_A[:, half:half + 512]), s_v, 1)
                fold_ev.append(s_v.n)
                b.wait("sync", s_v, s_v.n)
                b.op("sync", lambda e, dc=dc: e.dma_start(
                    out=wqeT_d.ap()[dc * 128:(dc + 1) * 128, :], in_=stageT[:, :]),
                    s_wq, 16)
            v_wqe_stores = s_wq.n

            bq_ps = [(PS_A, 0), (PS_A, 512), (PS_B, 0), (PS_B, 512)]
            b.wait("tensor", s_v, fold_ev[-1])
            b.wait("tensor", s_v, fold_ev[-2])
            for jc in range(4):
                ps, off = bq_ps[jc]
                for icc in range(4):
                    b.op("tensor", lambda e, jc=jc, icc=icc, ps=ps, off=off: e.matmul(
                        ps[:, off:off + 1],
                        T_b[:, icc * 512 + jc * 128: icc * 512 + jc * 128 + 128],
                        bq4_b[:, icc:icc + 1],
                        start=(icc == 0), stop=(icc == 3)),
                        s_t if (jc == 3 and icc == 3) else None, 1)
            v_bq = s_t.n
            b.wait("vector", s_t, v_bq)
            for jc in range(4):
                ps, off = bq_ps[jc]
                b.op("vector", lambda e, jc=jc, ps=ps, off=off: e.tensor_copy(
                    out=bqe_s[:, jc:jc + 1], in_=ps[:, off:off + 1]),
                    s_v if jc == 3 else None, 1)

            b.wait("vector", s_gset, v_gcasts)
            b.op("vector", lambda e: e.tensor_mul(cs_s[:, 0:2], cs_s[:, 0:2], cs_s[:, 2:4]))
            for chh in range(2):
                b.op("vector", lambda e, chh=chh: e.tensor_scalar_mul(
                    bfp_b[:, chh * 512: chh * 512 + 512],
                    bfp_b[:, chh * 512: chh * 512 + 512],
                    cs_s[:, chh:chh + 1]))

            b.wait("vector", s_wq, v_wqe_stores)
            last_apply = 0
            last_tp = 0
            for i in range(LNCH):
                xb = (i % 2) * D
                if i >= 2:
                    b.wait("sync", s_s, ln_apply_done[i - 2])
                b.op("sync", lambda e, i=i, xb=xb: e.dma_start(
                    out=x_chunk[:, xb:xb + D], in_=x_tok[i * 128:(i + 1) * 128, :]), s_xn, 16)
                v_x = s_xn.n
                b.wait("vector", s_xn, v_x)
                b.op("vector", lambda e, xb=xb: e.reduce_sum(lns[:, 0:1], x_chunk[:, xb:xb + D], axis=AX.X))
                b.wait("scalar", s_xn, v_x)
                b.op("scalar", lambda e, xb=xb: e.activation(
                    out=xsq[:, :], in_=x_chunk[:, xb:xb + D], func=AF.Square), s_s, 1)
                b.wait("vector", s_s, s_s.n)
                b.op("vector", lambda e: e.reduce_sum(lns[:, 1:2], xsq[:, :], axis=AX.X))
                b.op("vector", lambda e: e.tensor_scalar_mul(lns[:, 6:7], lns[:, 0:1], 1.0 / D))
                b.op("vector", lambda e: e.tensor_scalar_mul(lns[:, 7:8], lns[:, 1:2], 1.0 / D))
                b.op("vector", lambda e: e.tensor_mul(lns[:, 8:9], lns[:, 6:7], lns[:, 6:7]))
                b.op("vector", lambda e: e.tensor_sub(lns[:, 9:10], lns[:, 7:8], lns[:, 8:9]), s_v, 1)
                b.wait("scalar", s_v, s_v.n)
                b.op("scalar", lambda e: e.activation(
                    out=lns[:, 10:11], in_=lns[:, 9:10], func=AF.Sqrt, bias=eps_s[:, 0:1]), s_s, 1)
                b.wait("vector", s_s, s_s.n)
                b.op("vector", lambda e: e.reciprocal(out=lns[:, 11:12], in_=lns[:, 10:11]))
                b.op("vector", lambda e: e.scalar_tensor_tensor(
                    out=lns[:, 12:13], in0=lns[:, 6:7], scalar=-1.0, in1=lns[:, 11:12],
                    op0=ALU.mult, op1=ALU.mult), s_v, 1)
                b.wait("scalar", s_v, s_v.n)
                if i >= 1:
                    b.wait("scalar", s_lntp, 16 * i)
                b.op("scalar", lambda e, xb=xb: e.activation(
                    out=xn_bf[:, :], in_=x_chunk[:, xb:xb + D], func=AF.Identity,
                    bias=lns[:, 12:13], scale=lns[:, 11:12]), s_s, 1)
                v_apply = s_s.n
                ln_apply_done.append(v_apply)
                b.wait("sync", s_s, v_apply)
                if i >= 1:
                    b.wait("sync", s_xnst, 16 * i)
                b.op("sync", lambda e: e.dma_start(
                    out=xnT_stage.rearrange("p (dc t) -> p dc t", t=128),
                    in_=xn_bf[:, :], transpose=True), s_lntp, 16)
                b.wait("sync", s_lntp, s_lntp.n)
                b.op("sync", lambda e, i=i: e.dma_start(
                    out=xnT_own.ap().rearrange("(dc p) t -> p dc t", p=128)[
                        :, :, i * 128:(i + 1) * 128],
                    in_=xnT_stage.rearrange("p (dc t) -> p dc t", t=128)),
                    s_xnst, 16)
            v_xnst = s_xnst.n

            b.wait("gpsimd", s_xnst, v_xnst)
            b.op("gpsimd", lambda e: e.collective_compute(
                "AllGather", ALU.bypass,
                replica_groups=[list(range(N_CORES))],
                ins=[xnT_own.ap().opt()], outs=[xn_all.ap().opt()]), s_cc, 1)
            v_cc = s_cc.n


        def xn_tile_ap(g, dc):
            r = (g * GT) // TOK_PC
            t0 = (g * GT) % TOK_PC
            return xn_all.ap()[r * D + dc * 128: r * D + dc * 128 + 128, t0:t0 + GT]

        with ExitStack() as s2:
            sb2 = lambda name, shp, dt: s2.enter_context(nc.sbuf_tensor(name, shp, dt))
            wqe_sb = sb2("wqe_sb", [128, NDC * 256], BF16)
            wk_sb = sb2("wk_sb", [128, NDC * 256], BF16)
            wv_sb = sb2("wv_sb", [128, NDC * 256], BF16)
            q_sb = sb2("q_sb", [128, 2 * T_TOK], BF16)
            k_sb = sb2("k_sb", [128, 2 * T_TOK], BF16)
            v_sb = sb2("v_sb", [128, NMC * 256], BF16)

            p1_last_mm = [0, 0]

            for hL in range(2):
                b.wait("gpsimd", s_wq, v_wqe_stores)
                if hL == 1:
                    b.wait("gpsimd", *p1_last_mm[0])
                b.op("gpsimd", lambda e, hL=hL: e.dma_start(
                    out=wqe_sb.rearrange("p (dc j) -> p dc j", j=256),
                    in_=wqeT_d.ap().rearrange("(dc p) j -> p dc j", p=128)[
                        :, :, hL * 256:(hL + 1) * 256]), s_g, 16)
                b.op("gpsimd", lambda e, hL=hL: e.dma_start(
                    out=wk_sb.rearrange("p (dc j) -> p dc j", j=256),
                    in_=wkT.rearrange("(dc p) j -> p dc j", p=128)[
                        :, :, hL * 256:(hL + 1) * 256]), s_g, 16)
                b.op("gpsimd", lambda e, hL=hL: e.dma_start(
                    out=wv_sb.rearrange("p (dc j) -> p dc j", j=256),
                    in_=wvT.rearrange("(dc p) j -> p dc j", p=128)[
                        :, :, hL * 256:(hL + 1) * 256]), s_g, 16)
                v_wpre = s_g.n
                b.wait("tensor", s_g, v_wpre)
                if hL == 1:
                    b.wait("tensor", s_oev, s_oev.n)
                    b.wait("tensor", s_exp, s_exp.n)
                    b.wait("tensor", s_bc, s_bc.n)

                b.wait("sync", s_cc, v_cc)
                base_tf = s_tf.n
                base_xt = [s_xt[0].n, s_xt[1].n]

                def p1_emit_loads(g):
                    for dc in range(NDC):
                        flat = g * NDC + dc
                        if flat >= 2:
                            b.wait("sync", s_tf, base_tf + flat - 1)
                        bsl = (flat % 2) * GT
                        b.op("sync", lambda e, g=g, dc=dc, bsl=bsl: e.dma_start(
                            out=xn_t[:, bsl:bsl + GT], in_=xn_tile_ap(g, dc)),
                            s_xt[flat % 2], 16)

                p1_evs_base = s_evs.n
                p1_tf_base = s_tf.n
                vst_base = s_vst.n
                for g in range(NG):
                    p1_emit_loads(g)
                    for dc in range(NDC):
                        flat = g * NDC + dc
                        b.wait("tensor", s_xt[flat % 2],
                               base_xt[flat % 2] + 16 * (flat // 2 + 1))
                        bsl = (flat % 2) * GT
                        for wsb, ps, ei in ((wqe_sb, PS_A, 1), (wk_sb, PS_B, 3), (wv_sb, PS_C, 5)):
                            for jc in range(2):
                                if dc == 0 and g > 0:
                                    b.wait("tensor", s_evs, p1_evs_base + 6 * (g - 1) + ei + jc)
                                b.op("tensor", lambda e, dc=dc, jc=jc, bsl=bsl, wsb=wsb, ps=ps: e.matmul(
                                    ps[:, jc * 512:(jc + 1) * 512],
                                    wsb[:, dc * 256 + jc * 128: dc * 256 + jc * 128 + 128],
                                    xn_t[:, bsl:bsl + GT],
                                    start=(dc == 0), stop=(dc == NDC - 1)),
                                    s_tf if (ps is PS_C and jc == 1) else None, 1)
                    v_gmm = p1_tf_base + NDC * (g + 1)
                    b.wait("scalar", s_tf, v_gmm)
                    for jc in range(2):
                        jcg = hL * 2 + jc
                        b.op("scalar", lambda e, g=g, jc=jc, jcg=jcg: e.activation(
                            out=q_sb[:, jc * T_TOK + g * GT: jc * T_TOK + g * GT + GT],
                            in_=PS_A[:, jc * 512:(jc + 1) * 512], func=AF.Identity,
                            bias=bqe_s[:, jcg:jcg + 1]), s_evs, 1)
                    for jc in range(2):
                        jcg = hL * 2 + jc
                        b.op("scalar", lambda e, g=g, jc=jc, jcg=jcg: e.activation(
                            out=k_sb[:, jc * T_TOK + g * GT: jc * T_TOK + g * GT + GT],
                            in_=PS_B[:, jc * 512:(jc + 1) * 512], func=AF.Identity,
                            bias=bk4_s[:, jcg:jcg + 1]), s_evs, 1)
                    b.wait("scalar", s_vst, vst_base + 16 * g)
                    for jc in range(2):
                        jcg = hL * 2 + jc
                        b.op("scalar", lambda e, jc=jc, jcg=jcg: e.activation(
                            out=vt_stage[:, jc * 512:(jc + 1) * 512],
                            in_=PS_C[:, jc * 512:(jc + 1) * 512], func=AF.Identity,
                            bias=bv4_s[:, jcg:jcg + 1]), s_evs, 1)
                    v_vev = s_evs.n
                    b.wait("sync", s_evs, v_vev)
                    b.op("sync", lambda e, g=g: e.dma_start(
                        out=v_scr.ap().rearrange("(ic p) t -> p ic t", p=128)[
                            :, :, g * GT:(g + 1) * GT],
                        in_=vt_stage.rearrange("p (ic t) -> p ic t", t=512)),
                        s_vst, 16)
                b.wait("sync", s_vst, s_vst.n)
                if hL == 1:
                    b.wait("sync", s_av, s_av.n)
                for ic in range(2):
                    b.op("sync", lambda e, ic=ic: e.dma_start(
                        out=v_sb.rearrange("p (mc f) -> p mc f", f=256)[
                            :, :, ic * 128:(ic + 1) * 128],
                        in_=v_scr.ap()[ic * 128:(ic + 1) * 128, :].rearrange(
                            "ii (mc t) -> ii mc t", t=128),
                        transpose=True), s_vtp, 16)
                v_vtp = s_vtp.n

                p1_last_mm[hL] = (s_tf, s_tf.n)
                lguard = [(s_evs, s_evs.n)] * 3
                ptguard = [(s_ptf[i], s_ptf[i].n) for i in range(3)]
                ptTguard = [(s_av, s_av.n)] * 4
                og_state = {"g": None if hL == 0 else (s_oev, s_oev.n)}
                last_dstripe = s_dst.n
                last_zupd = s_v.n
                last_final_dst = s_dst.n
                pending = []
                pair_tp = {}
                rt_ready = None
                final_G = None
                step = 0

                def drain(n):
                    for _ in range(min(n, len(pending))):
                        pending.pop(0)()

                def flush():
                    while pending:
                        pending.pop(0)()

                def queue_av(cpav, v_vtp):
                    for ci2 in range(2):
                        cav = cpav * 2 + ci2
                        tslot = cav % 4
                        first = (cav == 0)
                        mark = pair_tp[cav]
                        ef = {"v": False}
                        for msub in range(8):
                            for dch in range(2):
                                st = (first and msub == 0)
                                sp = (msub == 7)

                                def part(cav=cav, msub=msub, dch=dch, tslot=tslot,
                                         st=st, sp=sp, first=first, mark=mark,
                                         vv=v_vtp, ef=ef):
                                    if not ef["v"]:
                                        b.wait("tensor", s_vtp, vv)
                                        b.wait("tensor", s_tpc, mark)
                                        if og_state["g"] is not None:
                                            b.wait("tensor", *og_state["g"])
                                        ef["v"] = True
                                    b.op("tensor",
                                         lambda e: e.matmul(
                                             PS_D[:, dch * 512:(dch + 1) * 512],
                                             v_sb[:, (cav * 8 + msub) * 256 + dch * 128:
                                                  (cav * 8 + msub) * 256 + dch * 128 + 128],
                                             ptT[:, tslot * 4096 + msub * 512:
                                                 tslot * 4096 + msub * 512 + 512],
                                             start=st, stop=sp,
                                             skip_group_check=(not first)),
                                         s_av if (msub == 7 and dch == 1) else None, 1)
                                pending.append(part)
                        ptTguard[tslot] = (s_av, _av_base[0] + ci2 + 1)

                def queue_rescale(scale_col_base, sem_src, v_src, is_final, Gf=None):
                    b.wait("gpsimd", s_bcm, s_bcm.n)
                    b.wait("gpsimd", sem_src, v_src)
                    for t in range(4):
                        b.op("gpsimd", lambda e, t=t, base=scale_col_base: e.tensor_scalar_mul(
                            dstripe[:, t * 128:(t + 1) * 128], ident_b_s[:, :],
                            asml[:, base + t:base + t + 1]), s_dst, 1, chain=False)
                    v_dst = s_dst.n

                    def resc():
                        bufb = (glob_step[0] + 1) % 3
                        Lb = LBUF[bufb]
                        b.wait("tensor", s_dst, v_dst)
                        if lguard[bufb] is not None:
                            b.wait("tensor", *lguard[bufb])
                        b.op("tensor", lambda e, Lb=Lb: e.matmul(
                            Lb[:, 0:512], ones128_s[:, :], dstripe[:, :],
                            start=True, stop=True), s_bcm, 1)
                        b.wait("vector", s_bcm, s_bcm.n)
                        b.op("vector", lambda e, Lb=Lb: e.tensor_copy(
                            out=bc_s[:, :], in_=Lb[:, 0:512]), s_bc, 1)
                        lguard[bufb] = (s_bc, s_bc.n)
                        b.wait("vector", s_av, s_av.n)
                        for dch in range(2):
                            b.op("vector", lambda e, dch=dch: e.tensor_mul(
                                PS_D[:, dch * 512:(dch + 1) * 512],
                                PS_D[:, dch * 512:(dch + 1) * 512],
                                bc_s[:, :]), s_rsc if dch == 1 else None, 1)
                        og_state["g"] = (s_rsc, s_rsc.n)
                        if is_final:
                            v_final_rsc = s_rsc.n
                            b.wait("scalar", s_rsc, v_final_rsc)
                            b.wait("scalar", s_osp, s_osp.n)
                            for dch in range(2):
                                b.op("scalar", lambda e, dch=dch: e.copy(
                                    out=o_stage[:, dch * 512:(dch + 1) * 512],
                                    in_=PS_D[:, dch * 512:(dch + 1) * 512]), s_oev, 1)
                            v_oev = s_oev.n
                            og_state["g"] = (s_oev, v_oev)
                            b.wait("sync", s_oev, v_oev)
                            b.op("sync", lambda e, Gf=Gf, hL=hL: e.dma_start(
                                out=o_scr.ap().rearrange("(r p) t -> p r t", p=128)[
                                    :, hL * 2: hL * 2 + 2, Gf * GT:(Gf + 1) * GT],
                                in_=o_stage.rearrange("p (r t) -> p r t", t=512)), s_osp, 16)
                    pending.append(resc)
                    return v_dst

                glob_step = [0]
                _av_base = [0]
                for G in range(NG):
                    for cp in range(NCP):
                        tp_records = {cp * 2: [], cp * 2 + 1: []}
                        for t in range(4):
                            for cisub in range(2):
                                c = cp * 2 + cisub
                                buf = step % 3
                                L = LBUF[buf]
                                glob_step[0] = step
                                if lguard[buf] is not None:
                                    b.wait("tensor", *lguard[buf])
                                for icc in range(2):
                                    for nb2 in range(2):
                                        b.op("tensor",
                                             lambda e, G=G, t=t, c=c, icc=icc, nb2=nb2, L=L: e.matmul(
                                                 L[:, nb2 * 512:(nb2 + 1) * 512],
                                                 q_sb[:, icc * T_TOK + G * GT + t * 128:
                                                      icc * T_TOK + G * GT + t * 128 + 128],
                                                 k_sb[:, icc * T_TOK + c * 1024 + nb2 * 512:
                                                      icc * T_TOK + c * 1024 + nb2 * 512 + 512],
                                                 start=(icc == 0), stop=(icc == 1)),
                                             s_qk if (icc == 1 and nb2 == 1) else None, 1)
                                v_qkstep = s_qk.n
                                drain(5)
                                b.wait("vector", s_qk, v_qkstep)
                                b.op("vector", lambda e, L=L, cisub=cisub: e.reduce_max(
                                    asml[:, MU + cisub:MU + cisub + 1], L[:, :],
                                    axis=AX.X, negate=True), s_mx, 1)
                                if cisub == 1:
                                    b.op("vector", lambda e: e.tensor_tensor(
                                        out=asml[:, MU:MU + 1], in0=asml[:, MU:MU + 1],
                                        in1=asml[:, MU + 1:MU + 2], op=ALU.min))
                                    if cp == 0:
                                        b.op("vector", lambda e, t=t: e.tensor_copy(
                                            out=asml[:, MRUN + t:MRUN + t + 1],
                                            in_=asml[:, MU:MU + 1]), s_v, 1)
                                    else:
                                        b.wait("vector", s_dst, last_dstripe)
                                        b.op("vector", lambda e, t=t: e.tensor_tensor(
                                            out=asml[:, DLT:DLT + 1], in0=asml[:, MU:MU + 1],
                                            in1=asml[:, MRUN + t:MRUN + t + 1], op=ALU.min))
                                        b.op("vector", lambda e, t=t: e.tensor_sub(
                                            asml[:, BETA + t:BETA + t + 1],
                                            asml[:, DLT:DLT + 1],
                                            asml[:, MRUN + t:MRUN + t + 1]))
                                        b.op("vector", lambda e, t=t: e.tensor_copy(
                                            out=asml[:, MRUN + t:MRUN + t + 1],
                                            in_=asml[:, DLT:DLT + 1]), s_v, 1)
                                    v_mready = s_v.n
                                    for cc2 in range(2):
                                        st2 = step - 1 + cc2
                                        slot2 = st2 % 3
                                        L2 = LBUF[slot2]
                                        b.wait("scalar", s_v, v_mready)
                                        b.wait("scalar", s_v, last_zupd)
                                        if ptguard[slot2] is not None:
                                            b.wait("scalar", *ptguard[slot2])
                                        b.op("scalar", lambda e, L2=L2, slot2=slot2, t=t, cc2=cc2: e.activation(
                                            out=pt[:, slot2 * 1024:(slot2 + 1) * 1024],
                                            in_=L2[:, :], func=AF.Exp,
                                            bias=asml[:, MRUN + t:MRUN + t + 1],
                                            accum_out=asml[:, ZC + 2 * t + cc2:ZC + 2 * t + cc2 + 1]),
                                            s_exp, 1)
                                        lguard[slot2] = (s_exp, s_exp.n)
                                        cthis = cp * 2 + cc2
                                        tslot = cthis % 4
                                        b.wait("sync", s_exp, s_exp.n)
                                        if t == 0 and ptTguard[tslot] is not None:
                                            b.wait("sync", *ptTguard[tslot])
                                        vtp = b.op("sync", lambda e, slot2=slot2, tslot=tslot, t=t: e.dma_start(
                                            out=ptT.rearrange("m (sl s q) -> m (sl s) q", q=GT, s=8)[
                                                :, tslot * 8:(tslot + 1) * 8, t * 128:(t + 1) * 128],
                                            in_=pt[:, slot2 * 1024:(slot2 + 1) * 1024],
                                            transpose=True), s_ptf[slot2], 16)
                                        ptguard[slot2] = (s_ptf[slot2], vtp)
                                        tp_records[cthis].append((s_ptf[slot2], vtp))
                                step += 1
                        for cch in (cp * 2, cp * 2 + 1):
                            for semx, valx in tp_records[cch]:
                                b.wait("sync", semx, valx)
                            b.op("sync", lambda e: e.nop(), s_tpc, 1)
                            pair_tp[cch] = s_tpc.n
                        b.wait("vector", s_exp, s_exp.n)
                        zc_even = asml[:, ZC:ZC + 8].rearrange("p (t two) -> p t two", two=2)[:, :, 0:1]
                        zc_odd = asml[:, ZC:ZC + 8].rearrange("p (t two) -> p t two", two=2)[:, :, 1:2]
                        zp_v = asml[:, ZP:ZP + 4].rearrange("p (t one) -> p t one", one=1)
                        b.op("vector", lambda e: e.tensor_tensor(
                            out=zp_v, in0=zc_even, in1=zc_odd, op=ALU.add), s_v, 1)
                        if cp == 0:
                            b.op("vector", lambda e: e.tensor_copy(
                                out=asml[:, ZT:ZT + 4], in_=asml[:, ZP:ZP + 4]), s_v, 1)
                        else:
                            b.wait("scalar", s_v, s_v.n - 1)
                            b.op("scalar", lambda e: e.activation(
                                out=asml[:, BETA:BETA + 4], in_=asml[:, BETA:BETA + 4],
                                func=AF.Exp), s_s, 1)
                            b.wait("vector", s_s, s_s.n)
                            b.op("vector", lambda e: e.tensor_mul(
                                asml[:, ZT:ZT + 4], asml[:, ZT:ZT + 4],
                                asml[:, BETA:BETA + 4]))
                            b.op("vector", lambda e: e.tensor_add(
                                asml[:, ZT:ZT + 4], asml[:, ZT:ZT + 4],
                                asml[:, ZP:ZP + 4]), s_v, 1)
                        last_zupd = s_v.n
                        flush()
                        _av_base[0] = s_av.n
                        if cp > 0:
                            last_dstripe = queue_rescale(BETA, s_s, s_s.n, False)
                        elif final_G is not None:
                            last_final_dst = queue_rescale(RT, s_v, rt_ready, True, Gf=final_G)
                            final_G = None
                        queue_av(cp, v_vtp)
                        if cp == NCP - 1:
                            b.wait("vector", s_exp, s_exp.n)
                            b.wait("vector", s_dst, last_final_dst)
                            for t in range(4):
                                b.op("vector", lambda e, t=t: e.reciprocal(
                                    out=asml[:, RT + t:RT + t + 1],
                                    in_=asml[:, ZT + t:ZT + t + 1]), s_v if t == 3 else None, 1)
                            rt_ready = s_v.n
                            final_G = G
                flush()
                _av_base[0] = s_av.n
                queue_rescale(RT, s_v, rt_ready, True, Gf=final_G)
                final_G = None
                flush()
            v_osp_all = s_osp.n


        with ExitStack() as s3:
            sb3 = lambda name, shp, dt: s3.enter_context(nc.sbuf_tensor(name, shp, dt))
            wf_sb = sb3("wf_sb", [128, NDC * 512], BF16)
            af_sb = sb3("af_sb", [128, NDC * 256], BF16)
            o_tile = sb3("o_tile", [128, 4 * 512], F32)
            xT_tile = sb3("xT_tile", [128, 4 * 512], F32)
            xa_sb = sb3("xa_sb", [128, 1024], BF16)
            stage_out = sb3("stage_out", [128, 4 * 512], F32)

            b.wait("gpsimd", *p1_last_mm[1])
            b.op("gpsimd", lambda e: e.dma_start(
                out=wf_sb.rearrange("p (dc o) -> p dc o", o=512),
                in_=wfT.rearrange("(dc p) o -> p dc o", p=128)), s_g, 16)
            b.op("gpsimd", lambda e: e.dma_start(
                out=af_sb.rearrange("p (dc o) -> p dc o", o=256),
                in_=afT.rearrange("(dc p) o -> p dc o", p=128)), s_g, 16)
            v_p3w = s_g.n
            b.wait("tensor", s_g, v_p3w)
            b.wait("tensor", s_exp, s_exp.n)
            b.wait("tensor", s_bc, s_bc.n)
            b.wait("tensor", s_oev, s_oev.n)

            base_tf = s_tf.n
            base_xt = [s_xt[0].n, s_xt[1].n]

            def p3_emit_loads(g):
                for dc in range(NDC):
                    flat = g * NDC + dc
                    if flat >= 2:
                        b.wait("sync", s_tf, base_tf + flat - 1)
                    bsl = (flat % 2) * GT
                    b.op("sync", lambda e, g=g, dc=dc, bsl=bsl: e.dma_start(
                        out=xn_t[:, bsl:bsl + GT], in_=xn_tile_ap(g, dc)),
                        s_xt[flat % 2], 16)

            b.wait("gpsimd", s_osp, v_osp_all)
            base_g3 = s_g3.n
            for g in range(NG):
                if g >= 1:
                    for _oc in range(4):
                        b.wait("gpsimd", s_ou[_oc], 16 * g)
                for oc in range(4):
                    b.op("gpsimd", lambda e, g=g, oc=oc: e.dma_start(
                        out=o_tile[:, oc * 512:(oc + 1) * 512],
                        in_=o_scr.ap()[oc * 128:(oc + 1) * 128, g * GT:(g + 1) * GT]),
                        s_g3, 16)
                    b.op("gpsimd", lambda e, g=g, oc=oc: e.dma_start(
                        out=xT_tile[:, oc * 512:(oc + 1) * 512],
                        in_=xT[oc * 128:(oc + 1) * 128, g * GT:(g + 1) * GT]),
                        s_g3, 16)

            PSO = [(PS_A, 0), (PS_A, 512), (PS_B, 0), (PS_B, 512)]
            ev3_base = s_evv.n
            p3_tf_base = s_tf.n
            p3_evac_done = 0
            pending_stores = []
            for g in range(NG):
                p3_emit_loads(g)
                for emit_store in pending_stores:
                    emit_store()
                pending_stores = []
                for dc in range(NDC):
                    flat = g * NDC + dc
                    b.wait("tensor", s_xt[flat % 2],
                           base_xt[flat % 2] + 16 * (flat // 2 + 1))
                    bsl = (flat % 2) * GT
                    if dc == 0 and g > 0:
                        b.wait("tensor", s_s, p3_evac_done)
                        b.wait("tensor", s_evv, ev3_base + 2 * g)
                    for oc in range(4):
                        ps, off = PSO[oc]
                        b.op("tensor", lambda e, dc=dc, oc=oc, bsl=bsl, ps=ps, off=off: e.matmul(
                            ps[:, off:off + 512],
                            wf_sb[:, dc * 512 + oc * 128: dc * 512 + oc * 128 + 128],
                            xn_t[:, bsl:bsl + GT],
                            start=(dc == 0), stop=False))
                    for chh in range(2):
                        b.op("tensor", lambda e, dc=dc, chh=chh, bsl=bsl: e.matmul(
                            PS_C[:, chh * 512:(chh + 1) * 512],
                            af_sb[:, dc * 256 + chh * 128: dc * 256 + chh * 128 + 128],
                            xn_t[:, bsl:bsl + GT],
                            start=(dc == 0), stop=(dc == NDC - 1)),
                            s_tf if chh == 1 else None, 1)
                v_xa = p3_tf_base + NDC * (g + 1)
                b.wait("vector", s_tf, v_xa)
                for chh in range(2):
                    b.op("vector", lambda e, chh=chh: e.tensor_copy(
                        out=xa_sb[:, chh * 512:(chh + 1) * 512],
                        in_=PS_C[:, chh * 512:(chh + 1) * 512]), s_evv, 1)
                v_xaev = s_evv.n
                b.wait("tensor", s_evv, v_xaev)
                for chh in range(2):
                    for oc in range(4):
                        ps, off = PSO[oc]
                        b.op("tensor", lambda e, chh=chh, oc=oc, ps=ps, off=off: e.matmul(
                            ps[:, off:off + 512],
                            bfp_b[:, chh * 512 + oc * 128: chh * 512 + oc * 128 + 128],
                            xa_sb[:, chh * 512:(chh + 1) * 512],
                            start=False, stop=(chh == 1)),
                            s_t if (chh == 1 and oc == 3) else None, 1)
                v_updmm = s_t.n
                b.wait("scalar", s_t, v_updmm)
                for oc in range(4):
                    ps, off = PSO[oc]
                    ssl = oc * 512
                    b.wait("scalar", s_ou[oc], 16 * g)
                    b.op("scalar", lambda e, ps=ps, off=off, oc=oc, ssl=ssl: e.activation(
                        out=stage_out[:, ssl:ssl + 512], in_=ps[:, off:off + 512],
                        func=AF.Identity, bias=bf4_s[:, oc:oc + 1]), s_s, 1)
                    v_ev1 = s_s.n
                    b.wait("vector", s_s, v_ev1)
                    b.wait("vector", s_g3, base_g3 + 16 * 8 * (g + 1))
                    b.op("vector", lambda e, ssl=ssl, oc=oc: e.tensor_add(
                        stage_out[:, ssl:ssl + 512], stage_out[:, ssl:ssl + 512],
                        o_tile[:, oc * 512:(oc + 1) * 512]))
                    b.op("vector", lambda e, ssl=ssl, oc=oc: e.tensor_add(
                        stage_out[:, ssl:ssl + 512], stage_out[:, ssl:ssl + 512],
                        xT_tile[:, oc * 512:(oc + 1) * 512]), s_v, 1)
                    v_add = s_v.n

                    def emit_store(g=g, oc=oc, ssl=ssl, v_add=v_add):
                        b.wait("sync", s_v, v_add)
                        b.op("sync", lambda e, g=g, oc=oc, ssl=ssl: e.dma_start(
                            out=out_d[oc * 128:(oc + 1) * 128, g * GT:(g + 1) * GT],
                            in_=stage_out[:, ssl:ssl + 512]), s_ou[oc], 16)
                    pending_stores.append(emit_store)
                p3_evac_done = s_s.n
            for emit_store in pending_stores:
                emit_store()

            for _oc in range(4):
                b.wait("sync", s_ou[_oc], s_ou[_oc].n)

        @block.sync
        def _(eng):
            for f in b.prog["sync"]:
                f(eng)

        @block.tensor
        def _(eng):
            for f in b.prog["tensor"]:
                f(eng)

        @block.vector
        def _(eng):
            for f in b.prog["vector"]:
                f(eng)

        @block.scalar
        def _(eng):
            for f in b.prog["scalar"]:
                f(eng)

        @block.gpsimd
        def _(eng):
            for f in b.prog["gpsimd"]:
                f(eng)

    return nc


def _host_prep(inputs):
    C_np, rev_np = _build_tables()
    biv = C_np[:, :, BIV_IDX].astype(np.float32)
    bivC_flat = biv.transpose(2, 0, 1).reshape(3, 64)
    bivC_rep = np.repeat(np.concatenate([bivC_flat[c].reshape(1, 64) for c in range(3)],
                                        axis=1), 2, axis=0).astype(np.float32)
    revs_rep = np.repeat(np.repeat(rev_np.reshape(8, 1), 8, axis=1).reshape(1, 64),
                         2, axis=0).astype(np.float32)
    d0 = (SCALE * rev_np.reshape(8, 1) * np.eye(8, dtype=np.float32)).reshape(1, 64)
    d0_rep = np.repeat(d0, 2, axis=0).astype(np.float32)

    x = np.asarray(inputs["L0_attn_x"], dtype=np.float32)[0]
    xTfull = np.ascontiguousarray(x.T)
    Wq = np.asarray(inputs["L0_attn_Wq"], dtype=np.float32)
    Wk = np.asarray(inputs["L0_attn_Wk"], dtype=np.float32)
    Wv = np.asarray(inputs["L0_attn_Wv"], dtype=np.float32)
    bq = np.asarray(inputs["L0_attn_bq"], dtype=np.float32)
    bk = np.asarray(inputs["L0_attn_bk"], dtype=np.float32)
    bv = np.asarray(inputs["L0_attn_bv"], dtype=np.float32)
    bwf = np.asarray(inputs["L0_attn_bias_weight"], dtype=np.float32)
    Wf = np.asarray(inputs["L0_ffn_W"], dtype=np.float32)
    bff = np.asarray(inputs["L0_ffn_bias"], dtype=np.float32)
    Af = np.asarray(inputs["L0_ffn_A"], dtype=np.float32)
    Bff = np.asarray(inputs["L0_ffn_Bf"], dtype=np.float32)
    coeffs = np.asarray(inputs["L0_ffn_coeffs"], dtype=np.float32)
    scales = np.asarray(inputs["L0_ffn_scales"], dtype=np.float32)

    afT_full = np.ascontiguousarray(Af.reshape(256, D).T)
    coeffs_rep = np.ascontiguousarray(np.repeat(coeffs, 16).reshape(2, 128).T)
    scales_rep = np.ascontiguousarray(np.repeat(scales, 16).reshape(2, 128).T)
    cs4 = np.concatenate([coeffs_rep, scales_rep], axis=1).astype(np.float32)

    ident_f = np.eye(128, dtype=np.float32)
    ident_bb = np.eye(128, dtype=np.float32).astype(bf16)
    ones_bb = np.ones((1, 128), dtype=bf16)
    ones128_bb = np.ones((128, 128), dtype=bf16)

    in_maps = []
    for c in range(N_CORES):
        Fsl = slice(c * FS, (c + 1) * FS)
        in_maps.append(dict(
            x_tok=np.ascontiguousarray(x[c * TOK_PC:(c + 1) * TOK_PC, :]),
            xT=np.ascontiguousarray(xTfull[Fsl, :]),
            wq=np.ascontiguousarray(Wq[Fsl, :]),
            wkT=np.ascontiguousarray(Wk[Fsl, :].T),
            wvT=np.ascontiguousarray(Wv[Fsl, :].T),
            bq4=np.ascontiguousarray(bq[Fsl].reshape(4, 128).T),
            bk4=np.ascontiguousarray(bk[Fsl].reshape(4, 128).T),
            bv4=np.ascontiguousarray(bv[Fsl].reshape(4, 128).T),
            bw=np.ascontiguousarray(bwf[2 * c:2 * c + 2, :]),
            wfT=np.ascontiguousarray(Wf[Fsl, :].T),
            bf4=np.ascontiguousarray(bff[Fsl].reshape(4, 128).T),
            afT=afT_full,
            bfp=np.ascontiguousarray(Bff[:, Fsl, :].transpose(0, 2, 1).reshape(256, FS)),
            cs4=cs4,
            bivC=bivC_rep, revs=revs_rep, d0c=d0_rep,
            ident_f=ident_f, ident_b=ident_bb, ones_b=ones_bb, ones128_b=ones128_bb,
        ))
    return in_maps


def kernel(**inputs):
    in_maps = _host_prep(inputs)
    nc = build_program()
    res = run_bass_kernel_spmd(nc, in_maps, core_ids=list(range(N_CORES)))
    shards = [np.asarray(r["out"], dtype=np.float32) for r in res.results]
    out = np.concatenate([s.T for s in shards], axis=1)
    return out.reshape(1, T_TOK, D)


# version 06
# speedup vs baseline: 1.2175x; 1.1745x over previous
import numpy as np
import ml_dtypes
from contextlib import ExitStack

import concourse.bass as bass
import concourse.mybir as mybir
from concourse.bass_utils import run_bass_kernel_spmd

F32 = mybir.dt.float32
BF16 = mybir.dt.bfloat16
AF = mybir.ActivationFunctionType
ALU = mybir.AluOpType
AX = mybir.AxisListType
bf16 = ml_dtypes.bfloat16

N_CORES = 8
T_TOK = 8192
D = 4096
FS = 512
GT = 512
NDC = 32
SCALE = 1.0 / 16.0
BIV_IDX = [4, 5, 6]


def _derived():
    global TOK_PC, NG, NCP, NMC, LNCH
    TOK_PC = T_TOK // N_CORES
    NG = T_TOK // GT
    NCP = T_TOK // 2048
    NMC = T_TOK // 128
    LNCH = TOK_PC // 128


def _set_tokens(t):
    global T_TOK
    T_TOK = t
    _derived()


_derived()

ENGINES = ("sync", "tensor", "vector", "scalar", "gpsimd")


def _build_tables():
    g = [1.0, 1.0, 1.0]
    dim, n = 3, 8
    masks = sorted(range(n), key=lambda m: (bin(m).count("1"), m))
    inv = {m: i for i, m in enumerate(masks)}
    C = np.zeros((n, n, n), dtype=np.float32)
    for i, a in enumerate(masks):
        for j, bb in enumerate(masks):
            sw, at = 0, a >> 1
            while at:
                sw += bin(at & bb).count("1")
                at >>= 1
            s = -1.0 if sw & 1 else 1.0
            for t in range(dim):
                if a >> t & 1 and bb >> t & 1:
                    s *= g[t]
            C[i, j, inv[a ^ bb]] = s
    grades = np.array([bin(m).count("1") for m in masks])
    rev = np.where(grades * (grades - 1) // 2 % 2 == 1, -1.0, 1.0).astype(np.float32)
    return C, rev


class Sem:
    def __init__(self, handle):
        self.h = handle
        self.n = 0

    def inc(self, k=1):
        self.n += k
        return self.n


class B:
    CHAIN_ENGINES = ("vector", "scalar")

    def __init__(self):
        self.prog = {e: [] for e in ENGINES}
        self.last_wait = {e: {} for e in ENGINES}
        self.tail = {e: None for e in ENGINES}
        self.chain_sem = {}

    def op(self, eng, fn, sem=None, k=1, chain=True):
        do_chain = chain and eng in self.CHAIN_ENGINES
        if do_chain and self.tail[eng] is not None:
            self.wait(eng, *self.tail[eng])
        if sem is None and do_chain:
            sem, k = self.chain_sem[eng], 1
        if sem is None:
            self.prog[eng].append(lambda e, fn=fn: fn(e))
            return None
        v = sem.inc(k)
        self.prog[eng].append(lambda e, fn=fn, s=sem, k=k: fn(e).then_inc(s.h, k))
        if do_chain:
            self.tail[eng] = (sem, v)
        return v

    def wait(self, eng, sem, val):
        if val is None or val <= 0:
            return
        seen = self.last_wait[eng].get(id(sem), -1)
        if val <= seen:
            return
        self.last_wait[eng][id(sem)] = val
        self.prog[eng].append(lambda e, s=sem, v=val: e.wait_ge(s.h, v))


def build_program():
    nc = bass.Bass("TRN2", target_bir_lowering=False)

    def din(name, shape, dt=F32):
        return nc.dram_tensor(name, list(shape), dt, kind="ExternalInput").ap()

    x_tok = din("x_tok", [TOK_PC, D])
    xT = din("xT", [FS, T_TOK])
    wq = din("wq", [FS, D])
    wkT = din("wkT", [D, FS])
    wvT = din("wvT", [D, FS])
    bq4 = din("bq4", [128, 4])
    bk4 = din("bk4", [128, 4])
    bv4 = din("bv4", [128, 4])
    bw = din("bw", [2, 3])
    wfT = din("wfT", [D, FS])
    bf4 = din("bf4", [128, 4])
    afT = din("afT", [D, 256])
    bfp = din("bfp", [256, FS])
    cs4 = din("cs4", [128, 4])
    bivC = din("bivC", [2, 192])
    revs = din("revs", [2, 64])
    d0c = din("d0c", [2, 64])
    ident_f = din("ident_f", [128, 128])
    ident_b = din("ident_b", [128, 128], BF16)
    ones_b = din("ones_b", [1, 128], BF16)
    ones128_b = din("ones128_b", [128, 128], BF16)

    out_d = nc.dram_tensor("out", [FS, T_TOK], F32, kind="ExternalOutput").ap()

    m_scr = nc.dram_tensor("m_scr", [2, 64], F32, kind="Internal")
    SPLIT_CC = False
    HTOK = TOK_PC // 2 if SPLIT_CC else TOK_PC
    xnT_own_a = nc.dram_tensor("xnT_own_a", [D, HTOK], BF16, kind="Internal")
    xn_all_a = nc.dram_tensor("xn_all_a", [N_CORES * D, HTOK], BF16, kind="Internal", addr_space="Shared")
    if SPLIT_CC:
        xnT_own_b = nc.dram_tensor("xnT_own_b", [D, HTOK], BF16, kind="Internal")
        xn_all_b = nc.dram_tensor("xn_all_b", [N_CORES * D, HTOK], BF16, kind="Internal", addr_space="Shared")
    else:
        xnT_own_b, xn_all_b = xnT_own_a, xn_all_a
    wqeT_d = nc.dram_tensor("wqeT_d", [D, FS], BF16, kind="Internal")
    o_scr = nc.dram_tensor("o_scr", [FS, T_TOK], BF16, kind="Internal")
    v_scr = nc.dram_tensor("v_scr", [256, T_TOK], BF16, kind="Internal")

    b = B()

    with ExitStack() as ctx:
        sbt = lambda name, shp, dt: ctx.enter_context(nc.sbuf_tensor(name, shp, dt))
        ident_f_s = sbt("ident_f_s", [128, 128], F32)
        ident_b_s = sbt("ident_b_s", [128, 128], BF16)
        ones_b_s = sbt("ones_b_s", [1, 128], BF16)
        ones128_s = sbt("ones128_s", [128, 128], BF16)
        bk4_s = sbt("bk4_s", [128, 4], F32)
        bf4_s = sbt("bf4_s", [128, 4], F32)
        bqe_s = sbt("bqe_s", [128, 4], F32)
        bv4_s = sbt("bv4_s", [128, 4], F32)
        cs_s = sbt("cs_s", [128, 4], F32)
        bfp_b = sbt("bfp_b", [128, 1024], BF16)
        pt = sbt("pt", [128, 3 * 1024], BF16)
        ptT = sbt("ptT", [128, 4 * 4096], BF16)
        bc_s = sbt("bc_s", [128, 512], F32)
        dstripe = sbt("dstripe", [128, 512], BF16)
        asml = sbt("asml", [128, 40], F32)
        xn_t = sbt("xn_t", [128, 8 * GT], BF16)
        o_stage = sbt("o_stage", [128, 1024], BF16)
        vt_stage = sbt("vt_stage", [128, 1024], BF16)
        eps_s = sbt("eps_s", [128, 1], F32)

        MRUN, MU, BETA, ZC, ZP, ZT, RT, DLT = 0, 4, 8, 12, 20, 24, 28, 32

        def sem(name):
            return Sem(ctx.enter_context(nc.semaphore(name)))

        b.chain_sem["vector"] = sem("s_chv")
        b.chain_sem["scalar"] = sem("s_chs")
        s_ld = sem("s_ld")
        s_g = sem("s_g")
        s_g3 = sem("s_g3")
        s_gset = sem("s_gset")
        s_v = sem("s_v")
        s_s = sem("s_s")
        s_t = sem("s_t")
        s_cc = sem("s_cc")
        s_xn = sem("s_xn")
        s_xt = [sem("s_xt0"), sem("s_xt1")]
        s_tf = sem("s_tf")
        s_evs = sem("s_evs")
        s_evv = sem("s_evv")
        s_wq = sem("s_wq")
        s_qk = sem("s_qk")
        s_mx = sem("s_mx")
        s_exp = sem("s_exp")
        s_ptf = [sem("s_pt0"), sem("s_pt1"), sem("s_pt2")]
        s_tpc = sem("s_tpc")
        s_av = sem("s_av")
        s_rsc = sem("s_rsc")
        s_bc = sem("s_bc")
        s_bcm = sem("s_bcm")
        s_oev = sem("s_oev")
        s_osp = sem("s_osp")
        s_ou = [sem("s_ou0"), sem("s_ou1"), sem("s_ou2"), sem("s_ou3")]
        s_xnst = sem("s_xnst")
        s_lntp = sem("s_lntp")
        s_dst = sem("s_dst")
        s_vst = sem("s_vst")
        s_vtp = sem("s_vtp")

        PS_A = ctx.enter_context(nc.psum_tensor("PS_A", [128, 1024], F32))
        PS_B = ctx.enter_context(nc.psum_tensor("PS_B", [128, 1024], F32))
        PS_C = ctx.enter_context(nc.psum_tensor("PS_C", [128, 1024], F32))
        PS_D = ctx.enter_context(nc.psum_tensor("PS_D", [128, 1024], F32))
        LBUF = (PS_A, PS_B, PS_C)

        block = ctx.enter_context(nc.Block())

        with ExitStack() as s1:
            sb1 = lambda name, shp, dt: s1.enter_context(nc.sbuf_tensor(name, shp, dt))
            wq_sb = sb1("wq_sb", [128, 4 * D], BF16)
            Tf = sb1("Tf", [128, 2048], F32)
            T_b = sb1("T_b", [128, 2048], BF16)
            x_chunk = sb1("x_chunk", [128, 2 * D], F32)
            xsq = sb1("xsq", [128, D], F32)
            xn_bf = sb1("xn_bf", [128, D], BF16)
            xnT_stage = sb1("xnT_stage", [128, D], BF16)
            stageT = sb1("stageT", [128, 512], BF16)
            ln_apply_done = []
            lns = sb1("lns", [128, 16], F32)
            bw_s = sb1("bw_s", [2, 3], F32)
            wk_s = sb1("wk_s", [2, 64], F32)
            M_s = sb1("M_s", [2, 64], F32)
            bivC_s = sb1("bivC_s", [2, 192], F32)
            revs_s = sb1("revs_s", [2, 64], F32)
            d0_s = sb1("d0_s", [2, 64], F32)
            bq4_b = sb1("bq4_b", [128, 4], BF16)

            for src, dst in [(ident_f, ident_f_s), (ident_b, ident_b_s),
                             (ones_b, ones_b_s), (ones128_b, ones128_s),
                             (bk4, bk4_s), (bf4, bf4_s), (cs4, cs_s),
                             (bivC, bivC_s), (revs, revs_s), (d0c, d0_s),
                             (bw, bw_s)]:
                b.op("sync", lambda e, s=src, d=dst: e.dma_start(out=d[:, :], in_=s), s_ld, 16)
            v_consts = s_ld.n
            b.op("gpsimd", lambda e: e.dma_start(out=bv4_s[:, :], in_=bv4), s_gset, 16)
            b.op("gpsimd", lambda e: e.dma_start(out=bq4_b[:, :], in_=bq4), s_gset, 16)
            b.op("gpsimd", lambda e: e.dma_start(
                out=wq_sb.rearrange("p (ic d) -> p ic d", d=D),
                in_=wq.rearrange("(ic p) d -> p ic d", p=128)), s_gset, 16)
            b.op("gpsimd", lambda e: e.dma_start(
                out=bfp_b.rearrange("p (ch o) -> p ch o", o=FS),
                in_=bfp.rearrange("(ch p) o -> p ch o", p=128)), s_gset, 16)
            v_gcasts = s_gset.n

            b.wait("vector", s_ld, v_consts)
            b.op("vector", lambda e: e.memset(eps_s[:, :], 1e-5))
            b.op("vector", lambda e: e.tensor_scalar_mul(wk_s[:, :], bivC_s[:, 0:64], bw_s[:, 0:1]))
            b.op("vector", lambda e: e.scalar_tensor_tensor(
                out=wk_s[:, :], in0=bivC_s[:, 64:128], scalar=bw_s[:, 1:2], in1=wk_s[:, :],
                op0=ALU.mult, op1=ALU.add))
            b.op("vector", lambda e: e.scalar_tensor_tensor(
                out=wk_s[:, :], in0=bivC_s[:, 128:192], scalar=bw_s[:, 2:3], in1=wk_s[:, :],
                op0=ALU.mult, op1=ALU.add))
            b.op("vector", lambda e: e.tensor_mul(M_s[:, :], wk_s[:, :], revs_s[:, :]))
            b.op("vector", lambda e: e.tensor_add(M_s[:, :], M_s[:, :], d0_s[:, :]))
            b.op("vector", lambda e: e.memset(Tf[:, :], 0.0), s_v, 1)
            v_m = s_v.n
            b.wait("sync", s_v, v_m)
            b.op("sync", lambda e: e.dma_start(out=m_scr.ap(), in_=M_s[:, :]), s_ld, 16)
            b.wait("sync", s_ld, s_ld.n)
            for h in range(2):
                for dd in range(32):
                    r0 = h * 256 + dd * 8
                    ic, p0 = r0 // 128, r0 % 128
                    b.op("sync", lambda e, h=h, p0=p0, ic=ic, r0=r0: e.dma_start(
                        out=Tf[p0:p0 + 8, ic * 512 + r0: ic * 512 + r0 + 8],
                        in_=m_scr.ap()[h:h + 1, :].rearrange("one (i j) -> (one i) j", i=8)),
                        s_ld, 16)
            v_scat = s_ld.n
            b.wait("vector", s_ld, v_scat)
            b.op("vector", lambda e: e.tensor_copy(out=T_b[:, :], in_=Tf[:, :]), s_v, 1)
            v_tb = s_v.n

            b.wait("tensor", s_v, v_tb)
            b.wait("tensor", s_gset, v_gcasts)
            fold_ev = []
            for dc in range(NDC):
                half = (dc % 2) * 512
                if dc >= 2:
                    b.wait("tensor", s_v, fold_ev[dc - 2])
                for icc in range(4):
                    b.op("tensor", lambda e, dc=dc, icc=icc, half=half: e.matmul(
                        PS_A[:, half:half + 512],
                        wq_sb[:, icc * D + dc * 128: icc * D + dc * 128 + 128],
                        T_b[:, icc * 512: icc * 512 + 512],
                        start=(icc == 0), stop=(icc == 3)),
                        s_t if icc == 3 else None, 1)
                vt = s_t.n
                b.wait("vector", s_t, vt)
                b.wait("vector", s_wq, s_wq.n)
                b.op("vector", lambda e, half=half: e.tensor_copy(
                    out=stageT[:, :], in_=PS_A[:, half:half + 512]), s_v, 1)
                fold_ev.append(s_v.n)
                b.wait("sync", s_v, s_v.n)
                b.op("sync", lambda e, dc=dc: e.dma_start(
                    out=wqeT_d.ap()[dc * 128:(dc + 1) * 128, :], in_=stageT[:, :]),
                    s_wq, 16)
            v_wqe_stores = s_wq.n

            bq_ps = [(PS_A, 0), (PS_A, 512), (PS_B, 0), (PS_B, 512)]
            b.wait("tensor", s_v, fold_ev[-1])
            b.wait("tensor", s_v, fold_ev[-2])
            for jc in range(4):
                ps, off = bq_ps[jc]
                for icc in range(4):
                    b.op("tensor", lambda e, jc=jc, icc=icc, ps=ps, off=off: e.matmul(
                        ps[:, off:off + 1],
                        T_b[:, icc * 512 + jc * 128: icc * 512 + jc * 128 + 128],
                        bq4_b[:, icc:icc + 1],
                        start=(icc == 0), stop=(icc == 3)),
                        s_t if (jc == 3 and icc == 3) else None, 1)
            v_bq = s_t.n
            b.wait("vector", s_t, v_bq)
            for jc in range(4):
                ps, off = bq_ps[jc]
                b.op("vector", lambda e, jc=jc, ps=ps, off=off: e.tensor_copy(
                    out=bqe_s[:, jc:jc + 1], in_=ps[:, off:off + 1]),
                    s_v if jc == 3 else None, 1)

            b.wait("vector", s_gset, v_gcasts)
            b.op("vector", lambda e: e.tensor_mul(cs_s[:, 0:2], cs_s[:, 0:2], cs_s[:, 2:4]))
            for chh in range(2):
                b.op("vector", lambda e, chh=chh: e.tensor_scalar_mul(
                    bfp_b[:, chh * 512: chh * 512 + 512],
                    bfp_b[:, chh * 512: chh * 512 + 512],
                    cs_s[:, chh:chh + 1]))

            b.wait("vector", s_wq, v_wqe_stores)
            last_apply = 0
            last_tp = 0
            for i in range(LNCH):
                xb = (i % 2) * D
                if i >= 2:
                    b.wait("sync", s_s, ln_apply_done[i - 2])
                b.op("sync", lambda e, i=i, xb=xb: e.dma_start(
                    out=x_chunk[:, xb:xb + D], in_=x_tok[i * 128:(i + 1) * 128, :]), s_xn, 16)
                v_x = s_xn.n
                b.wait("vector", s_xn, v_x)
                b.op("vector", lambda e, xb=xb: e.reduce_sum(lns[:, 0:1], x_chunk[:, xb:xb + D], axis=AX.X))
                b.wait("scalar", s_xn, v_x)
                b.op("scalar", lambda e, xb=xb: e.activation(
                    out=xsq[:, :], in_=x_chunk[:, xb:xb + D], func=AF.Square), s_s, 1)
                b.wait("vector", s_s, s_s.n)
                b.op("vector", lambda e: e.reduce_sum(lns[:, 1:2], xsq[:, :], axis=AX.X))
                b.op("vector", lambda e: e.tensor_scalar_mul(lns[:, 6:7], lns[:, 0:1], 1.0 / D))
                b.op("vector", lambda e: e.tensor_scalar_mul(lns[:, 7:8], lns[:, 1:2], 1.0 / D))
                b.op("vector", lambda e: e.tensor_mul(lns[:, 8:9], lns[:, 6:7], lns[:, 6:7]))
                b.op("vector", lambda e: e.tensor_sub(lns[:, 9:10], lns[:, 7:8], lns[:, 8:9]), s_v, 1)
                b.wait("scalar", s_v, s_v.n)
                b.op("scalar", lambda e: e.activation(
                    out=lns[:, 10:11], in_=lns[:, 9:10], func=AF.Sqrt, bias=eps_s[:, 0:1]), s_s, 1)
                b.wait("vector", s_s, s_s.n)
                b.op("vector", lambda e: e.reciprocal(out=lns[:, 11:12], in_=lns[:, 10:11]))
                b.op("vector", lambda e: e.scalar_tensor_tensor(
                    out=lns[:, 12:13], in0=lns[:, 6:7], scalar=-1.0, in1=lns[:, 11:12],
                    op0=ALU.mult, op1=ALU.mult), s_v, 1)
                b.wait("scalar", s_v, s_v.n)
                if i >= 1:
                    b.wait("scalar", s_lntp, 16 * i)
                b.op("scalar", lambda e, xb=xb: e.activation(
                    out=xn_bf[:, :], in_=x_chunk[:, xb:xb + D], func=AF.Identity,
                    bias=lns[:, 12:13], scale=lns[:, 11:12]), s_s, 1)
                v_apply = s_s.n
                ln_apply_done.append(v_apply)
                b.wait("sync", s_s, v_apply)
                if i >= 1:
                    b.wait("sync", s_xnst, 16 * i)
                b.op("sync", lambda e: e.dma_start(
                    out=xnT_stage.rearrange("p (dc t) -> p dc t", t=128),
                    in_=xn_bf[:, :], transpose=True), s_lntp, 16)
                b.wait("sync", s_lntp, s_lntp.n)
                half_t = xnT_own_a if (not SPLIT_CC or i < LNCH // 2) else xnT_own_b
                i_loc = i if (not SPLIT_CC or i < LNCH // 2) else i - LNCH // 2
                b.op("sync", lambda e, half_t=half_t, i_loc=i_loc: e.dma_start(
                    out=half_t.ap().rearrange("(dc p) t -> p dc t", p=128)[
                        :, :, i_loc * 128:(i_loc + 1) * 128],
                    in_=xnT_stage.rearrange("p (dc t) -> p dc t", t=128)),
                    s_xnst, 16)
                if i == (LNCH // 2 if SPLIT_CC else LNCH) - 1:
                    v_xnst_a = s_xnst.n
            v_xnst = s_xnst.n

            b.wait("gpsimd", s_xnst, v_xnst_a)
            b.op("gpsimd", lambda e: e.collective_compute(
                "AllGather", ALU.bypass,
                replica_groups=[list(range(N_CORES))],
                ins=[xnT_own_a.ap().opt()], outs=[xn_all_a.ap().opt()]), s_cc, 1)
            v_cc_a = s_cc.n
            if SPLIT_CC:
                b.wait("gpsimd", s_xnst, v_xnst)
                b.op("gpsimd", lambda e: e.collective_compute(
                    "AllGather", ALU.bypass,
                    replica_groups=[list(range(N_CORES))],
                    ins=[xnT_own_b.ap().opt()], outs=[xn_all_b.ap().opt()]), s_cc, 1)
            v_cc_b = s_cc.n


        if SPLIT_CC:
            G_ORDER = [g for g in range(NG) if g % 2 == 0] + [g for g in range(NG) if g % 2 == 1]
        else:
            G_ORDER = list(range(NG))

        with ExitStack() as s2:
            sb2 = lambda name, shp, dt: s2.enter_context(nc.sbuf_tensor(name, shp, dt))
            wqe_sb = sb2("wqe_sb", [128, NDC * 256], BF16)
            wk_sb = sb2("wk_sb", [128, NDC * 256], BF16)
            wv_sb = sb2("wv_sb", [128, NDC * 256], BF16)
            q_sb = sb2("q_sb", [128, 2 * T_TOK], BF16)
            k_sb = sb2("k_sb", [128, 2 * T_TOK], BF16)
            v_sb = sb2("v_sb", [128, NMC * 256], BF16)

            p1_last_mm = [0, 0]

            for hL in range(2):
                b.wait("gpsimd", s_wq, v_wqe_stores)
                if hL == 1:
                    b.wait("gpsimd", *p1_last_mm[0])
                b.op("gpsimd", lambda e, hL=hL: e.dma_start(
                    out=wqe_sb.rearrange("p (dc j) -> p dc j", j=256),
                    in_=wqeT_d.ap().rearrange("(dc p) j -> p dc j", p=128)[
                        :, :, hL * 256:(hL + 1) * 256]), s_g, 16)
                b.op("gpsimd", lambda e, hL=hL: e.dma_start(
                    out=wk_sb.rearrange("p (dc j) -> p dc j", j=256),
                    in_=wkT.rearrange("(dc p) j -> p dc j", p=128)[
                        :, :, hL * 256:(hL + 1) * 256]), s_g, 16)
                b.op("gpsimd", lambda e, hL=hL: e.dma_start(
                    out=wv_sb.rearrange("p (dc j) -> p dc j", j=256),
                    in_=wvT.rearrange("(dc p) j -> p dc j", p=128)[
                        :, :, hL * 256:(hL + 1) * 256]), s_g, 16)
                v_wpre = s_g.n
                b.wait("tensor", s_g, v_wpre)
                if hL == 1:
                    b.wait("tensor", s_oev, s_oev.n)
                    b.wait("tensor", s_exp, s_exp.n)
                    b.wait("tensor", s_bc, s_bc.n)

                base_tf = s_tf.n
                base_xt = [s_xt[0].n, s_xt[1].n]

                def p1_emit_loads(gi, g):
                    if SPLIT_CC:
                        b.wait("sync", s_cc, v_cc_a if g % 2 == 0 else v_cc_b)
                        r, t, c0 = g // 2, (xn_all_a if g % 2 == 0 else xn_all_b), 0
                    else:
                        b.wait("sync", s_cc, v_cc_b)
                        r, t, c0 = (g * GT) // TOK_PC, xn_all_a, (g * GT) % TOK_PC
                    for lg in range(8):
                        flat_lg = gi * 8 + lg
                        if flat_lg >= 2:
                            b.wait("sync", s_tf, base_tf + 4 * (flat_lg - 1))
                        sl = (flat_lg % 2) * 4 * GT
                        b.op("sync", lambda e, r=r, t=t, lg=lg, sl=sl, c0=c0: e.dma_start(
                            out=xn_t.rearrange("p (dc t) -> p dc t", t=GT)[
                                :, (sl // GT):(sl // GT) + 4, :],
                            in_=t.ap().rearrange("(dc p) t -> p dc t", p=128)[
                                :, r * 32 + lg * 4: r * 32 + lg * 4 + 4, c0:c0 + GT]),
                            s_xt[flat_lg % 2], 16)

                p1_evs_base = s_evs.n
                p1_tf_base = s_tf.n
                vst_base = s_vst.n
                for gi, g in enumerate(G_ORDER):
                    p1_emit_loads(gi, g)
                    for dc in range(NDC):
                        flat = gi * NDC + dc
                        flat_lg = gi * 8 + dc // 4
                        b.wait("tensor", s_xt[flat_lg % 2],
                               base_xt[flat_lg % 2] + 16 * (flat_lg // 2 + 1))
                        bsl = (flat_lg % 2) * 4 * GT + (dc % 4) * GT
                        for wsb, ps, ei in ((wqe_sb, PS_A, 1), (wk_sb, PS_B, 3), (wv_sb, PS_C, 5)):
                            for jc in range(2):
                                if dc == 0 and gi > 0:
                                    b.wait("tensor", s_evs, p1_evs_base + 6 * (gi - 1) + ei + jc)
                                b.op("tensor", lambda e, dc=dc, jc=jc, bsl=bsl, wsb=wsb, ps=ps: e.matmul(
                                    ps[:, jc * 512:(jc + 1) * 512],
                                    wsb[:, dc * 256 + jc * 128: dc * 256 + jc * 128 + 128],
                                    xn_t[:, bsl:bsl + GT],
                                    start=(dc == 0), stop=(dc == NDC - 1)),
                                    s_tf if (ps is PS_C and jc == 1) else None, 1)
                    v_gmm = p1_tf_base + NDC * (gi + 1)
                    b.wait("scalar", s_tf, v_gmm)
                    for jc in range(2):
                        jcg = hL * 2 + jc
                        b.op("scalar", lambda e, g=g, jc=jc, jcg=jcg: e.activation(
                            out=q_sb[:, jc * T_TOK + g * GT: jc * T_TOK + g * GT + GT],
                            in_=PS_A[:, jc * 512:(jc + 1) * 512], func=AF.Identity,
                            bias=bqe_s[:, jcg:jcg + 1]), s_evs, 1)
                    for jc in range(2):
                        jcg = hL * 2 + jc
                        b.op("scalar", lambda e, g=g, jc=jc, jcg=jcg: e.activation(
                            out=k_sb[:, jc * T_TOK + g * GT: jc * T_TOK + g * GT + GT],
                            in_=PS_B[:, jc * 512:(jc + 1) * 512], func=AF.Identity,
                            bias=bk4_s[:, jcg:jcg + 1]), s_evs, 1)
                    b.wait("scalar", s_vst, vst_base + 16 * gi)
                    for jc in range(2):
                        jcg = hL * 2 + jc
                        b.op("scalar", lambda e, jc=jc, jcg=jcg: e.activation(
                            out=vt_stage[:, jc * 512:(jc + 1) * 512],
                            in_=PS_C[:, jc * 512:(jc + 1) * 512], func=AF.Identity,
                            bias=bv4_s[:, jcg:jcg + 1]), s_evs, 1)
                    v_vev = s_evs.n
                    b.wait("sync", s_evs, v_vev)
                    b.op("sync", lambda e, g=g: e.dma_start(
                        out=v_scr.ap().rearrange("(ic p) t -> p ic t", p=128)[
                            :, :, g * GT:(g + 1) * GT],
                        in_=vt_stage.rearrange("p (ic t) -> p ic t", t=512)),
                        s_vst, 16)
                b.wait("sync", s_vst, s_vst.n)
                if hL == 1:
                    b.wait("sync", s_av, s_av.n)
                for ic in range(2):
                    b.op("sync", lambda e, ic=ic: e.dma_start(
                        out=v_sb.rearrange("p (mc f) -> p mc f", f=256)[
                            :, :, ic * 128:(ic + 1) * 128],
                        in_=v_scr.ap()[ic * 128:(ic + 1) * 128, :].rearrange(
                            "ii (mc t) -> ii mc t", t=128),
                        transpose=True), s_vtp, 16)
                v_vtp = s_vtp.n

                p1_last_mm[hL] = (s_tf, s_tf.n)
                lguard = [(s_evs, s_evs.n)] * 3
                ptguard = [(s_ptf[i], s_ptf[i].n) for i in range(3)]
                ptTguard = [(s_av, s_av.n)] * 4
                og_state = {"g": None if hL == 0 else (s_oev, s_oev.n)}
                last_dstripe = s_dst.n
                last_zupd = s_v.n
                last_final_dst = s_dst.n
                pending = []
                pair_tp = {}
                rt_ready = None
                final_G = None
                step = 0

                def drain(n):
                    for _ in range(min(n, len(pending))):
                        pending.pop(0)()

                def flush():
                    while pending:
                        pending.pop(0)()

                def queue_av(cpav, v_vtp):
                    for ci2 in range(2):
                        cav = cpav * 2 + ci2
                        tslot = cav % 4
                        first = (cav == 0)
                        mark = pair_tp[cav]
                        ef = {"v": False}
                        for msub in range(8):
                            for dch in range(2):
                                st = (first and msub == 0)
                                sp = (msub == 7)

                                def part(cav=cav, msub=msub, dch=dch, tslot=tslot,
                                         st=st, sp=sp, first=first, mark=mark,
                                         vv=v_vtp, ef=ef):
                                    if not ef["v"]:
                                        b.wait("tensor", s_vtp, vv)
                                        b.wait("tensor", s_tpc, mark)
                                        if og_state["g"] is not None:
                                            b.wait("tensor", *og_state["g"])
                                        ef["v"] = True
                                    b.op("tensor",
                                         lambda e: e.matmul(
                                             PS_D[:, dch * 512:(dch + 1) * 512],
                                             v_sb[:, (cav * 8 + msub) * 256 + dch * 128:
                                                  (cav * 8 + msub) * 256 + dch * 128 + 128],
                                             ptT[:, tslot * 4096 + msub * 512:
                                                 tslot * 4096 + msub * 512 + 512],
                                             start=st, stop=sp,
                                             skip_group_check=(not first)),
                                         s_av if (msub == 7 and dch == 1) else None, 1)
                                pending.append(part)
                        ptTguard[tslot] = (s_av, _av_base[0] + ci2 + 1)

                def queue_rescale(scale_col_base, sem_src, v_src, is_final, Gf=None):
                    b.wait("gpsimd", s_bcm, s_bcm.n)
                    b.wait("gpsimd", sem_src, v_src)
                    for t in range(4):
                        b.op("gpsimd", lambda e, t=t, base=scale_col_base: e.tensor_scalar_mul(
                            dstripe[:, t * 128:(t + 1) * 128], ident_b_s[:, :],
                            asml[:, base + t:base + t + 1]), s_dst, 1, chain=False)
                    v_dst = s_dst.n

                    def resc():
                        bufb = (glob_step[0] + 1) % 3
                        Lb = LBUF[bufb]
                        b.wait("tensor", s_dst, v_dst)
                        if lguard[bufb] is not None:
                            b.wait("tensor", *lguard[bufb])
                        b.op("tensor", lambda e, Lb=Lb: e.matmul(
                            Lb[:, 0:512], ones128_s[:, :], dstripe[:, :],
                            start=True, stop=True), s_bcm, 1)
                        b.wait("scalar", s_bcm, s_bcm.n)
                        b.op("scalar", lambda e, Lb=Lb: e.copy(
                            out=bc_s[:, :], in_=Lb[:, 0:512]), s_bc, 1)
                        lguard[bufb] = (s_bc, s_bc.n)
                        b.wait("vector", s_bc, s_bc.n)
                        b.wait("vector", s_av, s_av.n)
                        for dch in range(2):
                            b.op("vector", lambda e, dch=dch: e.tensor_mul(
                                PS_D[:, dch * 512:(dch + 1) * 512],
                                PS_D[:, dch * 512:(dch + 1) * 512],
                                bc_s[:, :]), s_rsc if dch == 1 else None, 1)
                        og_state["g"] = (s_rsc, s_rsc.n)
                        if is_final:
                            v_final_rsc = s_rsc.n
                            b.wait("scalar", s_rsc, v_final_rsc)
                            b.wait("scalar", s_osp, s_osp.n)
                            for dch in range(2):
                                b.op("scalar", lambda e, dch=dch: e.copy(
                                    out=o_stage[:, dch * 512:(dch + 1) * 512],
                                    in_=PS_D[:, dch * 512:(dch + 1) * 512]), s_oev, 1)
                            v_oev = s_oev.n
                            og_state["g"] = (s_oev, v_oev)
                            b.wait("sync", s_oev, v_oev)
                            b.op("sync", lambda e, Gf=Gf, hL=hL: e.dma_start(
                                out=o_scr.ap().rearrange("(r p) t -> p r t", p=128)[
                                    :, hL * 2: hL * 2 + 2, Gf * GT:(Gf + 1) * GT],
                                in_=o_stage.rearrange("p (r t) -> p r t", t=512)), s_osp, 16)
                    pending.append(resc)
                    return v_dst

                glob_step = [0]
                _av_base = [0]
                for G in range(NG):
                    for cp in range(NCP):
                        tp_records = {cp * 2: [], cp * 2 + 1: []}
                        for t in range(4):
                            for cisub in range(2):
                                c = cp * 2 + cisub
                                buf = step % 3
                                L = LBUF[buf]
                                glob_step[0] = step
                                if lguard[buf] is not None:
                                    b.wait("tensor", *lguard[buf])
                                for icc in range(2):
                                    for nb2 in range(2):
                                        b.op("tensor",
                                             lambda e, G=G, t=t, c=c, icc=icc, nb2=nb2, L=L: e.matmul(
                                                 L[:, nb2 * 512:(nb2 + 1) * 512],
                                                 q_sb[:, icc * T_TOK + G * GT + t * 128:
                                                      icc * T_TOK + G * GT + t * 128 + 128],
                                                 k_sb[:, icc * T_TOK + c * 1024 + nb2 * 512:
                                                      icc * T_TOK + c * 1024 + nb2 * 512 + 512],
                                                 start=(icc == 0), stop=(icc == 1)),
                                             s_qk if (icc == 1 and nb2 == 1) else None, 1)
                                v_qkstep = s_qk.n
                                in_pair_step = (t * 2 + cisub)
                                drain(1 if in_pair_step == 0 else 6)
                                b.wait("vector", s_qk, v_qkstep)
                                b.op("vector", lambda e, L=L, cisub=cisub: e.reduce_max(
                                    asml[:, MU + cisub:MU + cisub + 1], L[:, :],
                                    axis=AX.X, negate=True), s_mx, 1)
                                if cisub == 1:
                                    b.op("vector", lambda e: e.tensor_tensor(
                                        out=asml[:, MU:MU + 1], in0=asml[:, MU:MU + 1],
                                        in1=asml[:, MU + 1:MU + 2], op=ALU.min))
                                    if cp == 0:
                                        b.op("vector", lambda e, t=t: e.tensor_copy(
                                            out=asml[:, MRUN + t:MRUN + t + 1],
                                            in_=asml[:, MU:MU + 1]), s_v, 1)
                                    else:
                                        b.wait("vector", s_dst, last_dstripe)
                                        b.op("vector", lambda e, t=t: e.tensor_tensor(
                                            out=asml[:, DLT:DLT + 1], in0=asml[:, MU:MU + 1],
                                            in1=asml[:, MRUN + t:MRUN + t + 1], op=ALU.min))
                                        b.op("vector", lambda e, t=t: e.tensor_sub(
                                            asml[:, BETA + t:BETA + t + 1],
                                            asml[:, DLT:DLT + 1],
                                            asml[:, MRUN + t:MRUN + t + 1]))
                                        b.op("vector", lambda e, t=t: e.tensor_copy(
                                            out=asml[:, MRUN + t:MRUN + t + 1],
                                            in_=asml[:, DLT:DLT + 1]), s_v, 1)
                                    v_mready = s_v.n
                                    for cc2 in range(2):
                                        st2 = step - 1 + cc2
                                        slot2 = st2 % 3
                                        L2 = LBUF[slot2]
                                        b.wait("scalar", s_v, v_mready)
                                        b.wait("scalar", s_v, last_zupd)
                                        if ptguard[slot2] is not None:
                                            b.wait("scalar", *ptguard[slot2])
                                        b.op("scalar", lambda e, L2=L2, slot2=slot2, t=t, cc2=cc2: e.activation(
                                            out=pt[:, slot2 * 1024:(slot2 + 1) * 1024],
                                            in_=L2[:, :], func=AF.Exp,
                                            bias=asml[:, MRUN + t:MRUN + t + 1],
                                            accum_out=asml[:, ZC + 2 * t + cc2:ZC + 2 * t + cc2 + 1]),
                                            s_exp, 1)
                                        lguard[slot2] = (s_exp, s_exp.n)
                                        cthis = cp * 2 + cc2
                                        tslot = cthis % 4
                                        b.wait("sync", s_exp, s_exp.n)
                                        if t == 0 and ptTguard[tslot] is not None:
                                            b.wait("sync", *ptTguard[tslot])
                                        vtp = b.op("sync", lambda e, slot2=slot2, tslot=tslot, t=t: e.dma_start(
                                            out=ptT.rearrange("m (sl s q) -> m (sl s) q", q=GT, s=8)[
                                                :, tslot * 8:(tslot + 1) * 8, t * 128:(t + 1) * 128],
                                            in_=pt[:, slot2 * 1024:(slot2 + 1) * 1024],
                                            transpose=True), s_ptf[slot2], 16)
                                        ptguard[slot2] = (s_ptf[slot2], vtp)
                                        tp_records[cthis].append((s_ptf[slot2], vtp))
                                step += 1
                        for cch in (cp * 2, cp * 2 + 1):
                            for semx, valx in tp_records[cch]:
                                b.wait("sync", semx, valx)
                            b.op("sync", lambda e: e.nop(), s_tpc, 1)
                            pair_tp[cch] = s_tpc.n
                        b.wait("vector", s_exp, s_exp.n)
                        zc_even = asml[:, ZC:ZC + 8].rearrange("p (t two) -> p t two", two=2)[:, :, 0:1]
                        zc_odd = asml[:, ZC:ZC + 8].rearrange("p (t two) -> p t two", two=2)[:, :, 1:2]
                        zp_v = asml[:, ZP:ZP + 4].rearrange("p (t one) -> p t one", one=1)
                        b.op("vector", lambda e: e.tensor_tensor(
                            out=zp_v, in0=zc_even, in1=zc_odd, op=ALU.add), s_v, 1)
                        if cp == 0:
                            b.op("vector", lambda e: e.tensor_copy(
                                out=asml[:, ZT:ZT + 4], in_=asml[:, ZP:ZP + 4]), s_v, 1)
                        else:
                            b.wait("scalar", s_v, s_v.n - 1)
                            b.op("scalar", lambda e: e.activation(
                                out=asml[:, BETA:BETA + 4], in_=asml[:, BETA:BETA + 4],
                                func=AF.Exp), s_s, 1)
                            b.wait("vector", s_s, s_s.n)
                            b.op("vector", lambda e: e.tensor_mul(
                                asml[:, ZT:ZT + 4], asml[:, ZT:ZT + 4],
                                asml[:, BETA:BETA + 4]))
                            b.op("vector", lambda e: e.tensor_add(
                                asml[:, ZT:ZT + 4], asml[:, ZT:ZT + 4],
                                asml[:, ZP:ZP + 4]), s_v, 1)
                        last_zupd = s_v.n
                        flush()
                        _av_base[0] = s_av.n
                        if cp > 0:
                            last_dstripe = queue_rescale(BETA, s_s, s_s.n, False)
                        elif final_G is not None:
                            last_final_dst = queue_rescale(RT, s_v, rt_ready, True, Gf=final_G)
                            final_G = None
                        queue_av(cp, v_vtp)
                        if cp == NCP - 1:
                            b.wait("vector", s_exp, s_exp.n)
                            b.wait("vector", s_dst, last_final_dst)
                            for t in range(4):
                                b.op("vector", lambda e, t=t: e.reciprocal(
                                    out=asml[:, RT + t:RT + t + 1],
                                    in_=asml[:, ZT + t:ZT + t + 1]), s_v if t == 3 else None, 1)
                            rt_ready = s_v.n
                            final_G = G
                flush()
                _av_base[0] = s_av.n
                queue_rescale(RT, s_v, rt_ready, True, Gf=final_G)
                final_G = None
                flush()
            v_osp_all = s_osp.n


        with ExitStack() as s3:
            sb3 = lambda name, shp, dt: s3.enter_context(nc.sbuf_tensor(name, shp, dt))
            wf_sb = sb3("wf_sb", [128, NDC * 512], BF16)
            af_sb = sb3("af_sb", [128, NDC * 256], BF16)
            o_tile = sb3("o_tile", [128, 4 * 512], F32)
            xT_tile = sb3("xT_tile", [128, 4 * 512], F32)
            xa_sb = sb3("xa_sb", [128, 1024], BF16)
            stage_out = sb3("stage_out", [128, 4 * 512], F32)

            b.wait("gpsimd", *p1_last_mm[1])
            b.op("gpsimd", lambda e: e.dma_start(
                out=wf_sb.rearrange("p (dc o) -> p dc o", o=512),
                in_=wfT.rearrange("(dc p) o -> p dc o", p=128)), s_g, 16)
            b.op("gpsimd", lambda e: e.dma_start(
                out=af_sb.rearrange("p (dc o) -> p dc o", o=256),
                in_=afT.rearrange("(dc p) o -> p dc o", p=128)), s_g, 16)
            v_p3w = s_g.n
            b.wait("tensor", s_g, v_p3w)
            b.wait("tensor", s_exp, s_exp.n)
            b.wait("tensor", s_bc, s_bc.n)
            b.wait("tensor", s_oev, s_oev.n)

            base_tf = s_tf.n
            base_xt = [s_xt[0].n, s_xt[1].n]

            def p3_emit_loads(gi, g):
                if SPLIT_CC:
                    r, t, c0 = g // 2, (xn_all_a if g % 2 == 0 else xn_all_b), 0
                else:
                    r, t, c0 = (g * GT) // TOK_PC, xn_all_a, (g * GT) % TOK_PC
                for lg in range(8):
                    flat_lg = gi * 8 + lg
                    if flat_lg >= 2:
                        b.wait("sync", s_tf, base_tf + 4 * (flat_lg - 1))
                    sl = (flat_lg % 2) * 4 * GT
                    b.op("sync", lambda e, r=r, t=t, lg=lg, sl=sl, c0=c0: e.dma_start(
                        out=xn_t.rearrange("p (dc t) -> p dc t", t=GT)[
                            :, (sl // GT):(sl // GT) + 4, :],
                        in_=t.ap().rearrange("(dc p) t -> p dc t", p=128)[
                            :, r * 32 + lg * 4: r * 32 + lg * 4 + 4, c0:c0 + GT]),
                        s_xt[flat_lg % 2], 16)

            b.wait("gpsimd", s_osp, v_osp_all)
            base_g3 = s_g3.n
            for gi2, g in enumerate(G_ORDER):
                if gi2 >= 1:
                    for _oc in range(4):
                        b.wait("gpsimd", s_ou[_oc], 16 * gi2)
                for oc in range(4):
                    b.op("gpsimd", lambda e, g=g, oc=oc: e.dma_start(
                        out=o_tile[:, oc * 512:(oc + 1) * 512],
                        in_=o_scr.ap()[oc * 128:(oc + 1) * 128, g * GT:(g + 1) * GT]),
                        s_g3, 16)
                    b.op("gpsimd", lambda e, g=g, oc=oc: e.dma_start(
                        out=xT_tile[:, oc * 512:(oc + 1) * 512],
                        in_=xT[oc * 128:(oc + 1) * 128, g * GT:(g + 1) * GT]),
                        s_g3, 16)

            PSO = [(PS_A, 0), (PS_A, 512), (PS_B, 0), (PS_B, 512)]
            ev3_base = s_evv.n
            p3_tf_base = s_tf.n
            p3_evac_done = 0
            pending_stores = []
            for gi, g in enumerate(G_ORDER):
                p3_emit_loads(gi, g)
                for emit_store in pending_stores:
                    emit_store()
                pending_stores = []
                for dc in range(NDC):
                    flat = gi * NDC + dc
                    flat_lg = gi * 8 + dc // 4
                    b.wait("tensor", s_xt[flat_lg % 2],
                           base_xt[flat_lg % 2] + 16 * (flat_lg // 2 + 1))
                    bsl = (flat_lg % 2) * 4 * GT + (dc % 4) * GT
                    if dc == 0 and gi > 0:
                        b.wait("tensor", s_s, p3_evac_done)
                        b.wait("tensor", s_evv, ev3_base + 2 * gi)
                    for oc in range(4):
                        ps, off = PSO[oc]
                        b.op("tensor", lambda e, dc=dc, oc=oc, bsl=bsl, ps=ps, off=off: e.matmul(
                            ps[:, off:off + 512],
                            wf_sb[:, dc * 512 + oc * 128: dc * 512 + oc * 128 + 128],
                            xn_t[:, bsl:bsl + GT],
                            start=(dc == 0), stop=False))
                    for chh in range(2):
                        b.op("tensor", lambda e, dc=dc, chh=chh, bsl=bsl: e.matmul(
                            PS_C[:, chh * 512:(chh + 1) * 512],
                            af_sb[:, dc * 256 + chh * 128: dc * 256 + chh * 128 + 128],
                            xn_t[:, bsl:bsl + GT],
                            start=(dc == 0), stop=(dc == NDC - 1)),
                            s_tf if chh == 1 else None, 1)
                v_xa = p3_tf_base + NDC * (g + 1)
                b.wait("vector", s_tf, v_xa)
                for chh in range(2):
                    b.op("vector", lambda e, chh=chh: e.tensor_copy(
                        out=xa_sb[:, chh * 512:(chh + 1) * 512],
                        in_=PS_C[:, chh * 512:(chh + 1) * 512]), s_evv, 1)
                v_xaev = s_evv.n
                b.wait("tensor", s_evv, v_xaev)
                for chh in range(2):
                    for oc in range(4):
                        ps, off = PSO[oc]
                        b.op("tensor", lambda e, chh=chh, oc=oc, ps=ps, off=off: e.matmul(
                            ps[:, off:off + 512],
                            bfp_b[:, chh * 512 + oc * 128: chh * 512 + oc * 128 + 128],
                            xa_sb[:, chh * 512:(chh + 1) * 512],
                            start=False, stop=(chh == 1)),
                            s_t if (chh == 1 and oc == 3) else None, 1)
                v_updmm = s_t.n
                b.wait("scalar", s_t, v_updmm)
                for oc in range(4):
                    ps, off = PSO[oc]
                    ssl = oc * 512
                    b.wait("scalar", s_ou[oc], 16 * gi)
                    b.op("scalar", lambda e, ps=ps, off=off, oc=oc, ssl=ssl: e.activation(
                        out=stage_out[:, ssl:ssl + 512], in_=ps[:, off:off + 512],
                        func=AF.Identity, bias=bf4_s[:, oc:oc + 1]), s_s, 1)
                    v_ev1 = s_s.n
                    b.wait("vector", s_s, v_ev1)
                    b.wait("vector", s_g3, base_g3 + 16 * 8 * (g + 1))
                    b.op("vector", lambda e, ssl=ssl, oc=oc: e.tensor_add(
                        stage_out[:, ssl:ssl + 512], stage_out[:, ssl:ssl + 512],
                        o_tile[:, oc * 512:(oc + 1) * 512]))
                    b.op("vector", lambda e, ssl=ssl, oc=oc: e.tensor_add(
                        stage_out[:, ssl:ssl + 512], stage_out[:, ssl:ssl + 512],
                        xT_tile[:, oc * 512:(oc + 1) * 512]), s_v, 1)
                    v_add = s_v.n

                    def emit_store(g=g, oc=oc, ssl=ssl, v_add=v_add):
                        b.wait("sync", s_v, v_add)
                        b.op("sync", lambda e, g=g, oc=oc, ssl=ssl: e.dma_start(
                            out=out_d[oc * 128:(oc + 1) * 128, g * GT:(g + 1) * GT],
                            in_=stage_out[:, ssl:ssl + 512]), s_ou[oc], 16)
                    pending_stores.append(emit_store)
                p3_evac_done = s_s.n
            for emit_store in pending_stores:
                emit_store()

            for _oc in range(4):
                b.wait("sync", s_ou[_oc], s_ou[_oc].n)

        @block.sync
        def _(eng):
            for f in b.prog["sync"]:
                f(eng)

        @block.tensor
        def _(eng):
            for f in b.prog["tensor"]:
                f(eng)

        @block.vector
        def _(eng):
            for f in b.prog["vector"]:
                f(eng)

        @block.scalar
        def _(eng):
            for f in b.prog["scalar"]:
                f(eng)

        @block.gpsimd
        def _(eng):
            for f in b.prog["gpsimd"]:
                f(eng)

    return nc


def _host_prep(inputs):
    C_np, rev_np = _build_tables()
    biv = C_np[:, :, BIV_IDX].astype(np.float32)
    bivC_flat = biv.transpose(2, 0, 1).reshape(3, 64)
    bivC_rep = np.repeat(np.concatenate([bivC_flat[c].reshape(1, 64) for c in range(3)],
                                        axis=1), 2, axis=0).astype(np.float32)
    revs_rep = np.repeat(np.repeat(rev_np.reshape(8, 1), 8, axis=1).reshape(1, 64),
                         2, axis=0).astype(np.float32)
    d0 = (SCALE * rev_np.reshape(8, 1) * np.eye(8, dtype=np.float32)).reshape(1, 64)
    d0_rep = np.repeat(d0, 2, axis=0).astype(np.float32)

    x = np.asarray(inputs["L0_attn_x"], dtype=np.float32)[0]
    xTfull = np.ascontiguousarray(x.T)
    Wq = np.asarray(inputs["L0_attn_Wq"], dtype=np.float32)
    Wk = np.asarray(inputs["L0_attn_Wk"], dtype=np.float32)
    Wv = np.asarray(inputs["L0_attn_Wv"], dtype=np.float32)
    bq = np.asarray(inputs["L0_attn_bq"], dtype=np.float32)
    bk = np.asarray(inputs["L0_attn_bk"], dtype=np.float32)
    bv = np.asarray(inputs["L0_attn_bv"], dtype=np.float32)
    bwf = np.asarray(inputs["L0_attn_bias_weight"], dtype=np.float32)
    Wf = np.asarray(inputs["L0_ffn_W"], dtype=np.float32)
    bff = np.asarray(inputs["L0_ffn_bias"], dtype=np.float32)
    Af = np.asarray(inputs["L0_ffn_A"], dtype=np.float32)
    Bff = np.asarray(inputs["L0_ffn_Bf"], dtype=np.float32)
    coeffs = np.asarray(inputs["L0_ffn_coeffs"], dtype=np.float32)
    scales = np.asarray(inputs["L0_ffn_scales"], dtype=np.float32)

    afT_full = np.ascontiguousarray(Af.reshape(256, D).T)
    coeffs_rep = np.ascontiguousarray(np.repeat(coeffs, 16).reshape(2, 128).T)
    scales_rep = np.ascontiguousarray(np.repeat(scales, 16).reshape(2, 128).T)
    cs4 = np.concatenate([coeffs_rep, scales_rep], axis=1).astype(np.float32)

    ident_f = np.eye(128, dtype=np.float32)
    ident_bb = np.eye(128, dtype=np.float32).astype(bf16)
    ones_bb = np.ones((1, 128), dtype=bf16)
    ones128_bb = np.ones((128, 128), dtype=bf16)

    in_maps = []
    for c in range(N_CORES):
        Fsl = slice(c * FS, (c + 1) * FS)
        in_maps.append(dict(
            x_tok=np.ascontiguousarray(x[c * TOK_PC:(c + 1) * TOK_PC, :]),
            xT=np.ascontiguousarray(xTfull[Fsl, :]),
            wq=np.ascontiguousarray(Wq[Fsl, :]),
            wkT=np.ascontiguousarray(Wk[Fsl, :].T),
            wvT=np.ascontiguousarray(Wv[Fsl, :].T),
            bq4=np.ascontiguousarray(bq[Fsl].reshape(4, 128).T),
            bk4=np.ascontiguousarray(bk[Fsl].reshape(4, 128).T),
            bv4=np.ascontiguousarray(bv[Fsl].reshape(4, 128).T),
            bw=np.ascontiguousarray(bwf[2 * c:2 * c + 2, :]),
            wfT=np.ascontiguousarray(Wf[Fsl, :].T),
            bf4=np.ascontiguousarray(bff[Fsl].reshape(4, 128).T),
            afT=afT_full,
            bfp=np.ascontiguousarray(Bff[:, Fsl, :].transpose(0, 2, 1).reshape(256, FS)),
            cs4=cs4,
            bivC=bivC_rep, revs=revs_rep, d0c=d0_rep,
            ident_f=ident_f, ident_b=ident_bb, ones_b=ones_bb, ones128_b=ones128_bb,
        ))
    return in_maps


def kernel(**inputs):
    in_maps = _host_prep(inputs)
    nc = build_program()
    res = run_bass_kernel_spmd(nc, in_maps, core_ids=list(range(N_CORES)))
    shards = [np.asarray(r["out"], dtype=np.float32) for r in res.results]
    out = np.concatenate([s.T for s in shards], axis=1)
    return out.reshape(1, T_TOK, D)


# version 07
# speedup vs baseline: 1.2886x; 1.0584x over previous
import numpy as np
import ml_dtypes
from contextlib import ExitStack

import concourse.bass as bass
import concourse.mybir as mybir
from concourse.bass_utils import run_bass_kernel_spmd

F32 = mybir.dt.float32
BF16 = mybir.dt.bfloat16
AF = mybir.ActivationFunctionType
ALU = mybir.AluOpType
AX = mybir.AxisListType
bf16 = ml_dtypes.bfloat16

N_CORES = 8
T_TOK = 8192
D = 4096
FS = 512
GT = 512
NDC = 32
SCALE = 1.0 / 16.0
BIV_IDX = [4, 5, 6]


def _derived():
    global TOK_PC, NG, NCP, NMC, LNCH
    TOK_PC = T_TOK // N_CORES
    NG = T_TOK // GT
    NCP = T_TOK // 2048
    NMC = T_TOK // 128
    LNCH = TOK_PC // 128


def _set_tokens(t):
    global T_TOK
    T_TOK = t
    _derived()


_derived()

ENGINES = ("sync", "tensor", "vector", "scalar", "gpsimd")


def _build_tables():
    g = [1.0, 1.0, 1.0]
    dim, n = 3, 8
    masks = sorted(range(n), key=lambda m: (bin(m).count("1"), m))
    inv = {m: i for i, m in enumerate(masks)}
    C = np.zeros((n, n, n), dtype=np.float32)
    for i, a in enumerate(masks):
        for j, bb in enumerate(masks):
            sw, at = 0, a >> 1
            while at:
                sw += bin(at & bb).count("1")
                at >>= 1
            s = -1.0 if sw & 1 else 1.0
            for t in range(dim):
                if a >> t & 1 and bb >> t & 1:
                    s *= g[t]
            C[i, j, inv[a ^ bb]] = s
    grades = np.array([bin(m).count("1") for m in masks])
    rev = np.where(grades * (grades - 1) // 2 % 2 == 1, -1.0, 1.0).astype(np.float32)
    return C, rev


class Sem:
    def __init__(self, handle):
        self.h = handle
        self.n = 0

    def inc(self, k=1):
        self.n += k
        return self.n


class B:
    CHAIN_ENGINES = ("vector", "scalar")

    def __init__(self):
        self.prog = {e: [] for e in ENGINES}
        self.last_wait = {e: {} for e in ENGINES}
        self.tail = {e: None for e in ENGINES}
        self.chain_sem = {}

    def op(self, eng, fn, sem=None, k=1, chain=True):
        do_chain = chain and eng in self.CHAIN_ENGINES
        if do_chain and self.tail[eng] is not None:
            self.wait(eng, *self.tail[eng])
        if sem is None and do_chain:
            sem, k = self.chain_sem[eng], 1
        if sem is None:
            self.prog[eng].append(lambda e, fn=fn: fn(e))
            return None
        v = sem.inc(k)
        self.prog[eng].append(lambda e, fn=fn, s=sem, k=k: fn(e).then_inc(s.h, k))
        if do_chain:
            self.tail[eng] = (sem, v)
        return v

    def wait(self, eng, sem, val):
        if val is None or val <= 0:
            return
        seen = self.last_wait[eng].get(id(sem), -1)
        if val <= seen:
            return
        self.last_wait[eng][id(sem)] = val
        self.prog[eng].append(lambda e, s=sem, v=val: e.wait_ge(s.h, v))


def build_program():
    nc = bass.Bass("TRN2", target_bir_lowering=False)

    def din(name, shape, dt=F32):
        return nc.dram_tensor(name, list(shape), dt, kind="ExternalInput").ap()

    x_tok = din("x_tok", [TOK_PC, D])
    xT = din("xT", [FS, T_TOK])
    wq = din("wq", [FS, D])
    wkT = din("wkT", [D, FS])
    wvT = din("wvT", [D, FS])
    bq4 = din("bq4", [128, 4])
    bk4 = din("bk4", [128, 4])
    bv4 = din("bv4", [128, 4])
    bw = din("bw", [2, 3])
    wfT = din("wfT", [D, FS])
    bf4 = din("bf4", [128, 4])
    afT = din("afT", [D, 256])
    bfp = din("bfp", [256, FS])
    cs4 = din("cs4", [128, 4])
    bivC = din("bivC", [2, 192])
    revs = din("revs", [2, 64])
    d0c = din("d0c", [2, 64])
    ident_f = din("ident_f", [128, 128])
    ident_b = din("ident_b", [128, 128], BF16)
    ones_b = din("ones_b", [1, 128], BF16)
    ones128_b = din("ones128_b", [128, 128], BF16)

    out_d = nc.dram_tensor("out", [FS, T_TOK], F32, kind="ExternalOutput").ap()

    m_scr = nc.dram_tensor("m_scr", [2, 64], F32, kind="Internal")
    SPLIT_CC = False
    HTOK = TOK_PC // 2 if SPLIT_CC else TOK_PC
    xnT_own_a = nc.dram_tensor("xnT_own_a", [D, HTOK], BF16, kind="Internal")
    xn_all_a = nc.dram_tensor("xn_all_a", [N_CORES * D, HTOK], BF16, kind="Internal", addr_space="Shared")
    if SPLIT_CC:
        xnT_own_b = nc.dram_tensor("xnT_own_b", [D, HTOK], BF16, kind="Internal")
        xn_all_b = nc.dram_tensor("xn_all_b", [N_CORES * D, HTOK], BF16, kind="Internal", addr_space="Shared")
    else:
        xnT_own_b, xn_all_b = xnT_own_a, xn_all_a
    wqeT_d = nc.dram_tensor("wqeT_d", [D, FS], BF16, kind="Internal")
    o_scr = nc.dram_tensor("o_scr", [FS, T_TOK], BF16, kind="Internal")
    v_scr = nc.dram_tensor("v_scr", [256, T_TOK], BF16, kind="Internal")

    b = B()

    with ExitStack() as ctx:
        sbt = lambda name, shp, dt: ctx.enter_context(nc.sbuf_tensor(name, shp, dt))
        ident_f_s = sbt("ident_f_s", [128, 128], F32)
        ident_b_s = sbt("ident_b_s", [128, 128], BF16)
        ones_b_s = sbt("ones_b_s", [1, 128], BF16)
        ones128_s = sbt("ones128_s", [128, 128], BF16)
        bk4_s = sbt("bk4_s", [128, 4], F32)
        bf4_s = sbt("bf4_s", [128, 4], F32)
        bqe_s = sbt("bqe_s", [128, 4], F32)
        bv4_s = sbt("bv4_s", [128, 4], F32)
        cs_s = sbt("cs_s", [128, 4], F32)
        bfp_b = sbt("bfp_b", [128, 1024], BF16)
        pt = sbt("pt", [128, 3 * 1024], BF16)
        ptT = sbt("ptT", [128, 4 * 4096], BF16)
        bc_s = sbt("bc_s", [128, 512], F32)
        dstripe = sbt("dstripe", [128, 512], BF16)
        asml = sbt("asml", [128, 40], F32)
        xn_t = sbt("xn_t", [128, 8 * GT], BF16)
        o_stage = sbt("o_stage", [128, 1024], BF16)
        vt_stage = sbt("vt_stage", [128, 1024], BF16)
        eps_s = sbt("eps_s", [128, 1], F32)

        MRUN, MU, BETA, ZC, ZP, ZT, RT, DLT = 0, 4, 8, 12, 20, 24, 28, 32

        def sem(name):
            return Sem(ctx.enter_context(nc.semaphore(name)))

        b.chain_sem["vector"] = sem("s_chv")
        b.chain_sem["scalar"] = sem("s_chs")
        s_ld = sem("s_ld")
        s_g = sem("s_g")
        s_g3 = sem("s_g3")
        s_gset = sem("s_gset")
        s_v = sem("s_v")
        s_s = sem("s_s")
        s_t = sem("s_t")
        s_cc = sem("s_cc")
        s_xn = sem("s_xn")
        s_xt = [sem("s_xt0"), sem("s_xt1")]
        s_tf = sem("s_tf")
        s_evs = sem("s_evs")
        s_evv = sem("s_evv")
        s_wq = sem("s_wq")
        s_qk = sem("s_qk")
        s_mx = sem("s_mx")
        s_exp = sem("s_exp")
        s_ptf = [sem("s_pt0"), sem("s_pt1"), sem("s_pt2")]
        s_tpc = sem("s_tpc")
        s_av = sem("s_av")
        s_rsc = sem("s_rsc")
        s_bc = sem("s_bc")
        s_bcm = sem("s_bcm")
        s_oev = sem("s_oev")
        s_osp = sem("s_osp")
        s_ou = [sem("s_ou0"), sem("s_ou1"), sem("s_ou2"), sem("s_ou3")]
        s_xnst = sem("s_xnst")
        s_lntp = sem("s_lntp")
        s_dst = sem("s_dst")
        s_dstv = sem("s_dstv")
        s_vst = sem("s_vst")
        s_vtp = sem("s_vtp")

        PS_A = ctx.enter_context(nc.psum_tensor("PS_A", [128, 1024], F32))
        PS_B = ctx.enter_context(nc.psum_tensor("PS_B", [128, 1024], F32))
        PS_C = ctx.enter_context(nc.psum_tensor("PS_C", [128, 1024], F32))
        PS_D = ctx.enter_context(nc.psum_tensor("PS_D", [128, 1024], F32))
        LBUF = (PS_A, PS_B, PS_C)

        block = ctx.enter_context(nc.Block())

        with ExitStack() as s1:
            sb1 = lambda name, shp, dt: s1.enter_context(nc.sbuf_tensor(name, shp, dt))
            wq_sb = sb1("wq_sb", [128, 4 * D], BF16)
            Tf = sb1("Tf", [128, 2048], F32)
            T_b = sb1("T_b", [128, 2048], BF16)
            x_chunk = sb1("x_chunk", [128, 2 * D], F32)
            xsq = sb1("xsq", [128, D], F32)
            xn_bf = sb1("xn_bf", [128, D], BF16)
            xnT_stage = sb1("xnT_stage", [128, D], BF16)
            stageT = sb1("stageT", [128, 512], BF16)
            ln_apply_done = []
            lns = sb1("lns", [128, 16], F32)
            bw_s = sb1("bw_s", [2, 3], F32)
            wk_s = sb1("wk_s", [2, 64], F32)
            M_s = sb1("M_s", [2, 64], F32)
            bivC_s = sb1("bivC_s", [2, 192], F32)
            revs_s = sb1("revs_s", [2, 64], F32)
            d0_s = sb1("d0_s", [2, 64], F32)
            bq4_b = sb1("bq4_b", [128, 4], BF16)

            for src, dst in [(ident_f, ident_f_s), (ident_b, ident_b_s),
                             (ones_b, ones_b_s), (ones128_b, ones128_s),
                             (bk4, bk4_s), (bf4, bf4_s), (cs4, cs_s),
                             (bivC, bivC_s), (revs, revs_s), (d0c, d0_s),
                             (bw, bw_s)]:
                b.op("sync", lambda e, s=src, d=dst: e.dma_start(out=d[:, :], in_=s), s_ld, 16)
            v_consts = s_ld.n
            b.op("gpsimd", lambda e: e.dma_start(out=bv4_s[:, :], in_=bv4), s_gset, 16)
            b.op("gpsimd", lambda e: e.dma_start(out=bq4_b[:, :], in_=bq4), s_gset, 16)
            b.op("gpsimd", lambda e: e.dma_start(
                out=wq_sb.rearrange("p (ic d) -> p ic d", d=D),
                in_=wq.rearrange("(ic p) d -> p ic d", p=128)), s_gset, 16)
            b.op("gpsimd", lambda e: e.dma_start(
                out=bfp_b.rearrange("p (ch o) -> p ch o", o=FS),
                in_=bfp.rearrange("(ch p) o -> p ch o", p=128)), s_gset, 16)
            v_gcasts = s_gset.n

            b.wait("vector", s_ld, v_consts)
            b.op("vector", lambda e: e.memset(eps_s[:, :], 1e-5))
            b.op("vector", lambda e: e.tensor_scalar_mul(wk_s[:, :], bivC_s[:, 0:64], bw_s[:, 0:1]))
            b.op("vector", lambda e: e.scalar_tensor_tensor(
                out=wk_s[:, :], in0=bivC_s[:, 64:128], scalar=bw_s[:, 1:2], in1=wk_s[:, :],
                op0=ALU.mult, op1=ALU.add))
            b.op("vector", lambda e: e.scalar_tensor_tensor(
                out=wk_s[:, :], in0=bivC_s[:, 128:192], scalar=bw_s[:, 2:3], in1=wk_s[:, :],
                op0=ALU.mult, op1=ALU.add))
            b.op("vector", lambda e: e.tensor_mul(M_s[:, :], wk_s[:, :], revs_s[:, :]))
            b.op("vector", lambda e: e.tensor_add(M_s[:, :], M_s[:, :], d0_s[:, :]))
            b.op("vector", lambda e: e.memset(Tf[:, :], 0.0), s_v, 1)
            v_m = s_v.n
            b.wait("sync", s_v, v_m)
            b.op("sync", lambda e: e.dma_start(out=m_scr.ap(), in_=M_s[:, :]), s_ld, 16)
            b.wait("sync", s_ld, s_ld.n)
            for h in range(2):
                for dd in range(32):
                    r0 = h * 256 + dd * 8
                    ic, p0 = r0 // 128, r0 % 128
                    b.op("sync", lambda e, h=h, p0=p0, ic=ic, r0=r0: e.dma_start(
                        out=Tf[p0:p0 + 8, ic * 512 + r0: ic * 512 + r0 + 8],
                        in_=m_scr.ap()[h:h + 1, :].rearrange("one (i j) -> (one i) j", i=8)),
                        s_ld, 16)
            v_scat = s_ld.n
            b.wait("vector", s_ld, v_scat)
            b.op("vector", lambda e: e.tensor_copy(out=T_b[:, :], in_=Tf[:, :]), s_v, 1)
            v_tb = s_v.n

            b.wait("tensor", s_v, v_tb)
            b.wait("tensor", s_gset, v_gcasts)
            fold_ev = []
            for dc in range(NDC):
                half = (dc % 2) * 512
                if dc >= 2:
                    b.wait("tensor", s_v, fold_ev[dc - 2])
                for icc in range(4):
                    b.op("tensor", lambda e, dc=dc, icc=icc, half=half: e.matmul(
                        PS_A[:, half:half + 512],
                        wq_sb[:, icc * D + dc * 128: icc * D + dc * 128 + 128],
                        T_b[:, icc * 512: icc * 512 + 512],
                        start=(icc == 0), stop=(icc == 3)),
                        s_t if icc == 3 else None, 1)
                vt = s_t.n
                b.wait("vector", s_t, vt)
                b.wait("vector", s_wq, s_wq.n)
                b.op("vector", lambda e, half=half: e.tensor_copy(
                    out=stageT[:, :], in_=PS_A[:, half:half + 512]), s_v, 1)
                fold_ev.append(s_v.n)
                b.wait("sync", s_v, s_v.n)
                b.op("sync", lambda e, dc=dc: e.dma_start(
                    out=wqeT_d.ap()[dc * 128:(dc + 1) * 128, :], in_=stageT[:, :]),
                    s_wq, 16)
            v_wqe_stores = s_wq.n

            bq_ps = [(PS_A, 0), (PS_A, 512), (PS_B, 0), (PS_B, 512)]
            b.wait("tensor", s_v, fold_ev[-1])
            b.wait("tensor", s_v, fold_ev[-2])
            for jc in range(4):
                ps, off = bq_ps[jc]
                for icc in range(4):
                    b.op("tensor", lambda e, jc=jc, icc=icc, ps=ps, off=off: e.matmul(
                        ps[:, off:off + 1],
                        T_b[:, icc * 512 + jc * 128: icc * 512 + jc * 128 + 128],
                        bq4_b[:, icc:icc + 1],
                        start=(icc == 0), stop=(icc == 3)),
                        s_t if (jc == 3 and icc == 3) else None, 1)
            v_bq = s_t.n
            b.wait("vector", s_t, v_bq)
            for jc in range(4):
                ps, off = bq_ps[jc]
                b.op("vector", lambda e, jc=jc, ps=ps, off=off: e.tensor_copy(
                    out=bqe_s[:, jc:jc + 1], in_=ps[:, off:off + 1]),
                    s_v if jc == 3 else None, 1)

            b.wait("vector", s_gset, v_gcasts)
            b.op("vector", lambda e: e.tensor_mul(cs_s[:, 0:2], cs_s[:, 0:2], cs_s[:, 2:4]))
            for chh in range(2):
                b.op("vector", lambda e, chh=chh: e.tensor_scalar_mul(
                    bfp_b[:, chh * 512: chh * 512 + 512],
                    bfp_b[:, chh * 512: chh * 512 + 512],
                    cs_s[:, chh:chh + 1]))

            b.wait("vector", s_wq, v_wqe_stores)
            last_apply = 0
            last_tp = 0
            for i in range(LNCH):
                xb = (i % 2) * D
                if i >= 2:
                    b.wait("sync", s_s, ln_apply_done[i - 2])
                b.op("sync", lambda e, i=i, xb=xb: e.dma_start(
                    out=x_chunk[:, xb:xb + D], in_=x_tok[i * 128:(i + 1) * 128, :]), s_xn, 16)
                v_x = s_xn.n
                b.wait("vector", s_xn, v_x)
                b.op("vector", lambda e, xb=xb: e.reduce_sum(lns[:, 0:1], x_chunk[:, xb:xb + D], axis=AX.X))
                b.wait("scalar", s_xn, v_x)
                b.op("scalar", lambda e, xb=xb: e.activation(
                    out=xsq[:, :], in_=x_chunk[:, xb:xb + D], func=AF.Square), s_s, 1)
                b.wait("vector", s_s, s_s.n)
                b.op("vector", lambda e: e.reduce_sum(lns[:, 1:2], xsq[:, :], axis=AX.X))
                b.op("vector", lambda e: e.tensor_scalar_mul(lns[:, 6:7], lns[:, 0:1], 1.0 / D))
                b.op("vector", lambda e: e.tensor_scalar_mul(lns[:, 7:8], lns[:, 1:2], 1.0 / D))
                b.op("vector", lambda e: e.tensor_mul(lns[:, 8:9], lns[:, 6:7], lns[:, 6:7]))
                b.op("vector", lambda e: e.tensor_sub(lns[:, 9:10], lns[:, 7:8], lns[:, 8:9]), s_v, 1)
                b.wait("scalar", s_v, s_v.n)
                b.op("scalar", lambda e: e.activation(
                    out=lns[:, 10:11], in_=lns[:, 9:10], func=AF.Sqrt, bias=eps_s[:, 0:1]), s_s, 1)
                b.wait("vector", s_s, s_s.n)
                b.op("vector", lambda e: e.reciprocal(out=lns[:, 11:12], in_=lns[:, 10:11]))
                b.op("vector", lambda e: e.scalar_tensor_tensor(
                    out=lns[:, 12:13], in0=lns[:, 6:7], scalar=-1.0, in1=lns[:, 11:12],
                    op0=ALU.mult, op1=ALU.mult), s_v, 1)
                b.wait("scalar", s_v, s_v.n)
                if i >= 1:
                    b.wait("scalar", s_lntp, 16 * i)
                b.op("scalar", lambda e, xb=xb: e.activation(
                    out=xn_bf[:, :], in_=x_chunk[:, xb:xb + D], func=AF.Identity,
                    bias=lns[:, 12:13], scale=lns[:, 11:12]), s_s, 1)
                v_apply = s_s.n
                ln_apply_done.append(v_apply)
                b.wait("sync", s_s, v_apply)
                if i >= 1:
                    b.wait("sync", s_xnst, 16 * i)
                b.op("sync", lambda e: e.dma_start(
                    out=xnT_stage.rearrange("p (dc t) -> p dc t", t=128),
                    in_=xn_bf[:, :], transpose=True), s_lntp, 16)
                b.wait("sync", s_lntp, s_lntp.n)
                half_t = xnT_own_a if (not SPLIT_CC or i < LNCH // 2) else xnT_own_b
                i_loc = i if (not SPLIT_CC or i < LNCH // 2) else i - LNCH // 2
                b.op("sync", lambda e, half_t=half_t, i_loc=i_loc: e.dma_start(
                    out=half_t.ap().rearrange("(dc p) t -> p dc t", p=128)[
                        :, :, i_loc * 128:(i_loc + 1) * 128],
                    in_=xnT_stage.rearrange("p (dc t) -> p dc t", t=128)),
                    s_xnst, 16)
                if i == (LNCH // 2 if SPLIT_CC else LNCH) - 1:
                    v_xnst_a = s_xnst.n
            v_xnst = s_xnst.n

            b.wait("gpsimd", s_xnst, v_xnst_a)
            b.op("gpsimd", lambda e: e.collective_compute(
                "AllGather", ALU.bypass,
                replica_groups=[list(range(N_CORES))],
                ins=[xnT_own_a.ap().opt()], outs=[xn_all_a.ap().opt()]), s_cc, 1)
            v_cc_a = s_cc.n
            if SPLIT_CC:
                b.wait("gpsimd", s_xnst, v_xnst)
                b.op("gpsimd", lambda e: e.collective_compute(
                    "AllGather", ALU.bypass,
                    replica_groups=[list(range(N_CORES))],
                    ins=[xnT_own_b.ap().opt()], outs=[xn_all_b.ap().opt()]), s_cc, 1)
            v_cc_b = s_cc.n


        if SPLIT_CC:
            G_ORDER = [g for g in range(NG) if g % 2 == 0] + [g for g in range(NG) if g % 2 == 1]
        else:
            G_ORDER = list(range(NG))

        with ExitStack() as s2:
            sb2 = lambda name, shp, dt: s2.enter_context(nc.sbuf_tensor(name, shp, dt))
            wqe_sb = sb2("wqe_sb", [128, NDC * 256], BF16)
            wk_sb = sb2("wk_sb", [128, NDC * 256], BF16)
            wv_sb = sb2("wv_sb", [128, NDC * 256], BF16)
            q_sb = sb2("q_sb", [128, 2 * T_TOK], BF16)
            k_sb = sb2("k_sb", [128, 2 * T_TOK], BF16)
            v_sb = sb2("v_sb", [128, NMC * 256], BF16)

            p1_last_mm = [0, 0]

            for hL in range(2):
                b.wait("gpsimd", s_wq, v_wqe_stores)
                if hL == 1:
                    b.wait("gpsimd", *p1_last_mm[0])
                b.op("gpsimd", lambda e, hL=hL: e.dma_start(
                    out=wqe_sb.rearrange("p (dc j) -> p dc j", j=256),
                    in_=wqeT_d.ap().rearrange("(dc p) j -> p dc j", p=128)[
                        :, :, hL * 256:(hL + 1) * 256]), s_g, 16)
                b.op("gpsimd", lambda e, hL=hL: e.dma_start(
                    out=wk_sb.rearrange("p (dc j) -> p dc j", j=256),
                    in_=wkT.rearrange("(dc p) j -> p dc j", p=128)[
                        :, :, hL * 256:(hL + 1) * 256]), s_g, 16)
                b.op("gpsimd", lambda e, hL=hL: e.dma_start(
                    out=wv_sb.rearrange("p (dc j) -> p dc j", j=256),
                    in_=wvT.rearrange("(dc p) j -> p dc j", p=128)[
                        :, :, hL * 256:(hL + 1) * 256]), s_g, 16)
                v_wpre = s_g.n
                b.wait("tensor", s_g, v_wpre)
                if hL == 1:
                    b.wait("tensor", s_oev, s_oev.n)
                    b.wait("tensor", s_exp, s_exp.n)
                    b.wait("tensor", s_bc, s_bc.n)

                base_tf = s_tf.n
                base_xt = [s_xt[0].n, s_xt[1].n]

                def p1_emit_loads(gi, g):
                    if SPLIT_CC:
                        b.wait("sync", s_cc, v_cc_a if g % 2 == 0 else v_cc_b)
                        r, t, c0 = g // 2, (xn_all_a if g % 2 == 0 else xn_all_b), 0
                    else:
                        b.wait("sync", s_cc, v_cc_b)
                        r, t, c0 = (g * GT) // TOK_PC, xn_all_a, (g * GT) % TOK_PC
                    for lg in range(8):
                        flat_lg = gi * 8 + lg
                        if flat_lg >= 2:
                            b.wait("sync", s_tf, base_tf + 4 * (flat_lg - 1))
                        sl = (flat_lg % 2) * 4 * GT
                        b.op("sync", lambda e, r=r, t=t, lg=lg, sl=sl, c0=c0: e.dma_start(
                            out=xn_t.rearrange("p (dc t) -> p dc t", t=GT)[
                                :, (sl // GT):(sl // GT) + 4, :],
                            in_=t.ap().rearrange("(dc p) t -> p dc t", p=128)[
                                :, r * 32 + lg * 4: r * 32 + lg * 4 + 4, c0:c0 + GT]),
                            s_xt[flat_lg % 2], 16)

                p1_evs_base = s_evs.n
                p1_tf_base = s_tf.n
                vst_base = s_vst.n
                for gi, g in enumerate(G_ORDER):
                    p1_emit_loads(gi, g)
                    for dc in range(NDC):
                        flat = gi * NDC + dc
                        flat_lg = gi * 8 + dc // 4
                        b.wait("tensor", s_xt[flat_lg % 2],
                               base_xt[flat_lg % 2] + 16 * (flat_lg // 2 + 1))
                        bsl = (flat_lg % 2) * 4 * GT + (dc % 4) * GT
                        for wsb, ps, ei in ((wqe_sb, PS_A, 1), (wk_sb, PS_B, 3), (wv_sb, PS_C, 5)):
                            for jc in range(2):
                                if dc == 0 and gi > 0:
                                    b.wait("tensor", s_evs, p1_evs_base + 6 * (gi - 1) + ei + jc)
                                b.op("tensor", lambda e, dc=dc, jc=jc, bsl=bsl, wsb=wsb, ps=ps: e.matmul(
                                    ps[:, jc * 512:(jc + 1) * 512],
                                    wsb[:, dc * 256 + jc * 128: dc * 256 + jc * 128 + 128],
                                    xn_t[:, bsl:bsl + GT],
                                    start=(dc == 0), stop=(dc == NDC - 1)),
                                    s_tf if (ps is PS_C and jc == 1) else None, 1)
                    v_gmm = p1_tf_base + NDC * (gi + 1)
                    b.wait("scalar", s_tf, v_gmm)
                    for jc in range(2):
                        jcg = hL * 2 + jc
                        b.op("scalar", lambda e, g=g, jc=jc, jcg=jcg: e.activation(
                            out=q_sb[:, jc * T_TOK + g * GT: jc * T_TOK + g * GT + GT],
                            in_=PS_A[:, jc * 512:(jc + 1) * 512], func=AF.Identity,
                            bias=bqe_s[:, jcg:jcg + 1]), s_evs, 1)
                    for jc in range(2):
                        jcg = hL * 2 + jc
                        b.op("scalar", lambda e, g=g, jc=jc, jcg=jcg: e.activation(
                            out=k_sb[:, jc * T_TOK + g * GT: jc * T_TOK + g * GT + GT],
                            in_=PS_B[:, jc * 512:(jc + 1) * 512], func=AF.Identity,
                            bias=bk4_s[:, jcg:jcg + 1]), s_evs, 1)
                    b.wait("scalar", s_vst, vst_base + 16 * gi)
                    for jc in range(2):
                        jcg = hL * 2 + jc
                        b.op("scalar", lambda e, jc=jc, jcg=jcg: e.activation(
                            out=vt_stage[:, jc * 512:(jc + 1) * 512],
                            in_=PS_C[:, jc * 512:(jc + 1) * 512], func=AF.Identity,
                            bias=bv4_s[:, jcg:jcg + 1]), s_evs, 1)
                    v_vev = s_evs.n
                    b.wait("sync", s_evs, v_vev)
                    b.op("sync", lambda e, g=g: e.dma_start(
                        out=v_scr.ap().rearrange("(ic p) t -> p ic t", p=128)[
                            :, :, g * GT:(g + 1) * GT],
                        in_=vt_stage.rearrange("p (ic t) -> p ic t", t=512)),
                        s_vst, 16)
                b.wait("sync", s_vst, s_vst.n)
                if hL == 1:
                    b.wait("sync", s_av, s_av.n)
                for ic in range(2):
                    b.op("sync", lambda e, ic=ic: e.dma_start(
                        out=v_sb.rearrange("p (mc f) -> p mc f", f=256)[
                            :, :, ic * 128:(ic + 1) * 128],
                        in_=v_scr.ap()[ic * 128:(ic + 1) * 128, :].rearrange(
                            "ii (mc t) -> ii mc t", t=128),
                        transpose=True), s_vtp, 16)
                v_vtp = s_vtp.n

                p1_last_mm[hL] = (s_tf, s_tf.n)
                lguard = [(s_evs, s_evs.n)] * 3
                ptguard = [(s_ptf[i], s_ptf[i].n) for i in range(3)]
                ptTguard = [(s_av, s_av.n)] * 4
                og_state = {"g": None if hL == 0 else (s_oev, s_oev.n)}
                last_dstripe = (s_dst.n, s_dstv.n)
                last_zupd = s_v.n
                last_final_dst = (s_dst.n, s_dstv.n)
                pending = []
                pair_tp = {}
                rt_ready = None
                final_G = None
                step = 0

                def drain(n):
                    for _ in range(min(n, len(pending))):
                        pending.pop(0)()

                def flush():
                    while pending:
                        pending.pop(0)()

                def queue_av(cpav, v_vtp):
                    for ci2 in range(2):
                        cav = cpav * 2 + ci2
                        tslot = cav % 4
                        first = (cav == 0)
                        mark = pair_tp[cav]
                        ef = {"v": False}
                        for msub in range(8):
                            for dch in range(2):
                                st = (first and msub == 0)
                                sp = (msub == 7)

                                def part(cav=cav, msub=msub, dch=dch, tslot=tslot,
                                         st=st, sp=sp, first=first, mark=mark,
                                         vv=v_vtp, ef=ef):
                                    if not ef["v"]:
                                        b.wait("tensor", s_vtp, vv)
                                        b.wait("tensor", s_tpc, mark)
                                        if og_state["g"] is not None:
                                            b.wait("tensor", *og_state["g"])
                                        ef["v"] = True
                                    b.op("tensor",
                                         lambda e: e.matmul(
                                             PS_D[:, dch * 512:(dch + 1) * 512],
                                             v_sb[:, (cav * 8 + msub) * 256 + dch * 128:
                                                  (cav * 8 + msub) * 256 + dch * 128 + 128],
                                             ptT[:, tslot * 4096 + msub * 512:
                                                 tslot * 4096 + msub * 512 + 512],
                                             start=st, stop=sp,
                                             skip_group_check=(not first)),
                                         s_av if (msub == 7 and dch == 1) else None, 1)
                                pending.append(part)
                        ptTguard[tslot] = (s_av, _av_base[0] + ci2 + 1)

                def queue_rescale(scale_col_base, sem_src, v_src, is_final, Gf=None):
                    b.wait("scalar", s_bcm, s_bcm.n)
                    b.wait("scalar", sem_src, v_src)
                    b.wait("vector", s_bcm, s_bcm.n)
                    b.wait("vector", sem_src, v_src)
                    for t in range(2):
                        b.op("scalar", lambda e, t=t, base=scale_col_base: e.activation(
                            out=dstripe[:, t * 128:(t + 1) * 128], in_=ident_b_s[:, :],
                            func=AF.Identity, scale=asml[:, base + t:base + t + 1]),
                            s_dst, 1)
                    for t in range(2, 4):
                        b.op("vector", lambda e, t=t, base=scale_col_base: e.tensor_scalar_mul(
                            dstripe[:, t * 128:(t + 1) * 128], ident_b_s[:, :],
                            asml[:, base + t:base + t + 1]), s_dstv, 1)
                    v_dst = (s_dst.n, s_dstv.n)

                    def resc():
                        bufb = (glob_step[0] + 1) % 3
                        Lb = LBUF[bufb]
                        b.wait("tensor", s_dst, v_dst[0])
                        b.wait("tensor", s_dstv, v_dst[1])
                        if lguard[bufb] is not None:
                            b.wait("tensor", *lguard[bufb])
                        b.op("tensor", lambda e, Lb=Lb: e.matmul(
                            Lb[:, 0:512], ones128_s[:, :], dstripe[:, :],
                            start=True, stop=True), s_bcm, 1)
                        b.wait("scalar", s_bcm, s_bcm.n)
                        b.op("scalar", lambda e, Lb=Lb: e.copy(
                            out=bc_s[:, :], in_=Lb[:, 0:512]), s_bc, 1)
                        lguard[bufb] = (s_bc, s_bc.n)
                        b.wait("vector", s_bc, s_bc.n)
                        b.wait("vector", s_av, s_av.n)
                        for dch in range(2):
                            b.op("vector", lambda e, dch=dch: e.tensor_mul(
                                PS_D[:, dch * 512:(dch + 1) * 512],
                                PS_D[:, dch * 512:(dch + 1) * 512],
                                bc_s[:, :]), s_rsc if dch == 1 else None, 1)
                        og_state["g"] = (s_rsc, s_rsc.n)
                        if is_final:
                            v_final_rsc = s_rsc.n
                            b.wait("scalar", s_rsc, v_final_rsc)
                            b.wait("scalar", s_osp, s_osp.n)
                            for dch in range(2):
                                b.op("scalar", lambda e, dch=dch: e.copy(
                                    out=o_stage[:, dch * 512:(dch + 1) * 512],
                                    in_=PS_D[:, dch * 512:(dch + 1) * 512]), s_oev, 1)
                            v_oev = s_oev.n
                            og_state["g"] = (s_oev, v_oev)
                            b.wait("sync", s_oev, v_oev)
                            b.op("sync", lambda e, Gf=Gf, hL=hL: e.dma_start(
                                out=o_scr.ap().rearrange("(r p) t -> p r t", p=128)[
                                    :, hL * 2: hL * 2 + 2, Gf * GT:(Gf + 1) * GT],
                                in_=o_stage.rearrange("p (r t) -> p r t", t=512)), s_osp, 16)
                    pending.append(resc)
                    return v_dst

                glob_step = [0]
                _av_base = [0]
                for G in range(NG):
                    for cp in range(NCP):
                        tp_records = {cp * 2: [], cp * 2 + 1: []}
                        for t in range(4):
                            for cisub in range(2):
                                c = cp * 2 + cisub
                                buf = step % 3
                                L = LBUF[buf]
                                glob_step[0] = step
                                if lguard[buf] is not None:
                                    b.wait("tensor", *lguard[buf])
                                for icc in range(2):
                                    for nb2 in range(2):
                                        b.op("tensor",
                                             lambda e, G=G, t=t, c=c, icc=icc, nb2=nb2, L=L: e.matmul(
                                                 L[:, nb2 * 512:(nb2 + 1) * 512],
                                                 q_sb[:, icc * T_TOK + G * GT + t * 128:
                                                      icc * T_TOK + G * GT + t * 128 + 128],
                                                 k_sb[:, icc * T_TOK + c * 1024 + nb2 * 512:
                                                      icc * T_TOK + c * 1024 + nb2 * 512 + 512],
                                                 start=(icc == 0), stop=(icc == 1)),
                                             s_qk if (icc == 1 and nb2 == 1) else None, 1)
                                v_qkstep = s_qk.n
                                in_pair_step = (t * 2 + cisub)
                                drain(1 if in_pair_step == 0 else 6)
                                b.wait("vector", s_qk, v_qkstep)
                                b.op("vector", lambda e, L=L, cisub=cisub: e.reduce_max(
                                    asml[:, MU + cisub:MU + cisub + 1], L[:, :],
                                    axis=AX.X, negate=True), s_mx, 1)
                                if cisub == 1:
                                    b.op("vector", lambda e: e.tensor_tensor(
                                        out=asml[:, MU:MU + 1], in0=asml[:, MU:MU + 1],
                                        in1=asml[:, MU + 1:MU + 2], op=ALU.min))
                                    if cp == 0:
                                        b.op("vector", lambda e, t=t: e.tensor_copy(
                                            out=asml[:, MRUN + t:MRUN + t + 1],
                                            in_=asml[:, MU:MU + 1]), s_v, 1)
                                    else:
                                        b.wait("vector", s_dst, last_dstripe[0])
                                        b.wait("vector", s_dstv, last_dstripe[1])
                                        b.op("vector", lambda e, t=t: e.tensor_tensor(
                                            out=asml[:, DLT:DLT + 1], in0=asml[:, MU:MU + 1],
                                            in1=asml[:, MRUN + t:MRUN + t + 1], op=ALU.min))
                                        b.op("vector", lambda e, t=t: e.tensor_sub(
                                            asml[:, BETA + t:BETA + t + 1],
                                            asml[:, DLT:DLT + 1],
                                            asml[:, MRUN + t:MRUN + t + 1]))
                                        b.op("vector", lambda e, t=t: e.tensor_copy(
                                            out=asml[:, MRUN + t:MRUN + t + 1],
                                            in_=asml[:, DLT:DLT + 1]), s_v, 1)
                                    v_mready = s_v.n
                                    for cc2 in range(2):
                                        st2 = step - 1 + cc2
                                        slot2 = st2 % 3
                                        L2 = LBUF[slot2]
                                        b.wait("scalar", s_v, v_mready)
                                        b.wait("scalar", s_v, last_zupd)
                                        if ptguard[slot2] is not None:
                                            b.wait("scalar", *ptguard[slot2])
                                        b.op("scalar", lambda e, L2=L2, slot2=slot2, t=t, cc2=cc2: e.activation(
                                            out=pt[:, slot2 * 1024:(slot2 + 1) * 1024],
                                            in_=L2[:, :], func=AF.Exp,
                                            bias=asml[:, MRUN + t:MRUN + t + 1],
                                            accum_out=asml[:, ZC + 2 * t + cc2:ZC + 2 * t + cc2 + 1]),
                                            s_exp, 1)
                                        lguard[slot2] = (s_exp, s_exp.n)
                                        cthis = cp * 2 + cc2
                                        tslot = cthis % 4
                                        b.wait("sync", s_exp, s_exp.n)
                                        if t == 0 and ptTguard[tslot] is not None:
                                            b.wait("sync", *ptTguard[tslot])
                                        vtp = b.op("sync", lambda e, slot2=slot2, tslot=tslot, t=t: e.dma_start(
                                            out=ptT.rearrange("m (sl s q) -> m (sl s) q", q=GT, s=8)[
                                                :, tslot * 8:(tslot + 1) * 8, t * 128:(t + 1) * 128],
                                            in_=pt[:, slot2 * 1024:(slot2 + 1) * 1024],
                                            transpose=True), s_ptf[slot2], 16)
                                        ptguard[slot2] = (s_ptf[slot2], vtp)
                                        tp_records[cthis].append((s_ptf[slot2], vtp))
                                step += 1
                        for cch in (cp * 2, cp * 2 + 1):
                            for semx, valx in tp_records[cch]:
                                b.wait("sync", semx, valx)
                            b.op("sync", lambda e: e.nop(), s_tpc, 1)
                            pair_tp[cch] = s_tpc.n
                        b.wait("vector", s_exp, s_exp.n)
                        zc_even = asml[:, ZC:ZC + 8].rearrange("p (t two) -> p t two", two=2)[:, :, 0:1]
                        zc_odd = asml[:, ZC:ZC + 8].rearrange("p (t two) -> p t two", two=2)[:, :, 1:2]
                        zp_v = asml[:, ZP:ZP + 4].rearrange("p (t one) -> p t one", one=1)
                        b.op("vector", lambda e: e.tensor_tensor(
                            out=zp_v, in0=zc_even, in1=zc_odd, op=ALU.add), s_v, 1)
                        if cp == 0:
                            b.op("vector", lambda e: e.tensor_copy(
                                out=asml[:, ZT:ZT + 4], in_=asml[:, ZP:ZP + 4]), s_v, 1)
                        else:
                            b.wait("scalar", s_v, s_v.n - 1)
                            b.op("scalar", lambda e: e.activation(
                                out=asml[:, BETA:BETA + 4], in_=asml[:, BETA:BETA + 4],
                                func=AF.Exp), s_s, 1)
                            b.wait("vector", s_s, s_s.n)
                            b.op("vector", lambda e: e.tensor_mul(
                                asml[:, ZT:ZT + 4], asml[:, ZT:ZT + 4],
                                asml[:, BETA:BETA + 4]))
                            b.op("vector", lambda e: e.tensor_add(
                                asml[:, ZT:ZT + 4], asml[:, ZT:ZT + 4],
                                asml[:, ZP:ZP + 4]), s_v, 1)
                        last_zupd = s_v.n
                        flush()
                        _av_base[0] = s_av.n
                        if cp > 0:
                            last_dstripe = queue_rescale(BETA, s_s, s_s.n, False)
                        elif final_G is not None:
                            last_final_dst = queue_rescale(RT, s_v, rt_ready, True, Gf=final_G)
                            final_G = None
                        queue_av(cp, v_vtp)
                        if cp == NCP - 1:
                            b.wait("vector", s_exp, s_exp.n)
                            b.wait("vector", s_dst, last_final_dst[0])
                            b.wait("vector", s_dstv, last_final_dst[1])
                            for t in range(4):
                                b.op("vector", lambda e, t=t: e.reciprocal(
                                    out=asml[:, RT + t:RT + t + 1],
                                    in_=asml[:, ZT + t:ZT + t + 1]), s_v if t == 3 else None, 1)
                            rt_ready = s_v.n
                            final_G = G
                flush()
                _av_base[0] = s_av.n
                queue_rescale(RT, s_v, rt_ready, True, Gf=final_G)
                final_G = None
                flush()
            v_osp_all = s_osp.n


        with ExitStack() as s3:
            sb3 = lambda name, shp, dt: s3.enter_context(nc.sbuf_tensor(name, shp, dt))
            wf_sb = sb3("wf_sb", [128, NDC * 512], BF16)
            af_sb = sb3("af_sb", [128, NDC * 256], BF16)
            o_tile = sb3("o_tile", [128, 4 * 512], F32)
            xT_tile = sb3("xT_tile", [128, 4 * 512], F32)
            xa_sb = sb3("xa_sb", [128, 1024], BF16)
            stage_out = sb3("stage_out", [128, 4 * 512], F32)

            b.wait("gpsimd", *p1_last_mm[1])
            b.op("gpsimd", lambda e: e.dma_start(
                out=wf_sb.rearrange("p (dc o) -> p dc o", o=512),
                in_=wfT.rearrange("(dc p) o -> p dc o", p=128)), s_g, 16)
            b.op("gpsimd", lambda e: e.dma_start(
                out=af_sb.rearrange("p (dc o) -> p dc o", o=256),
                in_=afT.rearrange("(dc p) o -> p dc o", p=128)), s_g, 16)
            v_p3w = s_g.n
            b.wait("tensor", s_g, v_p3w)
            b.wait("tensor", s_exp, s_exp.n)
            b.wait("tensor", s_bc, s_bc.n)
            b.wait("tensor", s_oev, s_oev.n)

            base_tf = s_tf.n
            base_xt = [s_xt[0].n, s_xt[1].n]

            def p3_emit_loads(gi, g):
                if SPLIT_CC:
                    r, t, c0 = g // 2, (xn_all_a if g % 2 == 0 else xn_all_b), 0
                else:
                    r, t, c0 = (g * GT) // TOK_PC, xn_all_a, (g * GT) % TOK_PC
                for lg in range(8):
                    flat_lg = gi * 8 + lg
                    if flat_lg >= 2:
                        b.wait("sync", s_tf, base_tf + 4 * (flat_lg - 1))
                    sl = (flat_lg % 2) * 4 * GT
                    b.op("sync", lambda e, r=r, t=t, lg=lg, sl=sl, c0=c0: e.dma_start(
                        out=xn_t.rearrange("p (dc t) -> p dc t", t=GT)[
                            :, (sl // GT):(sl // GT) + 4, :],
                        in_=t.ap().rearrange("(dc p) t -> p dc t", p=128)[
                            :, r * 32 + lg * 4: r * 32 + lg * 4 + 4, c0:c0 + GT]),
                        s_xt[flat_lg % 2], 16)

            b.wait("gpsimd", s_osp, v_osp_all)
            base_g3 = s_g3.n
            for gi2, g in enumerate(G_ORDER):
                if gi2 >= 1:
                    for _oc in range(4):
                        b.wait("gpsimd", s_ou[_oc], 16 * gi2)
                for oc in range(4):
                    b.op("gpsimd", lambda e, g=g, oc=oc: e.dma_start(
                        out=o_tile[:, oc * 512:(oc + 1) * 512],
                        in_=o_scr.ap()[oc * 128:(oc + 1) * 128, g * GT:(g + 1) * GT]),
                        s_g3, 16)
                    b.op("gpsimd", lambda e, g=g, oc=oc: e.dma_start(
                        out=xT_tile[:, oc * 512:(oc + 1) * 512],
                        in_=xT[oc * 128:(oc + 1) * 128, g * GT:(g + 1) * GT]),
                        s_g3, 16)

            PSO = [(PS_A, 0), (PS_A, 512), (PS_B, 0), (PS_B, 512)]
            ev3_base = s_evv.n
            p3_tf_base = s_tf.n
            p3_evac_done = 0
            pending_stores = []
            for gi, g in enumerate(G_ORDER):
                p3_emit_loads(gi, g)
                for emit_store in pending_stores:
                    emit_store()
                pending_stores = []
                for dc in range(NDC):
                    flat = gi * NDC + dc
                    flat_lg = gi * 8 + dc // 4
                    b.wait("tensor", s_xt[flat_lg % 2],
                           base_xt[flat_lg % 2] + 16 * (flat_lg // 2 + 1))
                    bsl = (flat_lg % 2) * 4 * GT + (dc % 4) * GT
                    if dc == 0 and gi > 0:
                        b.wait("tensor", s_s, p3_evac_done)
                        b.wait("tensor", s_evv, ev3_base + 2 * gi)
                    for oc in range(4):
                        ps, off = PSO[oc]
                        b.op("tensor", lambda e, dc=dc, oc=oc, bsl=bsl, ps=ps, off=off: e.matmul(
                            ps[:, off:off + 512],
                            wf_sb[:, dc * 512 + oc * 128: dc * 512 + oc * 128 + 128],
                            xn_t[:, bsl:bsl + GT],
                            start=(dc == 0), stop=False))
                    for chh in range(2):
                        b.op("tensor", lambda e, dc=dc, chh=chh, bsl=bsl: e.matmul(
                            PS_C[:, chh * 512:(chh + 1) * 512],
                            af_sb[:, dc * 256 + chh * 128: dc * 256 + chh * 128 + 128],
                            xn_t[:, bsl:bsl + GT],
                            start=(dc == 0), stop=(dc == NDC - 1)),
                            s_tf if chh == 1 else None, 1)
                v_xa = p3_tf_base + NDC * (g + 1)
                b.wait("vector", s_tf, v_xa)
                for chh in range(2):
                    b.op("vector", lambda e, chh=chh: e.tensor_copy(
                        out=xa_sb[:, chh * 512:(chh + 1) * 512],
                        in_=PS_C[:, chh * 512:(chh + 1) * 512]), s_evv, 1)
                v_xaev = s_evv.n
                b.wait("tensor", s_evv, v_xaev)
                for chh in range(2):
                    for oc in range(4):
                        ps, off = PSO[oc]
                        b.op("tensor", lambda e, chh=chh, oc=oc, ps=ps, off=off: e.matmul(
                            ps[:, off:off + 512],
                            bfp_b[:, chh * 512 + oc * 128: chh * 512 + oc * 128 + 128],
                            xa_sb[:, chh * 512:(chh + 1) * 512],
                            start=False, stop=(chh == 1)),
                            s_t if (chh == 1 and oc == 3) else None, 1)
                v_updmm = s_t.n
                b.wait("scalar", s_t, v_updmm)
                for oc in range(4):
                    ps, off = PSO[oc]
                    ssl = oc * 512
                    b.wait("scalar", s_ou[oc], 16 * gi)
                    b.op("scalar", lambda e, ps=ps, off=off, oc=oc, ssl=ssl: e.activation(
                        out=stage_out[:, ssl:ssl + 512], in_=ps[:, off:off + 512],
                        func=AF.Identity, bias=bf4_s[:, oc:oc + 1]), s_s, 1)
                    v_ev1 = s_s.n
                    b.wait("vector", s_s, v_ev1)
                    b.wait("vector", s_g3, base_g3 + 16 * 8 * (g + 1))
                    b.op("vector", lambda e, ssl=ssl, oc=oc: e.tensor_add(
                        stage_out[:, ssl:ssl + 512], stage_out[:, ssl:ssl + 512],
                        o_tile[:, oc * 512:(oc + 1) * 512]))
                    b.op("vector", lambda e, ssl=ssl, oc=oc: e.tensor_add(
                        stage_out[:, ssl:ssl + 512], stage_out[:, ssl:ssl + 512],
                        xT_tile[:, oc * 512:(oc + 1) * 512]), s_v, 1)
                    v_add = s_v.n

                    def emit_store(g=g, oc=oc, ssl=ssl, v_add=v_add):
                        b.wait("sync", s_v, v_add)
                        b.op("sync", lambda e, g=g, oc=oc, ssl=ssl: e.dma_start(
                            out=out_d[oc * 128:(oc + 1) * 128, g * GT:(g + 1) * GT],
                            in_=stage_out[:, ssl:ssl + 512]), s_ou[oc], 16)
                    pending_stores.append(emit_store)
                p3_evac_done = s_s.n
            for emit_store in pending_stores:
                emit_store()

            for _oc in range(4):
                b.wait("sync", s_ou[_oc], s_ou[_oc].n)

        @block.sync
        def _(eng):
            for f in b.prog["sync"]:
                f(eng)

        @block.tensor
        def _(eng):
            for f in b.prog["tensor"]:
                f(eng)

        @block.vector
        def _(eng):
            for f in b.prog["vector"]:
                f(eng)

        @block.scalar
        def _(eng):
            for f in b.prog["scalar"]:
                f(eng)

        @block.gpsimd
        def _(eng):
            for f in b.prog["gpsimd"]:
                f(eng)

    return nc


def _host_prep(inputs):
    C_np, rev_np = _build_tables()
    biv = C_np[:, :, BIV_IDX].astype(np.float32)
    bivC_flat = biv.transpose(2, 0, 1).reshape(3, 64)
    bivC_rep = np.repeat(np.concatenate([bivC_flat[c].reshape(1, 64) for c in range(3)],
                                        axis=1), 2, axis=0).astype(np.float32)
    revs_rep = np.repeat(np.repeat(rev_np.reshape(8, 1), 8, axis=1).reshape(1, 64),
                         2, axis=0).astype(np.float32)
    d0 = (SCALE * rev_np.reshape(8, 1) * np.eye(8, dtype=np.float32)).reshape(1, 64)
    d0_rep = np.repeat(d0, 2, axis=0).astype(np.float32)

    x = np.asarray(inputs["L0_attn_x"], dtype=np.float32)[0]
    xTfull = np.ascontiguousarray(x.T)
    Wq = np.asarray(inputs["L0_attn_Wq"], dtype=np.float32)
    Wk = np.asarray(inputs["L0_attn_Wk"], dtype=np.float32)
    Wv = np.asarray(inputs["L0_attn_Wv"], dtype=np.float32)
    bq = np.asarray(inputs["L0_attn_bq"], dtype=np.float32)
    bk = np.asarray(inputs["L0_attn_bk"], dtype=np.float32)
    bv = np.asarray(inputs["L0_attn_bv"], dtype=np.float32)
    bwf = np.asarray(inputs["L0_attn_bias_weight"], dtype=np.float32)
    Wf = np.asarray(inputs["L0_ffn_W"], dtype=np.float32)
    bff = np.asarray(inputs["L0_ffn_bias"], dtype=np.float32)
    Af = np.asarray(inputs["L0_ffn_A"], dtype=np.float32)
    Bff = np.asarray(inputs["L0_ffn_Bf"], dtype=np.float32)
    coeffs = np.asarray(inputs["L0_ffn_coeffs"], dtype=np.float32)
    scales = np.asarray(inputs["L0_ffn_scales"], dtype=np.float32)

    afT_full = np.ascontiguousarray(Af.reshape(256, D).T)
    coeffs_rep = np.ascontiguousarray(np.repeat(coeffs, 16).reshape(2, 128).T)
    scales_rep = np.ascontiguousarray(np.repeat(scales, 16).reshape(2, 128).T)
    cs4 = np.concatenate([coeffs_rep, scales_rep], axis=1).astype(np.float32)

    ident_f = np.eye(128, dtype=np.float32)
    ident_bb = np.eye(128, dtype=np.float32).astype(bf16)
    ones_bb = np.ones((1, 128), dtype=bf16)
    ones128_bb = np.ones((128, 128), dtype=bf16)

    in_maps = []
    for c in range(N_CORES):
        Fsl = slice(c * FS, (c + 1) * FS)
        in_maps.append(dict(
            x_tok=np.ascontiguousarray(x[c * TOK_PC:(c + 1) * TOK_PC, :]),
            xT=np.ascontiguousarray(xTfull[Fsl, :]),
            wq=np.ascontiguousarray(Wq[Fsl, :]),
            wkT=np.ascontiguousarray(Wk[Fsl, :].T),
            wvT=np.ascontiguousarray(Wv[Fsl, :].T),
            bq4=np.ascontiguousarray(bq[Fsl].reshape(4, 128).T),
            bk4=np.ascontiguousarray(bk[Fsl].reshape(4, 128).T),
            bv4=np.ascontiguousarray(bv[Fsl].reshape(4, 128).T),
            bw=np.ascontiguousarray(bwf[2 * c:2 * c + 2, :]),
            wfT=np.ascontiguousarray(Wf[Fsl, :].T),
            bf4=np.ascontiguousarray(bff[Fsl].reshape(4, 128).T),
            afT=afT_full,
            bfp=np.ascontiguousarray(Bff[:, Fsl, :].transpose(0, 2, 1).reshape(256, FS)),
            cs4=cs4,
            bivC=bivC_rep, revs=revs_rep, d0c=d0_rep,
            ident_f=ident_f, ident_b=ident_bb, ones_b=ones_bb, ones128_b=ones128_bb,
        ))
    return in_maps


def kernel(**inputs):
    in_maps = _host_prep(inputs)
    nc = build_program()
    res = run_bass_kernel_spmd(nc, in_maps, core_ids=list(range(N_CORES)))
    shards = [np.asarray(r["out"], dtype=np.float32) for r in res.results]
    out = np.concatenate([s.T for s in shards], axis=1)
    return out.reshape(1, T_TOK, D)


# version 08
# speedup vs baseline: 1.3211x; 1.0252x over previous
import numpy as np
import ml_dtypes
from contextlib import ExitStack

import concourse.bass as bass
import concourse.mybir as mybir
from concourse.bass_utils import run_bass_kernel_spmd

F32 = mybir.dt.float32
BF16 = mybir.dt.bfloat16
AF = mybir.ActivationFunctionType
ALU = mybir.AluOpType
AX = mybir.AxisListType
bf16 = ml_dtypes.bfloat16

N_CORES = 8
T_TOK = 8192
D = 4096
FS = 512
GT = 512
NDC = 32
SCALE = 1.0 / 16.0
BIV_IDX = [4, 5, 6]


def _derived():
    global TOK_PC, NG, NCP, NMC, LNCH
    TOK_PC = T_TOK // N_CORES
    NG = T_TOK // GT
    NCP = T_TOK // 2048
    NMC = T_TOK // 128
    LNCH = TOK_PC // 128


def _set_tokens(t):
    global T_TOK
    T_TOK = t
    _derived()


_derived()

ENGINES = ("sync", "tensor", "vector", "scalar", "gpsimd")


def _build_tables():
    g = [1.0, 1.0, 1.0]
    dim, n = 3, 8
    masks = sorted(range(n), key=lambda m: (bin(m).count("1"), m))
    inv = {m: i for i, m in enumerate(masks)}
    C = np.zeros((n, n, n), dtype=np.float32)
    for i, a in enumerate(masks):
        for j, bb in enumerate(masks):
            sw, at = 0, a >> 1
            while at:
                sw += bin(at & bb).count("1")
                at >>= 1
            s = -1.0 if sw & 1 else 1.0
            for t in range(dim):
                if a >> t & 1 and bb >> t & 1:
                    s *= g[t]
            C[i, j, inv[a ^ bb]] = s
    grades = np.array([bin(m).count("1") for m in masks])
    rev = np.where(grades * (grades - 1) // 2 % 2 == 1, -1.0, 1.0).astype(np.float32)
    return C, rev


class Sem:
    def __init__(self, handle):
        self.h = handle
        self.n = 0

    def inc(self, k=1):
        self.n += k
        return self.n


class B:
    CHAIN_ENGINES = ("vector", "scalar")

    def __init__(self):
        self.prog = {e: [] for e in ENGINES}
        self.last_wait = {e: {} for e in ENGINES}
        self.tail = {e: None for e in ENGINES}
        self.chain_sem = {}

    def op(self, eng, fn, sem=None, k=1, chain=True):
        do_chain = chain and eng in self.CHAIN_ENGINES
        if do_chain and self.tail[eng] is not None:
            self.wait(eng, *self.tail[eng])
        if sem is None and do_chain:
            sem, k = self.chain_sem[eng], 1
        if sem is None:
            self.prog[eng].append(lambda e, fn=fn: fn(e))
            return None
        v = sem.inc(k)
        self.prog[eng].append(lambda e, fn=fn, s=sem, k=k: fn(e).then_inc(s.h, k))
        if do_chain:
            self.tail[eng] = (sem, v)
        return v

    def wait(self, eng, sem, val):
        if val is None or val <= 0:
            return
        seen = self.last_wait[eng].get(id(sem), -1)
        if val <= seen:
            return
        self.last_wait[eng][id(sem)] = val
        self.prog[eng].append(lambda e, s=sem, v=val: e.wait_ge(s.h, v))


def build_program():
    nc = bass.Bass("TRN2", target_bir_lowering=False)

    def din(name, shape, dt=F32):
        return nc.dram_tensor(name, list(shape), dt, kind="ExternalInput").ap()

    x_tok = din("x_tok", [TOK_PC, D])
    xT = din("xT", [FS, T_TOK])
    wq = din("wq", [FS, D])
    wkT = din("wkT", [D, FS])
    wvT = din("wvT", [D, FS])
    bq4 = din("bq4", [128, 4])
    bk4 = din("bk4", [128, 4])
    bv4 = din("bv4", [128, 4])
    bw = din("bw", [2, 3])
    wfT = din("wfT", [D, FS])
    bf4 = din("bf4", [128, 4])
    afT = din("afT", [D, 256])
    bfp = din("bfp", [256, FS])
    cs4 = din("cs4", [128, 4])
    bivC = din("bivC", [2, 192])
    revs = din("revs", [2, 64])
    d0c = din("d0c", [2, 64])
    ident_f = din("ident_f", [128, 128])
    ident_b = din("ident_b", [128, 128], BF16)
    ones_b = din("ones_b", [1, 128], BF16)
    ones128_b = din("ones128_b", [128, 128], BF16)

    out_d = nc.dram_tensor("out", [FS, T_TOK], F32, kind="ExternalOutput").ap()

    m_scr = nc.dram_tensor("m_scr", [2, 64], F32, kind="Internal")
    SPLIT_CC = False
    HTOK = TOK_PC // 2 if SPLIT_CC else TOK_PC
    xnT_own_a = nc.dram_tensor("xnT_own_a", [D, HTOK], BF16, kind="Internal")
    xn_all_a = nc.dram_tensor("xn_all_a", [N_CORES * D, HTOK], BF16, kind="Internal", addr_space="Shared")
    if SPLIT_CC:
        xnT_own_b = nc.dram_tensor("xnT_own_b", [D, HTOK], BF16, kind="Internal")
        xn_all_b = nc.dram_tensor("xn_all_b", [N_CORES * D, HTOK], BF16, kind="Internal", addr_space="Shared")
    else:
        xnT_own_b, xn_all_b = xnT_own_a, xn_all_a
    wqeT_d = nc.dram_tensor("wqeT_d", [D, FS], BF16, kind="Internal")
    o_scr = nc.dram_tensor("o_scr", [FS, T_TOK], BF16, kind="Internal")
    v_scr = nc.dram_tensor("v_scr", [256, T_TOK], BF16, kind="Internal")

    b = B()

    with ExitStack() as ctx:
        sbt = lambda name, shp, dt: ctx.enter_context(nc.sbuf_tensor(name, shp, dt))
        ident_f_s = sbt("ident_f_s", [128, 128], F32)
        ident_b_s = sbt("ident_b_s", [128, 128], BF16)
        ones_b_s = sbt("ones_b_s", [1, 128], BF16)
        ones128_s = sbt("ones128_s", [128, 128], BF16)
        bk4_s = sbt("bk4_s", [128, 4], F32)
        bf4_s = sbt("bf4_s", [128, 4], F32)
        bqe_s = sbt("bqe_s", [128, 4], F32)
        bv4_s = sbt("bv4_s", [128, 4], F32)
        cs_s = sbt("cs_s", [128, 4], F32)
        bfp_b = sbt("bfp_b", [128, 1024], BF16)
        pt = sbt("pt", [128, 3 * 1024], BF16)
        ptT = sbt("ptT", [128, 4 * 4096], BF16)
        bc_s = sbt("bc_s", [128, 512], F32)
        dstripe = sbt("dstripe", [128, 512], BF16)
        asml = sbt("asml", [128, 40], F32)
        xn_t = sbt("xn_t", [128, 8 * GT], BF16)
        o_stage = sbt("o_stage", [128, 1024], BF16)
        vt_stage = sbt("vt_stage", [128, 1024], BF16)
        eps_s = sbt("eps_s", [128, 1], F32)

        MRUN, MU, BETA, ZC, ZP, ZT, RT, DLT = 0, 4, 8, 12, 20, 24, 28, 32

        def sem(name):
            return Sem(ctx.enter_context(nc.semaphore(name)))

        b.chain_sem["vector"] = sem("s_chv")
        b.chain_sem["scalar"] = sem("s_chs")
        s_ld = sem("s_ld")
        s_g = sem("s_g")
        s_g3 = sem("s_g3")
        s_gset = sem("s_gset")
        s_v = sem("s_v")
        s_s = sem("s_s")
        s_t = sem("s_t")
        s_cc = sem("s_cc")
        s_xn = sem("s_xn")
        s_xt = [sem("s_xt0"), sem("s_xt1")]
        s_tf = sem("s_tf")
        s_evs = sem("s_evs")
        s_evv = sem("s_evv")
        s_wq = sem("s_wq")
        s_qk = sem("s_qk")
        s_mx = sem("s_mx")
        s_exp = sem("s_exp")
        s_ptf = [sem("s_pt0"), sem("s_pt1"), sem("s_pt2")]
        s_tpc = sem("s_tpc")
        s_av = sem("s_av")
        s_rsc = sem("s_rsc")
        s_bc = sem("s_bc")
        s_bcm = sem("s_bcm")
        s_oev = sem("s_oev")
        s_osp = sem("s_osp")
        s_ou = [sem("s_ou0"), sem("s_ou1"), sem("s_ou2"), sem("s_ou3")]
        s_xnst = sem("s_xnst")
        s_xn2 = [sem("s_xn2a"), sem("s_xn2b")]
        s_lntp = sem("s_lntp")
        s_dst = sem("s_dst")
        s_dstv = sem("s_dstv")
        s_vst = sem("s_vst")
        s_vtp = sem("s_vtp")

        PS_A = ctx.enter_context(nc.psum_tensor("PS_A", [128, 1024], F32))
        PS_B = ctx.enter_context(nc.psum_tensor("PS_B", [128, 1024], F32))
        PS_C = ctx.enter_context(nc.psum_tensor("PS_C", [128, 1024], F32))
        PS_D = ctx.enter_context(nc.psum_tensor("PS_D", [128, 1024], F32))
        LBUF = (PS_A, PS_B, PS_C)

        block = ctx.enter_context(nc.Block())

        with ExitStack() as s1:
            sb1 = lambda name, shp, dt: s1.enter_context(nc.sbuf_tensor(name, shp, dt))
            wq_sb = sb1("wq_sb", [128, 4 * D], BF16)
            Tf = sb1("Tf", [128, 2048], F32)
            T_b = sb1("T_b", [128, 2048], BF16)
            x_chunk = sb1("x_chunk", [128, 2 * D], F32)
            xsq2 = sb1("xsq2", [128, 2 * D], F32)
            xn_bf2 = sb1("xn_bf2", [128, 2 * D], BF16)
            xnT_stage2 = sb1("xnT_stage2", [128, 2 * D], BF16)
            stageT = sb1("stageT", [128, 512], BF16)
            ln_apply_done = []
            lns = sb1("lns", [128, 32], F32)
            bw_s = sb1("bw_s", [2, 3], F32)
            wk_s = sb1("wk_s", [2, 64], F32)
            M_s = sb1("M_s", [2, 64], F32)
            bivC_s = sb1("bivC_s", [2, 192], F32)
            revs_s = sb1("revs_s", [2, 64], F32)
            d0_s = sb1("d0_s", [2, 64], F32)
            bq4_b = sb1("bq4_b", [128, 4], BF16)

            for src, dst in [(ident_f, ident_f_s), (ident_b, ident_b_s),
                             (ones_b, ones_b_s), (ones128_b, ones128_s),
                             (bk4, bk4_s), (bf4, bf4_s), (cs4, cs_s),
                             (bivC, bivC_s), (revs, revs_s), (d0c, d0_s),
                             (bw, bw_s)]:
                b.op("sync", lambda e, s=src, d=dst: e.dma_start(out=d[:, :], in_=s), s_ld, 16)
            v_consts = s_ld.n
            b.op("gpsimd", lambda e: e.dma_start(out=bv4_s[:, :], in_=bv4), s_gset, 16)
            b.op("gpsimd", lambda e: e.dma_start(out=bq4_b[:, :], in_=bq4), s_gset, 16)
            b.op("gpsimd", lambda e: e.dma_start(
                out=wq_sb.rearrange("p (ic d) -> p ic d", d=D),
                in_=wq.rearrange("(ic p) d -> p ic d", p=128)), s_gset, 16)
            b.op("gpsimd", lambda e: e.dma_start(
                out=bfp_b.rearrange("p (ch o) -> p ch o", o=FS),
                in_=bfp.rearrange("(ch p) o -> p ch o", p=128)), s_gset, 16)
            v_gcasts = s_gset.n

            b.wait("vector", s_ld, v_consts)
            b.op("vector", lambda e: e.memset(eps_s[:, :], 1e-5))
            b.op("vector", lambda e: e.tensor_scalar_mul(wk_s[:, :], bivC_s[:, 0:64], bw_s[:, 0:1]))
            b.op("vector", lambda e: e.scalar_tensor_tensor(
                out=wk_s[:, :], in0=bivC_s[:, 64:128], scalar=bw_s[:, 1:2], in1=wk_s[:, :],
                op0=ALU.mult, op1=ALU.add))
            b.op("vector", lambda e: e.scalar_tensor_tensor(
                out=wk_s[:, :], in0=bivC_s[:, 128:192], scalar=bw_s[:, 2:3], in1=wk_s[:, :],
                op0=ALU.mult, op1=ALU.add))
            b.op("vector", lambda e: e.tensor_mul(M_s[:, :], wk_s[:, :], revs_s[:, :]))
            b.op("vector", lambda e: e.tensor_add(M_s[:, :], M_s[:, :], d0_s[:, :]))
            b.op("vector", lambda e: e.memset(Tf[:, :], 0.0), s_v, 1)
            v_m = s_v.n
            b.wait("sync", s_v, v_m)
            b.op("sync", lambda e: e.dma_start(out=m_scr.ap(), in_=M_s[:, :]), s_ld, 16)
            b.wait("sync", s_ld, s_ld.n)
            for h in range(2):
                for dd in range(32):
                    r0 = h * 256 + dd * 8
                    ic, p0 = r0 // 128, r0 % 128
                    b.op("sync", lambda e, h=h, p0=p0, ic=ic, r0=r0: e.dma_start(
                        out=Tf[p0:p0 + 8, ic * 512 + r0: ic * 512 + r0 + 8],
                        in_=m_scr.ap()[h:h + 1, :].rearrange("one (i j) -> (one i) j", i=8)),
                        s_ld, 16)
            v_scat = s_ld.n
            b.wait("vector", s_ld, v_scat)
            b.op("vector", lambda e: e.tensor_copy(out=T_b[:, :], in_=Tf[:, :]), s_v, 1)
            v_tb = s_v.n

            b.wait("tensor", s_v, v_tb)
            b.wait("tensor", s_gset, v_gcasts)
            fold_ev = []
            for dc in range(NDC):
                half = (dc % 2) * 512
                if dc >= 2:
                    b.wait("tensor", s_v, fold_ev[dc - 2])
                for icc in range(4):
                    b.op("tensor", lambda e, dc=dc, icc=icc, half=half: e.matmul(
                        PS_A[:, half:half + 512],
                        wq_sb[:, icc * D + dc * 128: icc * D + dc * 128 + 128],
                        T_b[:, icc * 512: icc * 512 + 512],
                        start=(icc == 0), stop=(icc == 3)),
                        s_t if icc == 3 else None, 1)
                vt = s_t.n
                b.wait("vector", s_t, vt)
                b.wait("vector", s_wq, s_wq.n)
                b.op("vector", lambda e, half=half: e.tensor_copy(
                    out=stageT[:, :], in_=PS_A[:, half:half + 512]), s_v, 1)
                fold_ev.append(s_v.n)
                b.wait("sync", s_v, s_v.n)
                b.op("sync", lambda e, dc=dc: e.dma_start(
                    out=wqeT_d.ap()[dc * 128:(dc + 1) * 128, :], in_=stageT[:, :]),
                    s_wq, 16)
            v_wqe_stores = s_wq.n

            bq_ps = [(PS_A, 0), (PS_A, 512), (PS_B, 0), (PS_B, 512)]
            b.wait("tensor", s_v, fold_ev[-1])
            b.wait("tensor", s_v, fold_ev[-2])
            for jc in range(4):
                ps, off = bq_ps[jc]
                for icc in range(4):
                    b.op("tensor", lambda e, jc=jc, icc=icc, ps=ps, off=off: e.matmul(
                        ps[:, off:off + 1],
                        T_b[:, icc * 512 + jc * 128: icc * 512 + jc * 128 + 128],
                        bq4_b[:, icc:icc + 1],
                        start=(icc == 0), stop=(icc == 3)),
                        s_t if (jc == 3 and icc == 3) else None, 1)
            v_bq = s_t.n
            b.wait("vector", s_t, v_bq)
            for jc in range(4):
                ps, off = bq_ps[jc]
                b.op("vector", lambda e, jc=jc, ps=ps, off=off: e.tensor_copy(
                    out=bqe_s[:, jc:jc + 1], in_=ps[:, off:off + 1]),
                    s_v if jc == 3 else None, 1)

            b.wait("vector", s_gset, v_gcasts)
            b.op("vector", lambda e: e.tensor_mul(cs_s[:, 0:2], cs_s[:, 0:2], cs_s[:, 2:4]))
            for chh in range(2):
                b.op("vector", lambda e, chh=chh: e.tensor_scalar_mul(
                    bfp_b[:, chh * 512: chh * 512 + 512],
                    bfp_b[:, chh * 512: chh * 512 + 512],
                    cs_s[:, chh:chh + 1]))

            b.wait("vector", s_wq, v_wqe_stores)
            last_apply = 0
            last_tp = 0
            for i in range(LNCH):
                xb = (i % 2) * D
                par = i % 2
                lb = par * 16
                xsq = xsq2[:, par * D:(par + 1) * D]
                xn_bf = xn_bf2[:, par * D:(par + 1) * D]
                xnT_stage = xnT_stage2[:, par * D:(par + 1) * D]
                if i >= 2:
                    b.wait("sync", s_s, ln_apply_done[i - 2])
                b.op("sync", lambda e, i=i, xb=xb: e.dma_start(
                    out=x_chunk[:, xb:xb + D], in_=x_tok[i * 128:(i + 1) * 128, :]), s_xn, 16)
                v_x = s_xn.n
                b.wait("vector", s_xn, v_x)
                b.op("vector", lambda e, xb=xb, lb=lb: e.reduce_sum(lns[:, lb:lb + 1], x_chunk[:, xb:xb + D], axis=AX.X))
                b.wait("scalar", s_xn, v_x)
                b.op("scalar", lambda e, xb=xb, xsq=xsq: e.activation(
                    out=xsq[:, :], in_=x_chunk[:, xb:xb + D], func=AF.Square), s_s, 1)
                b.wait("vector", s_s, s_s.n)
                b.op("vector", lambda e, lb=lb, xsq=xsq: e.reduce_sum(lns[:, lb + 1:lb + 2], xsq[:, :], axis=AX.X))
                b.op("vector", lambda e, lb=lb: e.tensor_scalar_mul(lns[:, lb + 6:lb + 7], lns[:, lb:lb + 1], 1.0 / D))
                b.op("vector", lambda e, lb=lb: e.tensor_scalar_mul(lns[:, lb + 7:lb + 8], lns[:, lb + 1:lb + 2], 1.0 / D))
                b.op("vector", lambda e, lb=lb: e.tensor_mul(lns[:, lb + 8:lb + 9], lns[:, lb + 6:lb + 7], lns[:, lb + 6:lb + 7]))
                b.op("vector", lambda e, lb=lb: e.tensor_sub(lns[:, lb + 9:lb + 10], lns[:, lb + 7:lb + 8], lns[:, lb + 8:lb + 9]), s_v, 1)
                b.wait("scalar", s_v, s_v.n)
                b.op("scalar", lambda e, lb=lb: e.activation(
                    out=lns[:, lb + 10:lb + 11], in_=lns[:, lb + 9:lb + 10], func=AF.Sqrt, bias=eps_s[:, 0:1]), s_s, 1)
                b.wait("vector", s_s, s_s.n)
                b.op("vector", lambda e, lb=lb: e.reciprocal(out=lns[:, lb + 11:lb + 12], in_=lns[:, lb + 10:lb + 11]))
                b.op("vector", lambda e, lb=lb: e.scalar_tensor_tensor(
                    out=lns[:, lb + 12:lb + 13], in0=lns[:, lb + 6:lb + 7], scalar=-1.0, in1=lns[:, lb + 11:lb + 12],
                    op0=ALU.mult, op1=ALU.mult), s_v, 1)
                b.wait("scalar", s_v, s_v.n)
                if i >= 2:
                    b.wait("scalar", s_lntp, 16 * (i - 1))
                b.op("scalar", lambda e, xb=xb, lb=lb, xn_bf=xn_bf: e.activation(
                    out=xn_bf[:, :], in_=x_chunk[:, xb:xb + D], func=AF.Identity,
                    bias=lns[:, lb + 12:lb + 13], scale=lns[:, lb + 11:lb + 12]), s_s, 1)
                v_apply = s_s.n
                ln_apply_done.append(v_apply)
                b.wait("sync", s_s, v_apply)
                if i >= 2:
                    b.wait("sync", s_xn2[par], 16 * (i // 2))
                b.op("sync", lambda e, xn_bf=xn_bf, xnT_stage=xnT_stage: e.dma_start(
                    out=xnT_stage.rearrange("p (dc t) -> p dc t", t=128),
                    in_=xn_bf[:, :], transpose=True), s_lntp, 16)
                b.wait("sync", s_lntp, s_lntp.n)
                half_t = xnT_own_a if (not SPLIT_CC or i < LNCH // 2) else xnT_own_b
                i_loc = i if (not SPLIT_CC or i < LNCH // 2) else i - LNCH // 2
                b.op("sync", lambda e, half_t=half_t, i_loc=i_loc, xnT_stage=xnT_stage: e.dma_start(
                    out=half_t.ap().rearrange("(dc p) t -> p dc t", p=128)[
                        :, :, i_loc * 128:(i_loc + 1) * 128],
                    in_=xnT_stage.rearrange("p (dc t) -> p dc t", t=128)),
                    s_xn2[par], 16)
            v_xnst_a = (s_xn2[0].n, s_xn2[1].n)
            v_xnst = v_xnst_a

            b.wait("gpsimd", s_xn2[0], v_xnst_a[0])
            b.wait("gpsimd", s_xn2[1], v_xnst_a[1])
            b.op("gpsimd", lambda e: e.collective_compute(
                "AllGather", ALU.bypass,
                replica_groups=[list(range(N_CORES))],
                ins=[xnT_own_a.ap().opt()], outs=[xn_all_a.ap().opt()]), s_cc, 1)
            v_cc_a = s_cc.n
            if SPLIT_CC:
                b.wait("gpsimd", s_xn2[0], v_xnst[0])
                b.wait("gpsimd", s_xn2[1], v_xnst[1])
                b.op("gpsimd", lambda e: e.collective_compute(
                    "AllGather", ALU.bypass,
                    replica_groups=[list(range(N_CORES))],
                    ins=[xnT_own_b.ap().opt()], outs=[xn_all_b.ap().opt()]), s_cc, 1)
            v_cc_b = s_cc.n


        if SPLIT_CC:
            G_ORDER = [g for g in range(NG) if g % 2 == 0] + [g for g in range(NG) if g % 2 == 1]
        else:
            G_ORDER = list(range(NG))

        with ExitStack() as s2:
            sb2 = lambda name, shp, dt: s2.enter_context(nc.sbuf_tensor(name, shp, dt))
            wqe_sb = sb2("wqe_sb", [128, NDC * 256], BF16)
            wk_sb = sb2("wk_sb", [128, NDC * 256], BF16)
            wv_sb = sb2("wv_sb", [128, NDC * 256], BF16)
            q_sb = sb2("q_sb", [128, 2 * T_TOK], BF16)
            k_sb = sb2("k_sb", [128, 2 * T_TOK], BF16)
            v_sb = sb2("v_sb", [128, NMC * 256], BF16)

            p1_last_mm = [0, 0]

            for hL in range(2):
                b.wait("gpsimd", s_wq, v_wqe_stores)
                if hL == 1:
                    b.wait("gpsimd", *p1_last_mm[0])
                b.op("gpsimd", lambda e, hL=hL: e.dma_start(
                    out=wqe_sb.rearrange("p (dc j) -> p dc j", j=256),
                    in_=wqeT_d.ap().rearrange("(dc p) j -> p dc j", p=128)[
                        :, :, hL * 256:(hL + 1) * 256]), s_g, 16)
                b.op("gpsimd", lambda e, hL=hL: e.dma_start(
                    out=wk_sb.rearrange("p (dc j) -> p dc j", j=256),
                    in_=wkT.rearrange("(dc p) j -> p dc j", p=128)[
                        :, :, hL * 256:(hL + 1) * 256]), s_g, 16)
                b.op("gpsimd", lambda e, hL=hL: e.dma_start(
                    out=wv_sb.rearrange("p (dc j) -> p dc j", j=256),
                    in_=wvT.rearrange("(dc p) j -> p dc j", p=128)[
                        :, :, hL * 256:(hL + 1) * 256]), s_g, 16)
                v_wpre = s_g.n
                b.wait("tensor", s_g, v_wpre)
                if hL == 1:
                    b.wait("tensor", s_oev, s_oev.n)
                    b.wait("tensor", s_exp, s_exp.n)
                    b.wait("tensor", s_bc, s_bc.n)

                base_tf = s_tf.n
                base_xt = [s_xt[0].n, s_xt[1].n]

                def p1_emit_loads(gi, g):
                    if SPLIT_CC:
                        b.wait("sync", s_cc, v_cc_a if g % 2 == 0 else v_cc_b)
                        r, t, c0 = g // 2, (xn_all_a if g % 2 == 0 else xn_all_b), 0
                    else:
                        b.wait("sync", s_cc, v_cc_b)
                        r, t, c0 = (g * GT) // TOK_PC, xn_all_a, (g * GT) % TOK_PC
                    for lg in range(8):
                        flat_lg = gi * 8 + lg
                        if flat_lg >= 2:
                            b.wait("sync", s_tf, base_tf + 4 * (flat_lg - 1))
                        sl = (flat_lg % 2) * 4 * GT
                        b.op("sync", lambda e, r=r, t=t, lg=lg, sl=sl, c0=c0: e.dma_start(
                            out=xn_t.rearrange("p (dc t) -> p dc t", t=GT)[
                                :, (sl // GT):(sl // GT) + 4, :],
                            in_=t.ap().rearrange("(dc p) t -> p dc t", p=128)[
                                :, r * 32 + lg * 4: r * 32 + lg * 4 + 4, c0:c0 + GT]),
                            s_xt[flat_lg % 2], 16)

                p1_evs_base = s_evs.n
                p1_tf_base = s_tf.n
                vst_base = s_vst.n
                for gi, g in enumerate(G_ORDER):
                    p1_emit_loads(gi, g)
                    for dc in range(NDC):
                        flat = gi * NDC + dc
                        flat_lg = gi * 8 + dc // 4
                        b.wait("tensor", s_xt[flat_lg % 2],
                               base_xt[flat_lg % 2] + 16 * (flat_lg // 2 + 1))
                        bsl = (flat_lg % 2) * 4 * GT + (dc % 4) * GT
                        for wsb, ps, ei in ((wqe_sb, PS_A, 1), (wk_sb, PS_B, 3), (wv_sb, PS_C, 5)):
                            for jc in range(2):
                                if dc == 0 and gi > 0:
                                    b.wait("tensor", s_evs, p1_evs_base + 6 * (gi - 1) + ei + jc)
                                b.op("tensor", lambda e, dc=dc, jc=jc, bsl=bsl, wsb=wsb, ps=ps: e.matmul(
                                    ps[:, jc * 512:(jc + 1) * 512],
                                    wsb[:, dc * 256 + jc * 128: dc * 256 + jc * 128 + 128],
                                    xn_t[:, bsl:bsl + GT],
                                    start=(dc == 0), stop=(dc == NDC - 1)),
                                    s_tf if (ps is PS_C and jc == 1) else None, 1)
                    v_gmm = p1_tf_base + NDC * (gi + 1)
                    b.wait("scalar", s_tf, v_gmm)
                    for jc in range(2):
                        jcg = hL * 2 + jc
                        b.op("scalar", lambda e, g=g, jc=jc, jcg=jcg: e.activation(
                            out=q_sb[:, jc * T_TOK + g * GT: jc * T_TOK + g * GT + GT],
                            in_=PS_A[:, jc * 512:(jc + 1) * 512], func=AF.Identity,
                            bias=bqe_s[:, jcg:jcg + 1]), s_evs, 1)
                    for jc in range(2):
                        jcg = hL * 2 + jc
                        b.op("scalar", lambda e, g=g, jc=jc, jcg=jcg: e.activation(
                            out=k_sb[:, jc * T_TOK + g * GT: jc * T_TOK + g * GT + GT],
                            in_=PS_B[:, jc * 512:(jc + 1) * 512], func=AF.Identity,
                            bias=bk4_s[:, jcg:jcg + 1]), s_evs, 1)
                    b.wait("scalar", s_vst, vst_base + 16 * gi)
                    for jc in range(2):
                        jcg = hL * 2 + jc
                        b.op("scalar", lambda e, jc=jc, jcg=jcg: e.activation(
                            out=vt_stage[:, jc * 512:(jc + 1) * 512],
                            in_=PS_C[:, jc * 512:(jc + 1) * 512], func=AF.Identity,
                            bias=bv4_s[:, jcg:jcg + 1]), s_evs, 1)
                    v_vev = s_evs.n
                    b.wait("sync", s_evs, v_vev)
                    b.op("sync", lambda e, g=g: e.dma_start(
                        out=v_scr.ap().rearrange("(ic p) t -> p ic t", p=128)[
                            :, :, g * GT:(g + 1) * GT],
                        in_=vt_stage.rearrange("p (ic t) -> p ic t", t=512)),
                        s_vst, 16)
                b.wait("sync", s_vst, s_vst.n)
                if hL == 1:
                    b.wait("sync", s_av, s_av.n)
                for ic in range(2):
                    b.op("sync", lambda e, ic=ic: e.dma_start(
                        out=v_sb.rearrange("p (mc f) -> p mc f", f=256)[
                            :, :, ic * 128:(ic + 1) * 128],
                        in_=v_scr.ap()[ic * 128:(ic + 1) * 128, :].rearrange(
                            "ii (mc t) -> ii mc t", t=128),
                        transpose=True), s_vtp, 16)
                v_vtp = s_vtp.n

                p1_last_mm[hL] = (s_tf, s_tf.n)
                lguard = [(s_evs, s_evs.n)] * 3
                ptguard = [(s_ptf[i], s_ptf[i].n) for i in range(3)]
                ptTguard = [(s_av, s_av.n)] * 4
                og_state = {"g": None if hL == 0 else (s_oev, s_oev.n)}
                last_dstripe = (s_dst.n, s_dstv.n)
                last_zupd = s_v.n
                last_final_dst = (s_dst.n, s_dstv.n)
                pending = []
                pair_tp = {}
                rt_ready = None
                final_G = None
                step = 0

                def drain(n):
                    for _ in range(min(n, len(pending))):
                        pending.pop(0)()

                def flush():
                    while pending:
                        pending.pop(0)()

                def queue_av(cpav, v_vtp):
                    for ci2 in range(2):
                        cav = cpav * 2 + ci2
                        tslot = cav % 4
                        first = (cav == 0)
                        mark = pair_tp[cav]
                        ef = {"v": False}
                        for msub in range(8):
                            for dch in range(2):
                                st = (first and msub == 0)
                                sp = (msub == 7)

                                def part(cav=cav, msub=msub, dch=dch, tslot=tslot,
                                         st=st, sp=sp, first=first, mark=mark,
                                         vv=v_vtp, ef=ef):
                                    if not ef["v"]:
                                        b.wait("tensor", s_vtp, vv)
                                        b.wait("tensor", s_tpc, mark)
                                        if og_state["g"] is not None:
                                            b.wait("tensor", *og_state["g"])
                                        ef["v"] = True
                                    b.op("tensor",
                                         lambda e: e.matmul(
                                             PS_D[:, dch * 512:(dch + 1) * 512],
                                             v_sb[:, (cav * 8 + msub) * 256 + dch * 128:
                                                  (cav * 8 + msub) * 256 + dch * 128 + 128],
                                             ptT[:, tslot * 4096 + msub * 512:
                                                 tslot * 4096 + msub * 512 + 512],
                                             start=st, stop=sp,
                                             skip_group_check=(not first)),
                                         s_av if (msub == 7 and dch == 1) else None, 1)
                                pending.append(part)
                        ptTguard[tslot] = (s_av, _av_base[0] + ci2 + 1)

                def queue_rescale(scale_col_base, sem_src, v_src, is_final, Gf=None):
                    b.wait("scalar", s_bcm, s_bcm.n)
                    b.wait("scalar", sem_src, v_src)
                    b.wait("vector", s_bcm, s_bcm.n)
                    b.wait("vector", sem_src, v_src)
                    for t in range(2):
                        b.op("scalar", lambda e, t=t, base=scale_col_base: e.activation(
                            out=dstripe[:, t * 128:(t + 1) * 128], in_=ident_b_s[:, :],
                            func=AF.Identity, scale=asml[:, base + t:base + t + 1]),
                            s_dst, 1)
                    for t in range(2, 4):
                        b.op("vector", lambda e, t=t, base=scale_col_base: e.tensor_scalar_mul(
                            dstripe[:, t * 128:(t + 1) * 128], ident_b_s[:, :],
                            asml[:, base + t:base + t + 1]), s_dstv, 1)
                    v_dst = (s_dst.n, s_dstv.n)

                    def resc():
                        bufb = (glob_step[0] + 1) % 3
                        Lb = LBUF[bufb]
                        b.wait("tensor", s_dst, v_dst[0])
                        b.wait("tensor", s_dstv, v_dst[1])
                        if lguard[bufb] is not None:
                            b.wait("tensor", *lguard[bufb])
                        b.op("tensor", lambda e, Lb=Lb: e.matmul(
                            Lb[:, 0:512], ones128_s[:, :], dstripe[:, :],
                            start=True, stop=True), s_bcm, 1)
                        b.wait("scalar", s_bcm, s_bcm.n)
                        b.op("scalar", lambda e, Lb=Lb: e.copy(
                            out=bc_s[:, :], in_=Lb[:, 0:512]), s_bc, 1)
                        lguard[bufb] = (s_bc, s_bc.n)
                        b.wait("vector", s_bc, s_bc.n)
                        b.wait("vector", s_av, s_av.n)
                        for dch in range(2):
                            b.op("vector", lambda e, dch=dch: e.tensor_mul(
                                PS_D[:, dch * 512:(dch + 1) * 512],
                                PS_D[:, dch * 512:(dch + 1) * 512],
                                bc_s[:, :]), s_rsc if dch == 1 else None, 1)
                        og_state["g"] = (s_rsc, s_rsc.n)
                        if is_final:
                            v_final_rsc = s_rsc.n
                            b.wait("scalar", s_rsc, v_final_rsc)
                            b.wait("scalar", s_osp, s_osp.n)
                            for dch in range(2):
                                b.op("scalar", lambda e, dch=dch: e.copy(
                                    out=o_stage[:, dch * 512:(dch + 1) * 512],
                                    in_=PS_D[:, dch * 512:(dch + 1) * 512]), s_oev, 1)
                            v_oev = s_oev.n
                            og_state["g"] = (s_oev, v_oev)
                            b.wait("sync", s_oev, v_oev)
                            b.op("sync", lambda e, Gf=Gf, hL=hL: e.dma_start(
                                out=o_scr.ap().rearrange("(r p) t -> p r t", p=128)[
                                    :, hL * 2: hL * 2 + 2, Gf * GT:(Gf + 1) * GT],
                                in_=o_stage.rearrange("p (r t) -> p r t", t=512)), s_osp, 16)
                    pending.append(resc)
                    return v_dst

                glob_step = [0]
                _av_base = [0]
                for G in range(NG):
                    for cp in range(NCP):
                        tp_records = {cp * 2: [], cp * 2 + 1: []}
                        for t in range(4):
                            for cisub in range(2):
                                c = cp * 2 + cisub
                                buf = step % 3
                                L = LBUF[buf]
                                glob_step[0] = step
                                if lguard[buf] is not None:
                                    b.wait("tensor", *lguard[buf])
                                for icc in range(2):
                                    for nb2 in range(2):
                                        b.op("tensor",
                                             lambda e, G=G, t=t, c=c, icc=icc, nb2=nb2, L=L: e.matmul(
                                                 L[:, nb2 * 512:(nb2 + 1) * 512],
                                                 q_sb[:, icc * T_TOK + G * GT + t * 128:
                                                      icc * T_TOK + G * GT + t * 128 + 128],
                                                 k_sb[:, icc * T_TOK + c * 1024 + nb2 * 512:
                                                      icc * T_TOK + c * 1024 + nb2 * 512 + 512],
                                                 start=(icc == 0), stop=(icc == 1)),
                                             s_qk if (icc == 1 and nb2 == 1) else None, 1)
                                v_qkstep = s_qk.n
                                in_pair_step = (t * 2 + cisub)
                                drain({0: 1, 1: 0}.get(in_pair_step, 6))
                                b.wait("vector", s_qk, v_qkstep)
                                b.op("vector", lambda e, L=L, cisub=cisub: e.reduce_max(
                                    asml[:, MU + cisub:MU + cisub + 1], L[:, :],
                                    axis=AX.X, negate=True), s_mx, 1)
                                if cisub == 1:
                                    b.op("vector", lambda e: e.tensor_tensor(
                                        out=asml[:, MU:MU + 1], in0=asml[:, MU:MU + 1],
                                        in1=asml[:, MU + 1:MU + 2], op=ALU.min))
                                    if cp == 0:
                                        b.op("vector", lambda e, t=t: e.tensor_copy(
                                            out=asml[:, MRUN + t:MRUN + t + 1],
                                            in_=asml[:, MU:MU + 1]), s_v, 1)
                                    else:
                                        b.wait("vector", s_dst, last_dstripe[0])
                                        b.wait("vector", s_dstv, last_dstripe[1])
                                        b.op("vector", lambda e, t=t: e.tensor_tensor(
                                            out=asml[:, DLT:DLT + 1], in0=asml[:, MU:MU + 1],
                                            in1=asml[:, MRUN + t:MRUN + t + 1], op=ALU.min))
                                        b.op("vector", lambda e, t=t: e.tensor_sub(
                                            asml[:, BETA + t:BETA + t + 1],
                                            asml[:, DLT:DLT + 1],
                                            asml[:, MRUN + t:MRUN + t + 1]))
                                        b.op("vector", lambda e, t=t: e.tensor_copy(
                                            out=asml[:, MRUN + t:MRUN + t + 1],
                                            in_=asml[:, DLT:DLT + 1]), s_v, 1)
                                    v_mready = s_v.n
                                    for cc2 in range(2):
                                        st2 = step - 1 + cc2
                                        slot2 = st2 % 3
                                        L2 = LBUF[slot2]
                                        b.wait("scalar", s_v, v_mready)
                                        b.wait("scalar", s_v, last_zupd)
                                        if ptguard[slot2] is not None:
                                            b.wait("scalar", *ptguard[slot2])
                                        b.op("scalar", lambda e, L2=L2, slot2=slot2, t=t, cc2=cc2: e.activation(
                                            out=pt[:, slot2 * 1024:(slot2 + 1) * 1024],
                                            in_=L2[:, :], func=AF.Exp,
                                            bias=asml[:, MRUN + t:MRUN + t + 1],
                                            accum_out=asml[:, ZC + 2 * t + cc2:ZC + 2 * t + cc2 + 1]),
                                            s_exp, 1)
                                        lguard[slot2] = (s_exp, s_exp.n)
                                        cthis = cp * 2 + cc2
                                        tslot = cthis % 4
                                        b.wait("sync", s_exp, s_exp.n)
                                        if t == 0 and ptTguard[tslot] is not None:
                                            b.wait("sync", *ptTguard[tslot])
                                        vtp = b.op("sync", lambda e, slot2=slot2, tslot=tslot, t=t: e.dma_start(
                                            out=ptT.rearrange("m (sl s q) -> m (sl s) q", q=GT, s=8)[
                                                :, tslot * 8:(tslot + 1) * 8, t * 128:(t + 1) * 128],
                                            in_=pt[:, slot2 * 1024:(slot2 + 1) * 1024],
                                            transpose=True), s_ptf[slot2], 16)
                                        ptguard[slot2] = (s_ptf[slot2], vtp)
                                        tp_records[cthis].append((s_ptf[slot2], vtp))
                                step += 1
                        for cch in (cp * 2, cp * 2 + 1):
                            for semx, valx in tp_records[cch]:
                                b.wait("sync", semx, valx)
                            b.op("sync", lambda e: e.nop(), s_tpc, 1)
                            pair_tp[cch] = s_tpc.n
                        b.wait("vector", s_exp, s_exp.n)
                        zc_even = asml[:, ZC:ZC + 8].rearrange("p (t two) -> p t two", two=2)[:, :, 0:1]
                        zc_odd = asml[:, ZC:ZC + 8].rearrange("p (t two) -> p t two", two=2)[:, :, 1:2]
                        zp_v = asml[:, ZP:ZP + 4].rearrange("p (t one) -> p t one", one=1)
                        b.op("vector", lambda e: e.tensor_tensor(
                            out=zp_v, in0=zc_even, in1=zc_odd, op=ALU.add), s_v, 1)
                        if cp == 0:
                            b.op("vector", lambda e: e.tensor_copy(
                                out=asml[:, ZT:ZT + 4], in_=asml[:, ZP:ZP + 4]), s_v, 1)
                        else:
                            b.wait("scalar", s_v, s_v.n - 1)
                            b.op("scalar", lambda e: e.activation(
                                out=asml[:, BETA:BETA + 4], in_=asml[:, BETA:BETA + 4],
                                func=AF.Exp), s_s, 1)
                            b.wait("vector", s_s, s_s.n)
                            b.op("vector", lambda e: e.tensor_mul(
                                asml[:, ZT:ZT + 4], asml[:, ZT:ZT + 4],
                                asml[:, BETA:BETA + 4]))
                            b.op("vector", lambda e: e.tensor_add(
                                asml[:, ZT:ZT + 4], asml[:, ZT:ZT + 4],
                                asml[:, ZP:ZP + 4]), s_v, 1)
                        last_zupd = s_v.n
                        flush()
                        _av_base[0] = s_av.n
                        if cp > 0:
                            last_dstripe = queue_rescale(BETA, s_s, s_s.n, False)
                        elif final_G is not None:
                            last_final_dst = queue_rescale(RT, s_v, rt_ready, True, Gf=final_G)
                            final_G = None
                        queue_av(cp, v_vtp)
                        if cp == NCP - 1:
                            b.wait("vector", s_exp, s_exp.n)
                            b.wait("vector", s_dst, last_final_dst[0])
                            b.wait("vector", s_dstv, last_final_dst[1])
                            for t in range(4):
                                b.op("vector", lambda e, t=t: e.reciprocal(
                                    out=asml[:, RT + t:RT + t + 1],
                                    in_=asml[:, ZT + t:ZT + t + 1]), s_v if t == 3 else None, 1)
                            rt_ready = s_v.n
                            final_G = G
                flush()
                _av_base[0] = s_av.n
                queue_rescale(RT, s_v, rt_ready, True, Gf=final_G)
                final_G = None
                flush()
            v_osp_all = s_osp.n


        with ExitStack() as s3:
            sb3 = lambda name, shp, dt: s3.enter_context(nc.sbuf_tensor(name, shp, dt))
            wf_sb = sb3("wf_sb", [128, NDC * 512], BF16)
            af_sb = sb3("af_sb", [128, NDC * 256], BF16)
            o_tile = sb3("o_tile", [128, 4 * 512], F32)
            xT_tile = sb3("xT_tile", [128, 4 * 512], F32)
            xa_sb = sb3("xa_sb", [128, 1024], BF16)
            stage_out = sb3("stage_out", [128, 4 * 512], F32)

            b.wait("gpsimd", *p1_last_mm[1])
            b.op("gpsimd", lambda e: e.dma_start(
                out=wf_sb.rearrange("p (dc o) -> p dc o", o=512),
                in_=wfT.rearrange("(dc p) o -> p dc o", p=128)), s_g, 16)
            b.op("gpsimd", lambda e: e.dma_start(
                out=af_sb.rearrange("p (dc o) -> p dc o", o=256),
                in_=afT.rearrange("(dc p) o -> p dc o", p=128)), s_g, 16)
            v_p3w = s_g.n
            b.wait("tensor", s_g, v_p3w)
            b.wait("tensor", s_exp, s_exp.n)
            b.wait("tensor", s_bc, s_bc.n)
            b.wait("tensor", s_oev, s_oev.n)

            base_tf = s_tf.n
            base_xt = [s_xt[0].n, s_xt[1].n]

            def p3_emit_loads(gi, g):
                if SPLIT_CC:
                    r, t, c0 = g // 2, (xn_all_a if g % 2 == 0 else xn_all_b), 0
                else:
                    r, t, c0 = (g * GT) // TOK_PC, xn_all_a, (g * GT) % TOK_PC
                for lg in range(8):
                    flat_lg = gi * 8 + lg
                    if flat_lg >= 2:
                        b.wait("sync", s_tf, base_tf + 4 * (flat_lg - 1))
                    sl = (flat_lg % 2) * 4 * GT
                    b.op("sync", lambda e, r=r, t=t, lg=lg, sl=sl, c0=c0: e.dma_start(
                        out=xn_t.rearrange("p (dc t) -> p dc t", t=GT)[
                            :, (sl // GT):(sl // GT) + 4, :],
                        in_=t.ap().rearrange("(dc p) t -> p dc t", p=128)[
                            :, r * 32 + lg * 4: r * 32 + lg * 4 + 4, c0:c0 + GT]),
                        s_xt[flat_lg % 2], 16)

            b.wait("gpsimd", s_osp, v_osp_all)
            base_g3 = s_g3.n
            for gi2, g in enumerate(G_ORDER):
                if gi2 >= 1:
                    for _oc in range(4):
                        b.wait("gpsimd", s_ou[_oc], 16 * gi2)
                for oc in range(4):
                    b.op("gpsimd", lambda e, g=g, oc=oc: e.dma_start(
                        out=o_tile[:, oc * 512:(oc + 1) * 512],
                        in_=o_scr.ap()[oc * 128:(oc + 1) * 128, g * GT:(g + 1) * GT]),
                        s_g3, 16)
                    b.op("gpsimd", lambda e, g=g, oc=oc: e.dma_start(
                        out=xT_tile[:, oc * 512:(oc + 1) * 512],
                        in_=xT[oc * 128:(oc + 1) * 128, g * GT:(g + 1) * GT]),
                        s_g3, 16)

            PSO = [(PS_A, 0), (PS_A, 512), (PS_B, 0), (PS_B, 512)]
            ev3_base = s_evv.n
            p3_tf_base = s_tf.n
            p3_evac_done = 0
            pending_stores = []
            for gi, g in enumerate(G_ORDER):
                p3_emit_loads(gi, g)
                for emit_store in pending_stores:
                    emit_store()
                pending_stores = []
                for dc in range(NDC):
                    flat = gi * NDC + dc
                    flat_lg = gi * 8 + dc // 4
                    b.wait("tensor", s_xt[flat_lg % 2],
                           base_xt[flat_lg % 2] + 16 * (flat_lg // 2 + 1))
                    bsl = (flat_lg % 2) * 4 * GT + (dc % 4) * GT
                    if dc == 0 and gi > 0:
                        b.wait("tensor", s_s, p3_evac_done)
                        b.wait("tensor", s_evv, ev3_base + 2 * gi)
                    for oc in range(4):
                        ps, off = PSO[oc]
                        b.op("tensor", lambda e, dc=dc, oc=oc, bsl=bsl, ps=ps, off=off: e.matmul(
                            ps[:, off:off + 512],
                            wf_sb[:, dc * 512 + oc * 128: dc * 512 + oc * 128 + 128],
                            xn_t[:, bsl:bsl + GT],
                            start=(dc == 0), stop=False))
                    for chh in range(2):
                        b.op("tensor", lambda e, dc=dc, chh=chh, bsl=bsl: e.matmul(
                            PS_C[:, chh * 512:(chh + 1) * 512],
                            af_sb[:, dc * 256 + chh * 128: dc * 256 + chh * 128 + 128],
                            xn_t[:, bsl:bsl + GT],
                            start=(dc == 0), stop=(dc == NDC - 1)),
                            s_tf if chh == 1 else None, 1)
                v_xa = p3_tf_base + NDC * (g + 1)
                b.wait("vector", s_tf, v_xa)
                for chh in range(2):
                    b.op("vector", lambda e, chh=chh: e.tensor_copy(
                        out=xa_sb[:, chh * 512:(chh + 1) * 512],
                        in_=PS_C[:, chh * 512:(chh + 1) * 512]), s_evv, 1)
                v_xaev = s_evv.n
                b.wait("tensor", s_evv, v_xaev)
                for chh in range(2):
                    for oc in range(4):
                        ps, off = PSO[oc]
                        b.op("tensor", lambda e, chh=chh, oc=oc, ps=ps, off=off: e.matmul(
                            ps[:, off:off + 512],
                            bfp_b[:, chh * 512 + oc * 128: chh * 512 + oc * 128 + 128],
                            xa_sb[:, chh * 512:(chh + 1) * 512],
                            start=False, stop=(chh == 1)),
                            s_t if (chh == 1 and oc == 3) else None, 1)
                v_updmm = s_t.n
                b.wait("scalar", s_t, v_updmm)
                for oc in range(4):
                    ps, off = PSO[oc]
                    ssl = oc * 512
                    b.wait("scalar", s_ou[oc], 16 * gi)
                    b.op("scalar", lambda e, ps=ps, off=off, oc=oc, ssl=ssl: e.activation(
                        out=stage_out[:, ssl:ssl + 512], in_=ps[:, off:off + 512],
                        func=AF.Identity, bias=bf4_s[:, oc:oc + 1]), s_s, 1)
                    v_ev1 = s_s.n
                    b.wait("vector", s_s, v_ev1)
                    b.wait("vector", s_g3, base_g3 + 16 * 8 * (g + 1))
                    b.op("vector", lambda e, ssl=ssl, oc=oc: e.tensor_add(
                        stage_out[:, ssl:ssl + 512], stage_out[:, ssl:ssl + 512],
                        o_tile[:, oc * 512:(oc + 1) * 512]))
                    b.op("vector", lambda e, ssl=ssl, oc=oc: e.tensor_add(
                        stage_out[:, ssl:ssl + 512], stage_out[:, ssl:ssl + 512],
                        xT_tile[:, oc * 512:(oc + 1) * 512]), s_v, 1)
                    v_add = s_v.n

                    def emit_store(g=g, oc=oc, ssl=ssl, v_add=v_add):
                        b.wait("sync", s_v, v_add)
                        b.op("sync", lambda e, g=g, oc=oc, ssl=ssl: e.dma_start(
                            out=out_d[oc * 128:(oc + 1) * 128, g * GT:(g + 1) * GT],
                            in_=stage_out[:, ssl:ssl + 512]), s_ou[oc], 16)
                    pending_stores.append(emit_store)
                p3_evac_done = s_s.n
            for emit_store in pending_stores:
                emit_store()

            for _oc in range(4):
                b.wait("sync", s_ou[_oc], s_ou[_oc].n)

        @block.sync
        def _(eng):
            for f in b.prog["sync"]:
                f(eng)

        @block.tensor
        def _(eng):
            for f in b.prog["tensor"]:
                f(eng)

        @block.vector
        def _(eng):
            for f in b.prog["vector"]:
                f(eng)

        @block.scalar
        def _(eng):
            for f in b.prog["scalar"]:
                f(eng)

        @block.gpsimd
        def _(eng):
            for f in b.prog["gpsimd"]:
                f(eng)

    return nc


def _host_prep(inputs):
    C_np, rev_np = _build_tables()
    biv = C_np[:, :, BIV_IDX].astype(np.float32)
    bivC_flat = biv.transpose(2, 0, 1).reshape(3, 64)
    bivC_rep = np.repeat(np.concatenate([bivC_flat[c].reshape(1, 64) for c in range(3)],
                                        axis=1), 2, axis=0).astype(np.float32)
    revs_rep = np.repeat(np.repeat(rev_np.reshape(8, 1), 8, axis=1).reshape(1, 64),
                         2, axis=0).astype(np.float32)
    d0 = (SCALE * rev_np.reshape(8, 1) * np.eye(8, dtype=np.float32)).reshape(1, 64)
    d0_rep = np.repeat(d0, 2, axis=0).astype(np.float32)

    x = np.asarray(inputs["L0_attn_x"], dtype=np.float32)[0]
    xTfull = np.ascontiguousarray(x.T)
    Wq = np.asarray(inputs["L0_attn_Wq"], dtype=np.float32)
    Wk = np.asarray(inputs["L0_attn_Wk"], dtype=np.float32)
    Wv = np.asarray(inputs["L0_attn_Wv"], dtype=np.float32)
    bq = np.asarray(inputs["L0_attn_bq"], dtype=np.float32)
    bk = np.asarray(inputs["L0_attn_bk"], dtype=np.float32)
    bv = np.asarray(inputs["L0_attn_bv"], dtype=np.float32)
    bwf = np.asarray(inputs["L0_attn_bias_weight"], dtype=np.float32)
    Wf = np.asarray(inputs["L0_ffn_W"], dtype=np.float32)
    bff = np.asarray(inputs["L0_ffn_bias"], dtype=np.float32)
    Af = np.asarray(inputs["L0_ffn_A"], dtype=np.float32)
    Bff = np.asarray(inputs["L0_ffn_Bf"], dtype=np.float32)
    coeffs = np.asarray(inputs["L0_ffn_coeffs"], dtype=np.float32)
    scales = np.asarray(inputs["L0_ffn_scales"], dtype=np.float32)

    afT_full = np.ascontiguousarray(Af.reshape(256, D).T)
    coeffs_rep = np.ascontiguousarray(np.repeat(coeffs, 16).reshape(2, 128).T)
    scales_rep = np.ascontiguousarray(np.repeat(scales, 16).reshape(2, 128).T)
    cs4 = np.concatenate([coeffs_rep, scales_rep], axis=1).astype(np.float32)

    ident_f = np.eye(128, dtype=np.float32)
    ident_bb = np.eye(128, dtype=np.float32).astype(bf16)
    ones_bb = np.ones((1, 128), dtype=bf16)
    ones128_bb = np.ones((128, 128), dtype=bf16)

    in_maps = []
    for c in range(N_CORES):
        Fsl = slice(c * FS, (c + 1) * FS)
        in_maps.append(dict(
            x_tok=np.ascontiguousarray(x[c * TOK_PC:(c + 1) * TOK_PC, :]),
            xT=np.ascontiguousarray(xTfull[Fsl, :]),
            wq=np.ascontiguousarray(Wq[Fsl, :]),
            wkT=np.ascontiguousarray(Wk[Fsl, :].T),
            wvT=np.ascontiguousarray(Wv[Fsl, :].T),
            bq4=np.ascontiguousarray(bq[Fsl].reshape(4, 128).T),
            bk4=np.ascontiguousarray(bk[Fsl].reshape(4, 128).T),
            bv4=np.ascontiguousarray(bv[Fsl].reshape(4, 128).T),
            bw=np.ascontiguousarray(bwf[2 * c:2 * c + 2, :]),
            wfT=np.ascontiguousarray(Wf[Fsl, :].T),
            bf4=np.ascontiguousarray(bff[Fsl].reshape(4, 128).T),
            afT=afT_full,
            bfp=np.ascontiguousarray(Bff[:, Fsl, :].transpose(0, 2, 1).reshape(256, FS)),
            cs4=cs4,
            bivC=bivC_rep, revs=revs_rep, d0c=d0_rep,
            ident_f=ident_f, ident_b=ident_bb, ones_b=ones_bb, ones128_b=ones128_bb,
        ))
    return in_maps


def kernel(**inputs):
    in_maps = _host_prep(inputs)
    nc = build_program()
    res = run_bass_kernel_spmd(nc, in_maps, core_ids=list(range(N_CORES)))
    shards = [np.asarray(r["out"], dtype=np.float32) for r in res.results]
    out = np.concatenate([s.T for s in shards], axis=1)
    return out.reshape(1, T_TOK, D)
